# version 20
# baseline (speedup 1.0000x reference)
import numpy as np
import concourse.bass as bass
import concourse.mybir as mybir
from concourse.bass_utils import run_bass_kernel_spmd

F32 = mybir.dt.float32
BF16 = mybir.dt.bfloat16
U8 = mybir.dt.uint8
ALU = mybir.AluOpType
AF = mybir.ActivationFunctionType
AX = mybir.AxisListType

B = 4
T_FULL = 4096
D = 1024
NCORES = 8
T = 2048
NH = 32
HD = 32
M = 2
CH = 128
NCH = T // CH
BMIN, BMAX = 0.85, 0.9995
EPS_DEN = 1e-6
LN_EPS = 1e-5
ROPE_BASE = 10000.0
H1 = 16
DH1 = 64
LVL = 3
DFF = 4096
INV_SQRT2 = 0.7071067811865476

STRIPE = 256
NT = STRIPE // 128

ENGINES = ("sync", "tensor", "vector", "scalar", "gpsimd")
NDMA = 16


class Tok:
    __slots__ = ("sem", "val")
    def __init__(self, sem, val):
        self.sem = sem
        self.val = val


class Ctx:

    def __init__(self, nc, sems):
        self.nc = nc
        self.sems = sems

    def begin(self, eng_name, handle):
        self.eng = eng_name
        self.h = handle
        self.dma_idx = {e: 0 for e in ENGINES}
        self.count = {k: 0 for k in self.sems}
        self.water = {e: {k: 0 for k in self.sems} for e in ENGINES}
        self.barrier_k = 0

    def _wait(self, sem, val):
        if val <= 0:
            return
        if self.water[self.eng][sem] >= val:
            return
        self.h.wait_ge(self.sems[sem], val)
        self.water[self.eng][sem] = val

    def dep(self, eng, toks):
        if self.eng != eng:
            return
        stack = list(toks)
        while stack:
            t = stack.pop()
            if t is None:
                continue
            if hasattr(t, "toks"):
                stack.extend(t.toks)
                continue
            self._wait(t.sem, t.val)

    def op(self, eng, fn, deps=()):
        self.dep(eng, deps)
        sem = "e_" + eng
        if self.eng == eng:
            inst = fn(self.h)
            inst.then_inc(self.sems[sem], 1)
        self.count[sem] += 1
        return Tok(sem, self.count[sem])

    def dma(self, eng, out, in_, deps=(), transpose=False):
        self.dep(eng, deps)
        idx = self.dma_idx[eng] % NDMA
        self.dma_idx[eng] += 1
        sem = f"d_{eng}{idx}"
        if self.eng == eng:
            self._wait(sem, self.count[sem])
            inst = self.h.dma_start(out=out, in_=in_, transpose=transpose)
            inst.then_inc(self.sems[sem], 16)
        self.count[sem] += 16
        return Tok(sem, self.count[sem])

    def barrier(self):
        self.barrier_k += 1
        target = 5 * self.barrier_k
        if self.eng in ("sync", "gpsimd", "scalar"):
            for i in range(NDMA):
                self._wait(f"d_{self.eng}{i}", self.count[f"d_{self.eng}{i}"])
        self.h.drain()
        self.h.sem_inc(self.sems["bar"], 1)
        self.h.wait_ge(self.sems["bar"], target)
        for e in ENGINES:
            for k in self.sems:
                self.water[e][k] = self.count[k]
        self.count["bar"] = target


class Buf:
    __slots__ = ("ap", "tok")
    def __init__(self, ap, tok=None):
        self.ap = ap
        self.tok = tok


def _rope_tables(positions):
    half = HD // 2
    inv_freq = 1.0 / ROPE_BASE ** (np.arange(half, dtype=np.float64) / half)
    freqs = positions[:, None].astype(np.float64) * inv_freq[None, :]
    cos = np.cos(freqs).astype(np.float32)
    sin = np.sin(freqs).astype(np.float32)
    nt = len(positions) // 128
    cos_tm = np.ascontiguousarray(cos.reshape(nt, 128, 16).transpose(1, 0, 2).reshape(128, nt * 16))
    sin_tm = np.ascontiguousarray(sin.reshape(nt, 128, 16).transpose(1, 0, 2).reshape(128, nt * 16))
    return cos_tm, sin_tm


def _consts():
    c = {}
    c["ident"] = np.eye(128, dtype=np.float32)
    jj, ii = np.meshgrid(np.arange(CH), np.arange(CH), indexing="ij")
    c["mask_causal"] = (jj <= ii).astype(np.float32)
    dd = np.arange(128)
    u_of_dd = dd // HD
    m_state = np.zeros((128, 132), dtype=np.float32)
    for u in range(4):
        rows = u_of_dd == u
        m_state[rows, u * 33:u * 33 + 33] = 1.0
    c["mask_state"] = m_state
    GA = np.zeros((2 * NH, 128), dtype=np.float32)
    MB = np.zeros((2 * NH, 16), dtype=np.float32)
    for r in range(2 * NH):
        m_r, h_r = r // NH, r % NH
        GA[r, :] = ((np.arange(128) // HD) == (h_r % 4)).astype(np.float32)
        col = (h_r // 4) * 2 + m_r
        MB[r, col] = 1.0
    c["GA"] = GA
    c["MB"] = MB
    import ml_dtypes
    c["identh"] = np.eye(128).astype(ml_dtypes.bfloat16)
    return c


def _per_core_inputs(inputs, core):
    b, half = core // 2, core % 2
    d = {}
    x = np.asarray(inputs["L0_attn_x"])
    d["x0"] = np.ascontiguousarray(x[b, half * T:(half + 1) * T, :])
    if half == 1:
        d["xpre"] = np.ascontiguousarray(x[b, 0:T, :])
    else:
        d["xpre"] = np.zeros((T, D), dtype=np.float32)
    d["pscale"] = np.full((128, 1), float(half), dtype=np.float32)
    pos_own = np.arange(half * T, (half + 1) * T, dtype=np.float64)
    pos_pre = np.arange(0, T, dtype=np.float64)
    d["cos_own"], d["sin_own"] = _rope_tables(pos_own)
    d["cos_pre"], d["sin_pre"] = _rope_tables(pos_pre)

    d["wqT"] = np.ascontiguousarray(np.asarray(inputs["L0_attn_Wq"]).T)
    d["wkT"] = np.ascontiguousarray(np.asarray(inputs["L0_attn_Wk"]).T)
    d["wvT"] = np.ascontiguousarray(np.asarray(inputs["L0_attn_Wv"]).T)
    d["woutT"] = np.ascontiguousarray(np.asarray(inputs["L0_attn_Wout"]).T)
    d["bout_rep"] = np.ascontiguousarray(
        np.broadcast_to(np.asarray(inputs["L0_attn_bout"])[None, :], (128, D)))
    Wbeta = np.asarray(inputs["L0_attn_Wbeta"])
    bbeta = np.asarray(inputs["L0_attn_bbeta"])
    perm_b = np.array([m * NH + h for m in range(M) for h in range(NH)])
    src = np.array([h * M + m for m in range(M) for h in range(NH)])
    d["wbetaT"] = np.ascontiguousarray(Wbeta[src, :].T)
    d["bbeta_col"] = np.ascontiguousarray(bbeta[src][:, None])
    Wmix = np.asarray(inputs["L0_attn_Wmix"])
    bmix = np.asarray(inputs["L0_attn_bmix"])
    d["wmixT"] = np.ascontiguousarray(Wmix[src, :].T)
    d["bmix_rep"] = np.ascontiguousarray(
        np.broadcast_to(bmix[src][None, :], (128, 2 * NH)))
    d["ln_g_rep"] = np.ascontiguousarray(
        np.broadcast_to(np.asarray(inputs["L0_attn_ln_g"])[None, :], (128, D)))
    d["ln_b_rep"] = np.ascontiguousarray(
        np.broadcast_to(np.asarray(inputs["L0_attn_ln_b"])[None, :], (128, D)))

    d["w1T"] = np.ascontiguousarray(np.asarray(inputs["L0_ffn_W1"]).T)
    d["b1_col"] = np.ascontiguousarray(np.asarray(inputs["L0_ffn_b1"]).reshape(DFF // 128, 128).T)
    d["w2T"] = np.ascontiguousarray(np.asarray(inputs["L0_ffn_W2"]).T)
    d["b2_rep"] = np.ascontiguousarray(
        np.broadcast_to(np.asarray(inputs["L0_ffn_b2"])[None, :], (128, D)))

    Wqkv = np.asarray(inputs["L1_attn_W_qkv"])
    bqkv = np.asarray(inputs["L1_attn_b_qkv"])
    sel = np.concatenate([np.arange(0, D), np.arange(2 * D, 3 * D)])
    d["wqvT"] = np.ascontiguousarray(Wqkv[sel, :].T)
    d["bqv_col"] = np.ascontiguousarray(bqkv[sel].reshape(16, 128).T)
    d["wout1T"] = np.ascontiguousarray(np.asarray(inputs["L1_attn_W_out"]).T)
    d["bout1_rep"] = np.ascontiguousarray(
        np.broadcast_to(np.asarray(inputs["L1_attn_b_out"])[None, :], (128, D)))
    wq1 = np.asarray(inputs["L1_attn_weight_q"])
    wv1 = np.asarray(inputs["L1_attn_weight_v"])
    d["hwq_col"] = np.ascontiguousarray(wq1.transpose(0, 2, 1).reshape(D, LVL))
    d["hwv_col"] = np.ascontiguousarray(wv1.transpose(0, 2, 1).reshape(D, LVL))

    d["wgT"] = np.ascontiguousarray(np.asarray(inputs["L1_ffn_w_gate"]).T)
    d["wuT"] = np.ascontiguousarray(np.asarray(inputs["L1_ffn_w_up"]).T)
    d["wdT"] = np.ascontiguousarray(np.asarray(inputs["L1_ffn_w_down"]).T)

    d.update(_consts())
    return {k: np.ascontiguousarray(v) for k, v in d.items()}


INPUT_SPECS = {
    "x0": ([T, D], F32), "xpre": ([T, D], F32),
    "cos_own": ([128, 16 * (T // 128)], F32), "sin_own": ([128, 16 * (T // 128)], F32),
    "cos_pre": ([128, 16 * (T // 128)], F32), "sin_pre": ([128, 16 * (T // 128)], F32),
    "wqT": ([D, D], F32), "wkT": ([D, D], F32), "wvT": ([D, D], F32),
    "woutT": ([D, D], F32), "bout_rep": ([128, D], F32),
    "wbetaT": ([D, 2 * NH], F32), "bbeta_col": ([2 * NH, 1], F32),
    "wmixT": ([D, 2 * NH], F32), "bmix_rep": ([128, 2 * NH], F32),
    "ln_g_rep": ([128, D], F32), "ln_b_rep": ([128, D], F32),
    "w1T": ([D, DFF], F32), "b1_col": ([128, DFF // 128], F32),
    "w2T": ([DFF, D], F32), "b2_rep": ([128, D], F32),
    "wqvT": ([D, 2 * D], F32), "bqv_col": ([128, 16], F32),
    "wout1T": ([D, D], F32), "bout1_rep": ([128, D], F32),
    "hwq_col": ([D, LVL], F32), "hwv_col": ([D, LVL], F32),
    "wgT": ([D, DFF], F32), "wuT": ([D, DFF], F32), "wdT": ([DFF, D], F32),
    "ident": ([128, 128], F32),
    "mask_causal": ([CH, CH], F32),
    "mask_state": ([128, 132], F32),
    "GA": ([2 * NH, 128], F32),
    "pscale": ([128, 1], F32),
    "identh": ([128, 128], BF16),
    "MB": ([2 * NH, 16], F32),
}


def kernel(**inputs):
    nc, run_order = build_program()
    in_maps = [_per_core_inputs(inputs, c) for c in range(NCORES)]
    res = run_bass_kernel_spmd(nc, in_maps, core_ids=list(range(NCORES)))
    out = np.empty((B, T_FULL, D), dtype=np.float32)
    for c in range(NCORES):
        b, half = c // 2, c % 2
        out[b, half * T:(half + 1) * T, :] = res.results[c]["out"]
    return out


def build_program(upto=None):
    nc = bass.Bass("TRN2", target_bir_lowering=False)
    g = {}

    dram = {}
    for name, (shape, dt) in INPUT_SPECS.items():
        dram[name] = nc.dram_tensor(name, shape, dt, kind="ExternalInput").ap()
    dram["out"] = nc.dram_tensor("out", [T, D], F32, kind="ExternalOutput").ap()
    scr = {
        "attn0": ([T, D], F32),
        "x1": ([T, D], F32),
        "h0": ([DFF, T], F32),
        "xn1_fm": ([D, T], F32),
        "xq_fm": ([D, T], BF16),
        "sx_col": ([T, 1], F32),
        "attn1": ([T, D], F32),
        "gate": ([T, DFF], F32),
        "gq_fm": ([DFF, T], BF16),
        "sg_col": ([T, 1], F32),
        "gammas": ([8, 1], F32),
        "w1q": ([D, DFF], BF16),
        "w2q": ([DFF, D], BF16),
        "wq_hl": ([D, 2 * D], BF16),
        "wk_hl": ([D, 2 * D], BF16),
        "wv_hl": ([D, 2 * D], BF16),
        "wo_hl": ([D, 2 * D], BF16),
        "wqv_hl0": ([D, 2 * D], BF16), "wqv_hl1": ([D, 2 * D], BF16),
        "wo1_hl": ([D, 2 * D], BF16),
        "xnh_fm": ([D, T], BF16), "xnl_fm": ([D, T], BF16),
        "wb_hl": ([D, 4 * NH], BF16),
        "wm_hl": ([D, 4 * NH], BF16),
        "wgq": ([D, DFF], BF16),
        "wuq": ([D, DFF], BF16),
        "wdq": ([DFF, D], BF16),
        "dbg": ([128, 4352], F32),
    }
    for name, (shape, dt) in scr.items():
        kind = "ExternalOutput" if (upto is not None and name == "dbg") else "Internal"
        dram[name] = nc.dram_tensor(name, shape, dt, kind=kind).ap()

    sem_names = (["e_" + e for e in ENGINES]
                 + [f"d_{e}{i}" for e in ("sync", "scalar", "gpsimd") for i in range(NDMA)]
                 + ["bar"])

    with _alloc(nc, g), _sems(nc, sem_names) as sems, nc.Block() as block:
        ctx = Ctx(nc, sems)

        def program(eng_name, handle):
            ctx.begin(eng_name, handle)
            emit_all(ctx, g, dram, upto)

        @block.sync
        def _(e):
            program("sync", e)

        @block.tensor
        def _(e):
            program("tensor", e)

        @block.vector
        def _(e):
            program("vector", e)

        @block.scalar
        def _(e):
            program("scalar", e)

        @block.gpsimd
        def _(e):
            program("gpsimd", e)

    return nc, None


class _sems:
    def __init__(self, nc, names):
        self.nc = nc
        self.names = names
        self.cms = []

    def __enter__(self):
        out = {}
        for n in self.names:
            cm = self.nc.semaphore(n)
            out[n] = cm.__enter__()
            self.cms.append(cm)
        return out

    def __exit__(self, *a):
        for cm in reversed(self.cms):
            cm.__exit__(*a)
        return False


class _alloc:
    def __init__(self, nc, g):
        self.nc = nc
        self.g = g
        self.cms = []

    def _sb(self, name, shape, dt=F32):
        cm = self.nc.sbuf_tensor(name, shape, dt)
        t = cm.__enter__()
        self.cms.append(cm)
        self.g[name] = t
        return t

    def _ps(self, name):
        cm = self.nc.psum_tensor(name, [128, 512], F32)
        t = cm.__enter__()
        self.cms.append(cm)
        self.g[name] = t
        return t

    def __enter__(self):
        sb = self._sb
        sb("c_ident", [128, 128])
        sb("c_identh", [128, 128], BF16)
        sb("c_mask", [CH, CH])
        sb("c_mstate", [128, 132])
        sb("c_cos", [128, 16 * (T // 128)]); sb("c_sin", [128, 16 * (T // 128)])
        sb("c_lng", [128, D]); sb("c_lnb", [128, D])
        sb("c_brep", [128, D])
        sb("c_bsmall", [128, 2 * NH])
        sb("c_stats", [128, 16]); sb("c_zero", [2 * NH, CH]); sb("c_eps", [128, 8])
        sb("c_GA", [2 * NH, 128]); sb("c_MB", [2 * NH, 16]); sb("c_pscale", [128, 1])
        sb("c_ones", [128, 128])
        sb("c_hw", [128, 48]); sb("c_gam", [128, 4]); sb("c_rsx", [128, 16])
        sb("bfA", [128, 1024], BF16); sb("bfB", [128, 1024], BF16)
        sb("c_bcol", [128, DFF // 128])
        sb("c_bbeta", [2 * NH, 1])
        sb("wAh", [128, 2048], BF16); sb("wBh", [128, 2048], BF16)
        sb("xqf", [128, 8 * STRIPE], BF16)
        sb("x_tm", [128, NT * D])
        sb("xn_tm", [128, NT * D])
        sb("xnfh", [128, (D // 128) * STRIPE], BF16)
        sb("xnfl", [128, (D // 128) * STRIPE], BF16)
        sb("hh", [128, 32 * STRIPE], BF16); sb("hl", [128, 32 * STRIPE], BF16)
        sb("yh", [128, NT * D], BF16); sb("yl", [128, NT * D], BF16)
        sb("q_tm", [128, NT * D])
        sb("k_tm", [128, NT * D])
        sb("v_aug", [128, NT * 8 * 132])
        sb("mix_tm", [128, NT * 2 * NH])
        sb("beta_fm", [2 * NH, STRIPE])
        sb("a_fm", [2 * NH, STRIPE]); sb("ra_fm", [2 * NH, STRIPE])
        sb("a_tm", [128, NT * 2 * NH]); sb("ra_tm", [128, NT * 2 * NH])
        sb("qh_tm", [128, NT * D]); sb("qh_tm2", [128, NT * D])
        sb("kt_tm", [128, NT * D]); sb("kt_tm2", [128, NT * D])
        sb("qh_fm", [128, D]); sb("kt_fm", [128, D])
        sb("at_sb", [128, 8 * CH])
        sb("y_tm", [128, NT * D])
        sb("s_all", [128, 2 * NH]); sb("dens", [128, 2 * NH])
        sb("aCd", [128, 16])
        sb("aC_small", [2 * NH, 16])
        sb("S0", [128, 16 * 132]); sb("S1", [128, 16 * 132])
        sb("kinit", [128, 2])
        sb("misc", [128, 1024])
        sb("misc2", [128, 1024])
        sb("u8", [128, 1024], U8)
        total = 0
        for name, t in self.g.items():
            if hasattr(t, "shape"):
                sh = list(t.shape)
                dtsz = 1 if name == "u8" else 4
                if sh[0] <= 128:
                    total += sh[1] * dtsz if len(sh) == 2 else 0
        for i in range(8):
            self._ps(f"p{i}")
        return self.g

    def __exit__(self, *a):
        for cm in reversed(self.cms):
            cm.__exit__(*a)
        return False


def emit_all(ctx, g, dram, upto=None):
    emit_load_consts(ctx, g, dram)
    if upto == "consts":
        ctx.barrier()
        emit_debug_dump(ctx, g, dram, g["c_cos"].ap()[:, 0:512])
        return
    emit_weight_prep(ctx, g, dram)
    ctx.barrier()
    w01_ready = emit_l0_attn(ctx, g, dram, prefix=True, upto=upto)
    if upto in ("pre_ln", "pre_scan", "pre_full"):
        return
    ctx.barrier()
    emit_l0_attn(ctx, g, dram, prefix=False, upto=upto)
    if upto in ("l0_ln", "l0_scan", "l0_attn", "l0_y", "l0_at", "l0_at2"):
        return
    ctx.barrier()
    wl1_ready = emit_l0_ffn(ctx, g, dram, upto=upto, w_ready=w01_ready)
    if upto in ("l0_ffn", "x1"):
        return
    ctx.barrier()
    emit_l1_attn(ctx, g, dram, upto=upto)
    if upto in ("l1_attn", "xq", "l1_haar", "l1_attn_chk"):
        return
    ctx.barrier()
    emit_l1_ffn(ctx, g, dram, upto=upto, w_ready=wl1_ready)


def emit_ln_tile(ctx, g, x_ap, xn_ap, toks, affine=None, scratch=None, so=0):
    op = ctx.op
    st = g["c_stats"].ap()
    mu = st[:, so:so + 1]
    var = st[:, so + 1:so + 2]
    rstd = st[:, so + 2:so + 3]
    sq = (scratch if scratch is not None else g["misc"].ap()[:, 0:D])
    t1 = op("vector", lambda h: h.reduce_sum(mu, x_ap, AX.X), deps=toks)
    t2 = op("vector", lambda h: h.tensor_scalar(mu, mu, 1.0 / D, None, ALU.mult), deps=[t1])
    t3 = op("vector", lambda h: h.tensor_scalar(xn_ap, x_ap, mu, None, ALU.subtract), deps=[t2])
    t4 = op("scalar", lambda h: h.activation(sq, xn_ap, AF.Square, accum_out=var), deps=[t3])
    t5 = op("vector", lambda h: h.tensor_scalar(var, var, 1.0 / D, None, ALU.mult), deps=[t4])
    t6 = op("scalar", lambda h: h.activation(rstd, var, AF.Sqrt,
                                             bias=g["c_eps"].ap()[:, 0:1]), deps=[t5])
    t7 = op("vector", lambda h: h.reciprocal(rstd, rstd), deps=[t6])
    t8 = op("vector", lambda h: h.tensor_scalar(xn_ap, xn_ap, rstd, None, ALU.mult), deps=[t7])
    if affine is not None:
        gg, bb = affine
        t8 = op("vector", lambda h: h.tensor_tensor(xn_ap, xn_ap, gg, ALU.mult), deps=[t8])
        t8 = op("vector", lambda h: h.tensor_tensor(xn_ap, xn_ap, bb, ALU.add), deps=[t8])
    return t8


def emit_transpose_128(ctx, g, src_ap, dst_ap, toks, bank, evac_eng="scalar"):
    pb = g[bank].ap()
    tm = ctx.op("tensor", lambda h: h.transpose(pb[:, 0:128], src_ap, g["c_ident"].ap()),
                deps=toks)
    te = ctx.op(evac_eng, lambda h: h.copy(dst_ap, pb[:, 0:128]) if evac_eng == "scalar"
                else h.tensor_copy(dst_ap, pb[:, 0:128]), deps=[tm])
    return te, te


class Ph:
    def __init__(self):
        self.tok = {}
        self.bank_free = {}

    def t(self, name):
        return self.tok.get(name)

    def set(self, name, tok):
        self.tok[name] = tok


def emit_l0_attn(ctx, g, dram, prefix, upto=None):
    op, dma = ctx.op, ctx.dma
    ph = Ph()
    qspills = []
    x_src = dram["xpre"] if prefix else dram["x0"]
    cos_src = dram["cos_pre"] if prefix else dram["cos_own"]
    sin_src = dram["sin_pre"] if prefix else dram["sin_own"]
    nstripes = T // STRIPE

    tcos = dma("sync", g["c_cos"].ap(), cos_src)
    tsin = dma("sync", g["c_sin"].ap(), sin_src)
    ph.set("tbl", Tok2([tcos, tsin]))

    vaug = g["v_aug"].ap()
    t_ones = op("vector", lambda h: h.memset(
        bass.AP(g["v_aug"], 32, [[NT * 8 * 132, 128], [132, NT * 8], [33, 4], [1, 1]]), 1.0))

    if prefix:
        tz = op("vector", lambda h: h.memset(g["S0"].ap(), 0.0))
        ph.cur_S, ph.new_S = "S0", "S1"
        for hg in range(8):
            for m in range(M):
                ph.set(f"S_{hg}_{m}", tz)
    else:
        ph.cur_S, ph.new_S = ctx.l0_S_cur, ("S1" if ctx.l0_S_cur == "S0" else "S0")
        for hg in range(8):
            for m in range(M):
                ph.set(f"S_{hg}_{m}", ctx.l0_S_tok)

    for s in range(nstripes):
        emit_l0_stripe(ctx, ph, g, dram, s, x_src, prefix, t_ones, upto)
        if prefix and s == 0:
            tq_st = emit_tq_pass1(ctx, ph, g, dram, "w1T", "w1q", 0,
                                  stg=("q_tm", "y_tm"), bankname="p2")
        if prefix and s == 1:
            qspills.append(emit_tq_finish(ctx, ph, g, dram, tq_st))
        if prefix and s == 3:
            tq_st = emit_tq_pass1(ctx, ph, g, dram, "w2T", "w2q", 1,
                                  stg=("q_tm", "y_tm"), bankname="p2")
        if prefix and s == 4:
            qspills.append(emit_tq_finish(ctx, ph, g, dram, tq_st))
        if upto in ("pre_ln", "l0_ln") and s == 0:
            return
        if upto in ("pre_scan", "l0_scan") and s == 0:
            return
        if upto in ("l0_y", "l0_at", "l0_at2") and not prefix and s == 0:
            return
    if upto == "pre_full" and prefix:
        emit_debug_dump(ctx, g, dram, g[ph.cur_S].ap()[:, 0:16 * 132])
        return
    ctx.l0_S_cur = ph.cur_S
    ctx.l0_S_tok = _tok_max([ph.t(f"S_{hg}_{m}") for hg in range(8) for m in range(M)])
    return Tok2([t for q in qspills for t in q.toks])


def emit_l0_stripe(ctx, ph, g, dram, s, x_src, prefix, t_ones, upto=None):
    op, dma = ctx.op, ctx.dma
    t0 = s * STRIPE

    x_tm, xn_tm = g["x_tm"].ap(), g["xn_tm"].ap()
    xfh, xfl = g["xnfh"].ap(), g["xnfl"].ap()
    for i in range(NT):
        xt = x_tm[:, i * D:(i + 1) * D]
        tk = dma("sync" if prefix else "gpsimd", xt,
                 x_src[t0 + i * 128: t0 + (i + 1) * 128, :],
                 deps=[ph.t(f"x_free{i}")] if ph.t(f"x_free{i}") else ())
        xn = xn_tm[:, i * D:(i + 1) * D]
        tln = emit_ln_tile(ctx, g, xt, xn, [tk, ph.t(f"xn_free{i}"), ph.t("misc_free")], so=4 * i)
        ph.set("misc_free", tln)
        ph.set(f"ln{i}", tln)
        ph.set(f"x_free{i}", tln)
        for c in range(8):
            src = xn[:, c * 128:(c + 1) * 128]
            co = c * STRIPE + i * 128
            bank = "p0" if (c + i) % 2 == 0 else "p1"
            dep = [tln, ph.bank_free.get(bank), ph.t("xnf_free")]
            pb = g[bank].ap()
            tm = op("tensor", lambda h, s_=src, p_=pb: h.transpose(p_[:, 0:128], s_, g["c_ident"].ap()),
                    deps=dep)
            te = op("scalar", lambda h, p_=pb, co_=co: h.copy(xfh[:, co_:co_ + 128], p_[:, 0:128]),
                    deps=[tm])
            te2 = op("vector", lambda h, p_=pb, co_=co: h.tensor_tensor(
                xfl[:, co_:co_ + 128], p_[:, 0:128], xfh[:, co_:co_ + 128], ALU.subtract),
                deps=[te])
            ph.bank_free[bank] = te2
            ph.set(f"xnf_{c}_{i}", te2)
    sp_xnf = []
    if not prefix:
        for c in range(8):
            dd = [ph.t(f"xnf_{c}_{i}") for i in range(NT)]
            sp_xnf.append(dma("scalar", dram["xnh_fm"][c * 128:(c + 1) * 128, t0:t0 + STRIPE],
                              xfh[:, c * STRIPE:(c + 1) * STRIPE], deps=dd))
            sp_xnf.append(dma("scalar", dram["xnl_fm"][c * 128:(c + 1) * 128, t0:t0 + STRIPE],
                              xfl[:, c * STRIPE:(c + 1) * STRIPE], deps=dd))
    if upto in ("pre_ln", "l0_ln"):
        emit_debug_dump(ctx, g, dram, g["xn_tm"].ap()[:, 0:NT * D])
        return

    xnf_toks = [ph.t(f"xnf_{c}_{i}") for c in range(8) for i in range(NT)]

    emit_l0_beta(ctx, ph, g, dram, s, xnf_toks)

    if not prefix:
        emit_proj_tm(ctx, ph, g, dram["wm_hl"], g["mix_tm"].ap(), 2 * NH,
                     xnf_toks, key="mix", bias_rep=g["c_bsmall"].ap(), bias_dram=dram["bmix_rep"])
        emit_mix_finish(ctx, ph, g)
        emit_proj_tm(ctx, ph, g, dram["wq_hl"], g["q_tm"].ap(), D, xnf_toks, key="q")
    emit_proj_tm(ctx, ph, g, dram["wk_hl"], g["k_tm"].ap(), D, xnf_toks, key="k")
    emit_proj_tm_vaug(ctx, ph, g, dram["wv_hl"], xnf_toks, t_ones)

    if not prefix:
        emit_rope_phi(ctx, ph, g, s, "q")
        emit_qk_scale(ctx, ph, g, "q")
    emit_rope_phi(ctx, ph, g, s, "k")
    emit_qk_scale(ctx, ph, g, "k", prefix=prefix)

    for ci in range(STRIPE // CH):
        emit_scan_chunk(ctx, ph, g, s, ci, prefix, dram=dram,
                        upto=(upto if s == 0 and ci == 0 else None))
        if upto == "l0_at2" and ci == 0 and not prefix:
            return
        if upto in ("pre_scan", "l0_scan") and ci == 0:
            emit_debug_dump(ctx, g, dram, g[ph.cur_S].ap()[:, 0:16 * 132])
            return
        if upto == "l0_at" and ci == 0:
            return

    keys = ["a_done", "k_done", "v_done"] + ([] if prefix else ["mix_done", "q_done"])
    ph.set("xnf_free", Tok2([ph.t(k) for k in keys] + sp_xnf))
    for i in range(NT):
        if not prefix:
            ph.set(f"q_free{i}", Tok2([ph.t(f"qh{m_}_{i}") for m_ in range(M)]))
        ph.set(f"k_free{i}", Tok2([ph.t(f"kt{m_}_{i}") for m_ in range(M)]))
    lastc = STRIPE // CH - 1
    vfree = [ph.t(f"S_{hg}_{m_}") for hg in range(8) for m_ in range(M)]
    if not prefix:
        vfree += [ph.t(f"y_{hg}_{lastc}") for hg in range(8)]
    ph.set("vaug_free", Tok2(vfree))
    ph.set("beta_free", Tok2(vfree + [ph.t("aCd_t1")]))
    if not prefix:
        for i in range(NT):
            ph.set(f"mix_free{i}", Tok2([ph.t(f"s_{hg}_{m_}") for hg in range(8) for m_ in range(M)]))

    if prefix:
        return

    if upto == "l0_y" and s == 0:
        emit_debug_dump(ctx, g, dram, g["y_tm"].ap()[:, 0:NT * D])
        return

    emit_l0_attn_out(ctx, ph, g, dram, s)


def emit_load_consts(ctx, g, dram):
    d = ctx.dma
    d("sync", g["c_ident"].ap(), dram["ident"])
    d("sync", g["c_identh"].ap(), dram["identh"])
    d("sync", g["c_mask"].ap(), dram["mask_causal"])
    d("sync", g["c_mstate"].ap(), dram["mask_state"])
    d("sync", g["c_lng"].ap(), dram["ln_g_rep"])
    d("sync", g["c_lnb"].ap(), dram["ln_b_rep"])
    d("sync", g["c_GA"].ap(), dram["GA"])
    d("sync", g["c_pscale"].ap(), dram["pscale"])
    d("sync", g["c_MB"].ap(), dram["MB"])
    d("sync", g["c_bbeta"].ap(), dram["bbeta_col"])
    ctx.op("vector", lambda h: h.memset(g["c_zero"].ap(), 0.0))
    ctx.op("vector", lambda h: h.memset(g["c_eps"].ap()[:, 0:1], LN_EPS))
    ctx.op("vector", lambda h: h.memset(g["c_eps"].ap()[:, 1:2], EPS_DEN))
    ctx.op("vector", lambda h: h.memset(g["c_eps"].ap()[:, 2:3], -1.0))
    ctx.op("vector", lambda h: h.memset(g["c_ones"].ap(), 1.0))
    ctx.op("vector", lambda h: h.memset(g["c_eps"].ap()[:, 3:4], 128.0))
    ctx.op("vector", lambda h: h.memset(g["c_eps"].ap()[:, 4:5], -128.0))
    ctx.op("vector", lambda h: h.memset(g["c_eps"].ap()[:, 5:6], 1.0 / 127.0))


def emit_proj_tm(ctx, ph, g, wsrc, out_ap, ncols, xnf_toks, key,
                 bias_rep=None, bias_dram=None, banks=("p0", "p1", "p2", "p3"),
                 evac_fn=None):
    op, dma = ctx.op, ctx.dma
    halves = [(h * 512, min(512, ncols - h * 512)) for h in range((ncols + 511) // 512)]
    xh, xl = g["xnfh"].ap(), g["xnfl"].ap()
    if bias_dram is not None and ph.t(key + "_bias") is None:
        ph.set(key + "_bias", dma("sync", bias_rep, bias_dram))
    mm_tok = {}
    slots = (("wAh", 0, "wfree_Ah"), ("wBh", 0, "wfree_Bh"),
             ("hh", 0, "wfree_hh0"), ("hh", 4096, "wfree_hh1"))
    for c in range(8):
        bn, soff, wfk = slots[c % 4]
        wbuf = g[bn]
        wtok = dma("sync", wbuf.ap()[:, soff:soff + 2 * ncols], wsrc[c * 128:(c + 1) * 128, :],
                   deps=[ph.t(wfk)])
        wtok2 = wtok
        whi = wbuf.ap()[:, soff:soff + ncols]
        wlo = wbuf.ap()[:, soff + ncols:soff + 2 * ncols]
        last = None
        for i in range(NT):
            lh = xh[:, c * STRIPE + i * 128: c * STRIPE + (i + 1) * 128]
            ll = xl[:, c * STRIPE + i * 128: c * STRIPE + (i + 1) * 128]
            for hi_, (off, n) in enumerate(halves):
                bk = banks[i * len(halves) + hi_]
                bank = g[bk].ap()
                deps = [wtok, wtok2]
                if c == 0:
                    deps = deps + xnf_toks + [ph.bank_free.get(bk)]
                last = op("tensor", lambda h, b_=bank, l_=lh, w_=whi[:, off:off + n], n_=n,
                          s_=(c == 0): h.matmul(b_[:, 0:n_], l_, w_, start=s_, stop=False),
                          deps=deps)
                last = op("tensor", lambda h, b_=bank, l_=ll, w_=whi[:, off:off + n], n_=n:
                          h.matmul(b_[:, 0:n_], l_, w_, start=False, stop=False))
                last = op("tensor", lambda h, b_=bank, l_=lh, w_=wlo[:, off:off + n], n_=n,
                          e_=(c == 7): h.matmul(b_[:, 0:n_], l_, w_, start=False, stop=e_))
                mm_tok[(i, hi_)] = last
        ph.set(wfk, last)
    ev_toks = []
    for i in range(NT):
        for hi_, (off, n) in enumerate(halves):
            bk = banks[i * len(halves) + hi_]
            bank = g[bk].ap()
            if evac_fn is not None:
                ev = evac_fn(i, hi_, off, n, bank, [mm_tok[(i, hi_)]])
            elif bias_rep is not None:
                ev = op("vector", lambda h, b_=bank, n_=n, o_=off, i_=i:
                        h.tensor_tensor(out_ap[:, i_ * ncols + o_: i_ * ncols + o_ + n_],
                                        b_[:, 0:n_], bias_rep[:, o_:o_ + n_], ALU.add),
                        deps=[mm_tok[(i, hi_)], ph.t(key + "_bias")])
            else:
                ev = op("scalar", lambda h, b_=bank, n_=n, o_=off, i_=i:
                        h.copy(out_ap[:, i_ * ncols + o_: i_ * ncols + o_ + n_], b_[:, 0:n_]),
                        deps=[mm_tok[(i, hi_)]])
            ph.bank_free[bk] = ev
            ev_toks.append(ev)
    ph.set(key + "_done", ev_toks[-1])
    ph.set(key + "_evs", ev_toks)
    return ev_toks


def emit_proj_tm_vaug(ctx, ph, g, wT_dram, xnf_toks, t_ones):
    op = ctx.op
    VROW = 8 * 132

    def evac(i, hi, off, n, bank, deps):
        dst = bass.AP(g["v_aug"], i * VROW + (off // 128) * 132,
                      [[NT * VROW, 128], [132, n // 128], [33, 4], [1, 32]])
        return op("scalar", lambda h: h.copy(dst, bank[:, 0:n]),
                  deps=deps + [t_ones, ph.t("vaug_free")])

    return emit_proj_tm(ctx, ph, g, wT_dram, None, D, xnf_toks, key="v",
                        evac_fn=evac)


def emit_l0_beta(ctx, ph, g, dram, s, xnf_toks):
    op, dma = ctx.op, ctx.dma
    beta, a_fm, ra_fm = g["beta_fm"].ap(), g["a_fm"].ap(), g["ra_fm"].ap()
    win = bass.AP(dram["wb_hl"].tensor, 0, [[128, 128], [128 * 128, 8], [1, 128]])
    wout = bass.AP(g["wAh"], 0, [[2048, 128], [128, 8], [1, 128]])
    wtok = dma("sync", wout, win, deps=[ph.t("wfree_Ah")])
    wtok2 = wtok
    xfh, xfl = g["xnfh"].ap(), g["xnfl"].ap()
    bank = g["p0"].ap()
    last = None
    for c in range(8):
        wh = g["wAh"].ap()[:, c * 128:c * 128 + 64]
        wl = g["wAh"].ap()[:, c * 128 + 64:c * 128 + 128]
        rh = xfh[:, c * STRIPE:(c + 1) * STRIPE]
        rl = xfl[:, c * STRIPE:(c + 1) * STRIPE]
        last = op("tensor", lambda h, w_=wh, r_=rh, s_=(c == 0): h.matmul(
            bank[0:64, 0:STRIPE], w_, r_, start=s_, stop=False),
            deps=[wtok, wtok2] + (xnf_toks + [ph.bank_free.get("p0")] if c == 0 else []))
        last = op("tensor", lambda h, w_=wh, r_=rl: h.matmul(
            bank[0:64, 0:STRIPE], w_, r_, start=False, stop=False))
        last = op("tensor", lambda h, w_=wl, r_=rh, e_=(c == 7): h.matmul(
            bank[0:64, 0:STRIPE], w_, r_, start=False, stop=e_))
    ph.set("wfree_Ah", last)
    t1 = op("scalar", lambda h: h.activation(beta, bank[0:64, 0:STRIPE], AF.Sigmoid,
                                             bias=g["c_bbeta"].ap()), deps=[last])
    ph.bank_free["p0"] = t1
    t2 = op("vector", lambda h: h.tensor_scalar(beta, beta, BMIN, BMAX, ALU.max, ALU.min),
            deps=[t1])
    sc_toks = []
    for ci in range(STRIPE // CH):
        sc = op("vector", lambda h, c_=ci: h.tensor_tensor_scan(
            a_fm[:, c_ * CH:(c_ + 1) * CH], beta[:, c_ * CH:(c_ + 1) * CH],
            g["c_zero"].ap()[0:2 * NH, 0:CH], 1.0, ALU.mult, ALU.add), deps=[t2])
        sc_toks.append(sc)
    tr = op("vector", lambda h: h.reciprocal(ra_fm, a_fm), deps=sc_toks)
    ph.set("a_done", tr)
    for i in range(NT):
        for nm, src_, dst in (("a", a_fm, g["a_tm"].ap()), ("ra", ra_fm, g["ra_tm"].ap())):
            bank2 = g["p1"].ap()
            tm = op("tensor", lambda h, s_=src_, i_=i: h.transpose(
                bank2[:, 0:64], s_[:, i_ * 128:(i_ + 1) * 128], g["c_ident"].ap()[0:64, 0:64]),
                deps=[tr, ph.bank_free.get("p1")])
            te = op("scalar", lambda h, d_=dst, i_=i: h.copy(d_[:, i_ * 64:(i_ + 1) * 64],
                                                             bank2[:, 0:64]), deps=[tm])
            ph.bank_free["p1"] = te
            ph.set(f"atm_{i}", te)
    return


def emit_mix_finish(ctx, ph, g):
    op = ctx.op
    mix = g["mix_tm"].ap()
    for i in range(NT):
        u0 = mix[:, i * 64: i * 64 + 32]
        u1 = mix[:, i * 64 + 32: i * 64 + 64]
        d01 = g["misc"].ap()[:, i * 32:(i + 1) * 32]
        t1 = op("vector", lambda h, u0_=u0, u1_=u1, d_=d01:
                h.tensor_tensor(d_, u0_, u1_, ALU.subtract),
                deps=[ph.t("mix_done"), ph.t("misc_free")])
        t2 = op("scalar", lambda h, u0_=u0, d_=d01:
                h.activation(u0_, d_, AF.Sigmoid), deps=[t1])
        t3 = op("scalar", lambda h, u1_=u1, d_=d01:
                h.activation(u1_, d_, AF.Sigmoid, scale=g["c_eps"].ap()[:, 2:3]), deps=[t1])
        ph.set(f"mixf_{i}", t3)
        ph.set(f"mixf2_{i}", t2)
        ph.set("misc_free", Tok2([t2, t3, ph.t("misc_free")]))


def emit_rope_phi(ctx, ph, g, s, key):
    op = ctx.op
    buf = g[key + "_tm"]
    FL = NT * D
    for i in range(NT):
        gi = s * NT + i
        q = buf.ap()[:, i * D:(i + 1) * D]
        qc, qs = g["misc"].ap()[:, 0:D], g["misc2"].ap()[:, 0:D]
        cosap = bass.AP(g["c_cos"], gi * 16, [[16 * (T // 128), 128], [0, 32], [0, 2], [1, 16]])
        sinap = bass.AP(g["c_sin"], gi * 16, [[16 * (T // 128), 128], [0, 32], [0, 2], [1, 16]])
        dep = [ph.t(key + "_done"), ph.t("tbl"), ph.t("misc_free"), ph.t("misc2_free")]
        t1 = op("vector", lambda h, q_=q, d_=qc, c_=cosap: h.tensor_tensor(d_, q_, c_, ALU.mult), deps=dep)
        t2 = op("vector", lambda h, q_=q, d_=qs, s_=sinap: h.tensor_tensor(d_, q_, s_, ALU.mult), deps=dep)
        lo = lambda t, o: bass.AP(t, o, [[D, 128], [32, 32], [1, 16]])
        loq = lambda o: bass.AP(buf, i * D + o, [[FL, 128], [32, 32], [1, 16]])
        t3 = op("vector", lambda h: h.tensor_tensor(loq(0), lo(g["misc"], 0), lo(g["misc2"], 16), ALU.subtract),
                deps=[t1, t2])
        t4 = op("vector", lambda h: h.tensor_tensor(loq(16), lo(g["misc2"], 0), lo(g["misc"], 16), ALU.add),
                deps=[t1, t2])
        t5 = op("vector", lambda h, q_=q: h.tensor_scalar(qc, q_, 0.0, None, ALU.min), deps=[t3, t4])
        t6 = op("scalar", lambda h: h.activation(qc, qc, AF.Exp), deps=[t5])
        t7 = op("vector", lambda h, q_=q: h.scalar_tensor_tensor(q_, q_, 0.0, qc, ALU.max, ALU.add),
                deps=[t6])
        ph.set(f"{key}phi_{i}", t7)
        ph.set("misc_free", t7)
        ph.set("misc2_free", t4)


def emit_qk_scale(ctx, ph, g, key, prefix=False):
    op = ctx.op
    src = g[key + "_tm"].ap()
    sc = g["a_tm"] if key == "q" else g["ra_tm"]
    for m in range(M):
        dstbuf = g[("qh_tm", "qh_tm2")[m] if key == "q" else ("kt_tm", "kt_tm2")[m]]
        for i in range(NT):
            scap = bass.AP(sc, i * 64 + m * 32, [[NT * 64, 128], [1, 32], [0, 32]])
            deps = [ph.t(f"{key}phi_{i}"), ph.t(f"atm_{i}")]
            if prefix:
                t = op("vector", lambda h, d_=dstbuf.ap()[:, i * D:(i + 1) * D],
                       s_=src[:, i * D:(i + 1) * D], a_=scap:
                       h.scalar_tensor_tensor(d_, s_, g["c_pscale"].ap(), a_,
                                              ALU.mult, ALU.mult), deps=deps)
            else:
                t = op("vector", lambda h, d_=dstbuf.ap()[:, i * D:(i + 1) * D],
                       s_=src[:, i * D:(i + 1) * D], a_=scap:
                       h.tensor_tensor(d_, s_, a_, ALU.mult), deps=deps)
            ph.set(f"{'qh' if key == 'q' else 'kt'}{m}_{i}", t)


def emit_scan_chunk(ctx, ph, g, s, ci, prefix, dram=None, upto=None):
    op = ctx.op
    i = ci
    VROW = 8 * 132
    vaug = g["v_aug"]
    Scur, Snew = g[ph.cur_S].ap(), g[ph.new_S].ap()

    ce = ci * CH + CH - 1
    aC = g["aC_small"].ap()
    t1 = op("vector", lambda h: h.tensor_scalar(aC[:, 0:16], g["c_MB"].ap(),
                                                g["a_fm"].ap()[:, ce:ce + 1], None, ALU.mult),
            deps=[ph.t("a_done"), ph.t("aCs_free")])
    ph.set("aCd_t1", t1)
    bank = g["p0"].ap()
    t2 = op("tensor", lambda h: h.matmul(bank[:, 0:16], g["c_GA"].ap()[0:64, :], aC[:, 0:16],
                                         start=True, stop=True),
            deps=[t1, ph.bank_free.get("p0")])
    t3 = op("scalar", lambda h: h.copy(g["aCd"].ap()[:, 0:16], bank[:, 0:16]), deps=[t2])
    ph.bank_free["p0"] = t3
    ph.set("aCs_free", t3)
    aCd_tok = t3

    seq = [(hg, m) for hg in range(8) for m in range(M)]

    def front(idx):
        hg, m = seq[idx]
        sl = idx % 2
        qh = g[("qh_tm", "qh_tm2")[m]]
        kt = g[("kt_tm", "kt_tm2")[m]]
        qh_fm = g["qh_fm"].ap()[:, sl * 128:(sl + 1) * 128]
        kt_fm = g["kt_fm"].ap()[:, sl * 128:(sl + 1) * 128]
        tb0, tb1 = ("p0", "p1") if sl == 0 else ("p1", "p0")
        tq = _tp128(ctx, ph, g, qh.ap()[:, i * D + hg * 128: i * D + (hg + 1) * 128],
                    qh_fm, tb0, [ph.t(f"qh{m}_{i}"), ph.t(f"qhfm_free{sl}")])
        tk = _tp128(ctx, ph, g, kt.ap()[:, i * D + hg * 128: i * D + (hg + 1) * 128],
                    kt_fm, tb1, [ph.t(f"kt{m}_{i}"), ph.t(f"ktfm_free{sl}")])
        at = g["at_sb"].ap()[:, sl * 512:(sl + 1) * 512]
        ev_toks = []
        mm1_toks = []
        for u in range(4):
            ab = ("p2", "p3")[u % 2]
            bank_a = g[ab].ap()
            mm = op("tensor", lambda h, u_=u, b_=bank_a: h.matmul(
                b_[:, 0:128], kt_fm[32 * u_:32 * (u_ + 1), :],
                qh_fm[32 * u_:32 * (u_ + 1), :],
                start=True, stop=True, tile_position=(32 * u_, 0)),
                deps=[tq, tk, ph.bank_free.get(ab)])
            mm1_toks.append(mm)
            ev = op("vector", lambda h, u_=u, b_=bank_a: h.tensor_tensor(
                at[:, u_ * 128:(u_ + 1) * 128], b_[:, 0:128], g["c_mask"].ap(),
                ALU.mult), deps=[mm, ph.t(f"at_free{sl}")])
            ph.bank_free[ab] = ev
            ev_toks.append(ev)
        ph.set(f"ktfm_free{sl}", mm1_toks[-1])
        ph.set(f"front_ev{sl}", ev_toks)

    def back(idx):
        hg, m = seq[idx]
        sl = idx % 2
        sblk = (hg * 2 + m) * 132
        Scur_blk = Scur[:, sblk:sblk + 132]
        Snew_blk = Snew[:, sblk:sblk + 132]
        qh_fm = g["qh_fm"].ap()[:, sl * 128:(sl + 1) * 128]
        kt = g[("kt_tm", "kt_tm2")[m]]
        ps_y = g[("p4", "p5")[m]].ap()
        ps_s = g[("p6", "p7")[m]].ap()
        at = g["at_sb"].ap()[:, sl * 512:(sl + 1) * 512]
        ev_toks = ph.t(f"front_ev{sl}")
        if not prefix:
            last = None
            for u in range(4):
                vap = bass.AP(vaug, i * VROW + hg * 132 + 33 * u, [[NT * VROW, 128], [1, 33]])
                last = op("tensor", lambda h, u_=u, v_=vap: h.matmul(
                    ps_y[:, 33 * u_:33 * u_ + 33], at[:, u_ * 128:(u_ + 1) * 128], v_,
                    start=(u_ == 0), stop=False),
                    deps=[ev_toks[u], ph.t("v_done")] + ([ph.bank_free.get(("p4", "p5")[m])] if u == 0 else []))
            mm3 = op("tensor", lambda h: h.matmul(ps_y[:, 0:132], qh_fm, Scur_blk,
                                                  start=False, stop=True),
                     deps=[ph.t(f"S_{hg}_{m}")])
            ph.set(f"at_free{sl}", mm3)
            ph.set(f"qhfm_free{sl}", mm3)
            ph.set(f"Sread_{hg}_{m}_y", mm3)
            scol = g["s_all"].ap()[:, m * 32 + hg * 4: m * 32 + hg * 4 + 4]
            dcol = g["dens"].ap()[:, m * 32 + hg * 4: m * 32 + hg * 4 + 4]
            dsrc = bass.AP(g[("p4", "p5")[m]], 32, [[512, 128], [33, 4]])
            td = op("scalar", lambda h: h.activation(dcol, dsrc, AF.Identity,
                                                     bias=g["c_eps"].ap()[:, 1:2]), deps=[mm3])
            tr = op("vector", lambda h: h.reciprocal(scol, dcol), deps=[td])
            tmx = op("vector", lambda h: h.tensor_tensor(
                scol, scol, g["mix_tm"].ap()[:, i * 64 + m * 32 + hg * 4:
                                             i * 64 + m * 32 + hg * 4 + 4], ALU.mult),
                deps=[tr, ph.t(f"mixf_{i}"), ph.t(f"mixf2_{i}")])
            ph.set(f"s_{hg}_{m}", tmx)
        else:
            ph.set(f"at_free{sl}", None)
            ph.set(f"qhfm_free{sl}", None)

        ktblk = kt.ap()[:, i * D + hg * 128: i * D + (hg + 1) * 128]
        vblk = bass.AP(vaug, i * VROW + hg * 132, [[NT * VROW, 128], [1, 132]])
        d1 = op("tensor", lambda h: h.matmul(ps_s[:, 0:132], g["c_ident"].ap(), Scur_blk,
                                             start=True, stop=False),
                deps=[ph.t(f"S_{hg}_{m}"), ph.bank_free.get(("p6", "p7")[m])])
        d2 = op("tensor", lambda h: h.matmul(ps_s[:, 0:132], ktblk, vblk,
                                             start=False, stop=True),
                deps=[ph.t(f"kt{m}_{i}"), ph.t("v_done")])
        ph.set(f"Sread_{hg}_{m}_s", d2)
        sev = op("vector", lambda h: h.scalar_tensor_tensor(
            Snew_blk, ps_s[:, 0:132], g["aCd"].ap()[:, hg * 2 + m: hg * 2 + m + 1],
            g["c_mstate"].ap(), ALU.mult, ALU.mult),
            deps=[d2, aCd_tok, ph.t(f"Snew_free_{hg}_{m}")])
        ph.bank_free[("p6", "p7")[m]] = sev
        ph.set(f"S_{hg}_{m}_next", sev)

        if not prefix and m == 1:
            p4, p5 = g["p4"].ap(), g["p5"].ap()
            ylast = []
            for u in range(4):
                yblk = g["y_tm"].ap()[:, i * D + hg * 128 + 32 * u: i * D + hg * 128 + 32 * (u + 1)]
                s0 = g["s_all"].ap()[:, hg * 4 + u: hg * 4 + u + 1]
                s1 = g["s_all"].ap()[:, 32 + hg * 4 + u: 32 + hg * 4 + u + 1]
                e1 = op("vector", lambda h, y_=yblk, u_=u, s_=s0: h.tensor_scalar(
                    y_, p4[:, 33 * u_:33 * u_ + 32], s_, None, ALU.mult),
                    deps=[ph.t(f"s_{hg}_0"), ph.t(f"Sread_{hg}_0_y"), ph.t(f"ytmfree_{i}")])
                e2 = op("vector", lambda h, y_=yblk, u_=u, s_=s1: h.scalar_tensor_tensor(
                    y_, p5[:, 33 * u_:33 * u_ + 32], s_, y_, ALU.mult, ALU.add),
                    deps=[e1, ph.t(f"s_{hg}_1"), ph.t(f"Sread_{hg}_1_y")])
                ylast.append(e2)
            ph.bank_free["p4"] = ylast[-1]
            ph.bank_free["p5"] = ylast[-1]
            ph.set(f"y_{hg}_{i}", ylast[-1])

    if prefix:
        for idx in range(16):
            back(idx)
    else:
        front(0)
        for idx in range(16):
            if idx + 1 < 16:
                front(idx + 1)
            back(idx)

    for hg in range(8):
        for m in range(M):
            rd = [ph.t(f"Sread_{hg}_{m}_s"), ph.t(f"Sread_{hg}_{m}_y")]
            ph.set(f"Snew_free_{hg}_{m}", _tok_max(rd))
            ph.set(f"S_{hg}_{m}", ph.t(f"S_{hg}_{m}_next"))
    ph.cur_S, ph.new_S = ph.new_S, ph.cur_S

    if upto == "l0_at2" and not prefix:
        emit_debug_dump(ctx, g, dram, g["dens"].ap()[:, 0:64], dst_off=0)
        emit_debug_dump(ctx, g, dram, g["s_all"].ap()[:, 0:64], dst_off=64)
        emit_debug_dump(ctx, g, dram, g["y_tm"].ap()[:, 0:D], dst_off=128)


def _tok_max(toks):
    toks = [t for t in toks if t is not None]
    if not toks:
        return None
    best = toks[0]
    for t in toks[1:]:
        if t.sem == best.sem:
            if t.val > best.val:
                best = t
        else:
            return Tok2(toks)
    return best


class Tok2:
    def __init__(self, toks):
        self.toks = [t for t in toks if t is not None]


L1SLOTS = (("wAh", 0), ("wBh", 0), ("hh", 0), ("hh", 2048),
           ("hh", 4096), ("hh", 6144), ("hl", 0), ("hl", 2048),
           ("hl", 6144))

L0FSLOTS = (("wAh", 0), ("wAh", 1024), ("wBh", 0), ("wBh", 1024),
            ("yh", 0), ("yh", 1024), ("yl", 0), ("yl", 1024))


def _tp128(ctx, ph, g, src_ap, dst_ap, bank, deps):
    pb = g[bank].ap()
    ident = g["c_identh" if src_ap.tensor.dtype == BF16 else "c_ident"].ap()
    tm = ctx.op("tensor", lambda h: h.transpose(pb[:, 0:128], src_ap, ident),
                deps=deps + [ph.bank_free.get(bank)])
    te = ctx.op("scalar", lambda h: h.copy(dst_ap, pb[:, 0:128]), deps=[tm])
    ph.bank_free[bank] = te
    return te


def emit_l0_attn_out(ctx, ph, g, dram, s):
    op, dma = ctx.op, ctx.dma
    t0 = s * STRIPE
    yh, yl = g["yh"].ap(), g["yl"].ap()
    tp_toks = {}
    for i in range(NT):
        for c in range(8):
            tb = ("p0", "p1")[c % 2]
            co = i * D + c * 128
            pb = g[tb].ap()
            tm = op("tensor", lambda h, s_=g["y_tm"].ap()[:, co:co + 128], p_=pb:
                    h.transpose(p_[:, 0:128], s_, g["c_ident"].ap()),
                    deps=[ph.t(f"y_{c}_{i}"), ph.bank_free.get(tb), ph.t("yhl_free")])
            te = op("scalar", lambda h, p_=pb, co_=co: h.copy(yh[:, co_:co_ + 128], p_[:, 0:128]),
                    deps=[tm])
            te2 = op("vector", lambda h, p_=pb, co_=co: h.tensor_tensor(
                yl[:, co_:co_ + 128], p_[:, 0:128], yh[:, co_:co_ + 128], ALU.subtract),
                deps=[te])
            ph.bank_free[tb] = te2
            tp_toks[(i, c)] = te2
    for i_ in range(NT):
        ph.set(f"ytmfree_{i_}", Tok2([tp_toks[(i_, c_)] for c_ in range(8)]))
    if ph.t("bout") is None:
        ph.set("bout", dma("sync", g["c_brep"].ap(), dram["bout_rep"]))
    banks = ["p2", "p3", "p4", "p5"]
    mm_tok = {}
    for c in range(8):
        bn = "Ah" if c % 2 == 0 else "Bh"
        wbuf = g["w" + bn]
        wtok = dma("sync", wbuf.ap()[:, 0:2 * D], dram["wo_hl"][c * 128:(c + 1) * 128, :],
                   deps=[ph.t("wfree_" + bn)])
        wtok2 = wtok
        last = None
        for i in range(NT):
            lh = yh[:, i * D + c * 128: i * D + (c + 1) * 128]
            ll = yl[:, i * D + c * 128: i * D + (c + 1) * 128]
            for hi in range(2):
                bk = banks[i * 2 + hi]
                deps = [wtok, wtok2, tp_toks[(i, c)]]
                if c == 0:
                    deps.append(ph.bank_free.get(bk))
                wh_ = wbuf.ap()[:, hi * 512:(hi + 1) * 512]
                wl_ = wbuf.ap()[:, D + hi * 512:D + (hi + 1) * 512]
                last = op("tensor", lambda h, b_=g[bk].ap(), l_=lh, w_=wh_, s_=(c == 0):
                          h.matmul(b_[:, 0:512], l_, w_, start=s_, stop=False), deps=deps)
                last = op("tensor", lambda h, b_=g[bk].ap(), l_=ll, w_=wh_:
                          h.matmul(b_[:, 0:512], l_, w_, start=False, stop=False))
                last = op("tensor", lambda h, b_=g[bk].ap(), l_=lh, w_=wl_, e_=(c == 7):
                          h.matmul(b_[:, 0:512], l_, w_, start=False, stop=e_))
                mm_tok[(i, hi)] = last
        ph.set("wfree_" + bn, last)
    ph.set("yhl_free", _tok_max([mm_tok[k] for k in mm_tok]))
    for i in range(NT):
        z = g["x_tm"].ap()[:, i * D:(i + 1) * D]
        zt = []
        for hi in range(2):
            bank = g[banks[i * 2 + hi]].ap()
            t1 = op("vector", lambda h, z_=z[:, hi * 512:(hi + 1) * 512], b_=bank,
                    o_=hi * 512: h.tensor_tensor(z_, b_[:, 0:512],
                                                 g["c_brep"].ap()[:, o_:o_ + 512], ALU.add),
                    deps=[mm_tok[(i, hi)], ph.t("bout"), ph.t(f"ln{i}")])
            t2 = op("vector", lambda h, z_=z[:, hi * 512:(hi + 1) * 512],
                    x_=g["xn_tm"].ap()[:, i * D + hi * 512: i * D + hi * 512 + 512]:
                    h.tensor_tensor(z_, z_, x_, ALU.add), deps=[t1])
            ph.bank_free[banks[i * 2 + hi]] = t2
            zt.append(t2)
        to = emit_ln_tile(ctx, g, z, g["xn_tm"].ap()[:, i * D:(i + 1) * D],
                          zt, affine=(g["c_lng"].ap(), g["c_lnb"].ap()),
                          scratch=(g["misc"] if i == 0 else g["misc2"]).ap()[:, 0:D],
                          so=8 + 4 * i)
        ph.set("misc_free" if i == 0 else "misc2_free", to)
        tk = dma("scalar", dram["attn0"][t0 + i * 128: t0 + (i + 1) * 128, :],
                 g["xn_tm"].ap()[:, i * D:(i + 1) * D], deps=[to])
        ph.set(f"x_free{i}", tk)
        ph.set(f"xn_free{i}", tk)


def emit_hilo_weight(ctx, ph, g, dram, wname, outname, col0=0, ncols=None):
    op, dma = ctx.op, ctx.dma
    W = dram[wname]
    nrows = W.shape[0]
    if ncols is None:
        ncols = W.shape[1] - col0
    cw = min(1024, ncols)
    nh = max(ncols // cw, 1)
    for c in range(nrows // 128):
        for hh in range(nh):
            bn = "A" if (c * nh + hh) % 2 == 0 else "B"
            win = g["x_tm" if bn == "A" else "xn_tm"]
            ob = g["wAh" if bn == "A" else "wBh"]
            obn = "wAh" if bn == "A" else "wBh"
            cc = cw
            wtok = dma("sync", win.ap()[:, 0:cc],
                       W[c * 128:(c + 1) * 128, col0 + hh * cw:col0 + hh * cw + cc],
                       deps=[ph.t("hw_in_" + bn)])
            t1 = op("scalar", lambda h: h.copy(ob.ap()[:, 0:cc], win.ap()[:, 0:cc]),
                    deps=[wtok, ph.t("hwfree_" + obn)])
            t2 = op("vector", lambda h: h.tensor_tensor(ob.ap()[:, cc:2 * cc], win.ap()[:, 0:cc],
                                                        ob.ap()[:, 0:cc], ALU.subtract),
                    deps=[t1, ph.t("hwfree_" + obn)])
            ph.set("hw_in_" + bn, t2)
            d1 = dma("scalar", dram[outname][c * 128:(c + 1) * 128, hh * cw:hh * cw + cc],
                     ob.ap()[:, 0:cc], deps=[t1])
            d2 = dma("scalar", dram[outname][c * 128:(c + 1) * 128, ncols + hh * cw:ncols + hh * cw + cc],
                     ob.ap()[:, cc:2 * cc], deps=[t2])
            ph.set("hwfree_" + obn, Tok2([d1, d2]))
    ph.set("wfree_Ah", ph.t("hwfree_wAh"))
    ph.set("wfree_Bh", ph.t("hwfree_wBh"))
    ph.set(f"x_free0", Tok2([ph.t("hw_in_A"), ph.t("hw_in_B")]))


def emit_weight_prep(ctx, g, dram):
    ph = Ph()
    emit_hilo_weight(ctx, ph, g, dram, "wqT", "wq_hl")
    emit_hilo_weight(ctx, ph, g, dram, "wkT", "wk_hl")
    emit_hilo_weight(ctx, ph, g, dram, "wvT", "wv_hl")
    emit_hilo_weight(ctx, ph, g, dram, "woutT", "wo_hl")
    emit_hilo_weight(ctx, ph, g, dram, "wqvT", "wqv_hl0", col0=0, ncols=D)
    emit_hilo_weight(ctx, ph, g, dram, "wqvT", "wqv_hl1", col0=D, ncols=D)
    emit_hilo_weight(ctx, ph, g, dram, "wout1T", "wo1_hl")
    emit_hilo_weight(ctx, ph, g, dram, "wbetaT", "wb_hl")
    emit_hilo_weight(ctx, ph, g, dram, "wmixT", "wm_hl")


def emit_tq_pass1(ctx, ph, g, dram, wname, outname, qidx,
                  l1=False, gcol=None,
                  stg=("x_tm", "xn_tm"), outb=("bfA", "bfB"),
                  bankname="p0", kp="tq", cwmax=1024):
    op, dma = ctx.op, ctx.dma
    W = dram[wname]
    nrows, ncols = W.shape
    nch = nrows // 128
    cw = min(cwmax, ncols)
    nh = ncols // cw
    st = g["c_stats"].ap()
    tot = st[:, 10:11]
    acc = st[:, 9:10]
    tz = op("vector", lambda h: h.memset(tot, 0.0), deps=[ph.t(kp + "stats_free")])
    prev = tz
    for ci in range(nch * nh):
        c, hh = divmod(ci, nh)
        wa = g[stg[ci % 2]]
        wfk = kp + ("wfA" if ci % 2 == 0 else "wfB")
        wtok = dma("gpsimd", wa.ap()[:, 0:cw],
                   W[c * 128:(c + 1) * 128, hh * cw:(hh + 1) * cw],
                   deps=[ph.t(wfk)])
        ta = op("scalar", lambda h, w_=wa: h.activation(w_.ap()[:, 0:cw], w_.ap()[:, 0:cw],
                                                        AF.Abs, accum_out=acc),
                deps=[wtok, prev])
        ph.set(wfk, ta)
        prev = op("vector", lambda h: h.tensor_tensor(tot, tot, acc, ALU.add),
                  deps=[ta, prev])
    return dict(W=W, outname=outname, qidx=qidx, l1=l1, gcol=gcol, stg=stg,
                outb=outb, bankname=bankname, kp=kp, cw=cw, nh=nh, nch=nch,
                nrows=nrows, ncols=ncols, prev=prev)


def emit_tq_finish(ctx, ph, g, dram, stq):
    op, dma = ctx.op, ctx.dma
    W, outname, l1, gcol = stq["W"], stq["outname"], stq["l1"], stq["gcol"]
    stg, outb, bankname, kp = stq["stg"], stq["outb"], stq["bankname"], stq["kp"]
    cw, nh, nch = stq["cw"], stq["nh"], stq["nch"]
    st = g["c_stats"].ap()
    tot = st[:, 10:11]
    bank = g[bankname].ap()
    t1 = op("tensor", lambda h: h.matmul(bank[0:1, 0:1], g["c_ones"].ap()[:, 0:1], tot,
                                         start=True, stop=True),
            deps=[stq["prev"], ph.bank_free.get(bankname)])
    cell = st[0:1, 11:12]
    sc_cell = st[0:1, 12:13]
    b_cell = st[0:1, 13:14]
    tm1 = op("vector", lambda h: h.memset(sc_cell, 1.0 / (stq["nrows"] * stq["ncols"])),
             deps=[ph.t(kp + "stats_free")])
    if not l1:
        tm2 = op("vector", lambda h: h.memset(b_cell, 1e-5), deps=[ph.t(kp + "stats_free")])
        t2 = op("scalar", lambda h: h.activation(cell, bank[0:1, 0:1], AF.Identity,
                                                 scale=sc_cell, bias=b_cell), deps=[t1, tm1, tm2])
    else:
        t2 = op("scalar", lambda h: h.activation(cell, bank[0:1, 0:1], AF.Identity,
                                                 scale=sc_cell), deps=[t1, tm1])
        t2 = op("vector", lambda h: h.tensor_scalar(cell, cell, 1e-5, None, ALU.max), deps=[t2])
    thr_cell = st[0:1, 13:14]
    t2b = op("vector", lambda h: h.tensor_scalar(thr_cell, cell, 0.5, None, ALU.mult), deps=[t2])
    ph.bank_free[bankname] = t2b
    thr = st[:, 14:15]
    nthr = st[:, 15:16]
    t3 = op("tensor", lambda h: h.matmul(bank[:, 3:4], g["c_ones"].ap()[0:1, :], thr_cell,
                                         start=True, stop=True), deps=[t2b, ph.bank_free.get(bankname)])
    t4 = op("scalar", lambda h: h.copy(thr, bank[:, 3:4]), deps=[t3])
    ph.bank_free[bankname] = t4
    if l1:
        t3g = op("tensor", lambda h: h.matmul(bank[:, 4:5], g["c_ones"].ap()[0:1, :], cell,
                                              start=True, stop=True), deps=[t2, t4])
        t4g = op("scalar", lambda h, gc_=gcol: h.copy(g["c_gam"].ap()[:, gc_:gc_ + 1], bank[:, 4:5]),
                 deps=[t3g])
        ph.bank_free[bankname] = t4g
        ph.set("gam", Tok2([t4g, ph.t("gam")]))
    t5 = op("vector", lambda h: h.tensor_scalar(nthr, thr, -1.0, None, ALU.mult), deps=[t4])
    dma("sync", dram["gammas"][stq["qidx"]:stq["qidx"] + 1, :], cell, deps=[t2])
    stoks = []
    for ci in range(nch * nh):
        c, hh = divmod(ci, nh)
        wa = g[stg[ci % 2]]
        wfk = kp + ("wfA" if ci % 2 == 0 else "wfB")
        ob = g[outb[ci % 2]]
        obk = kp + ("obA" if ci % 2 == 0 else "obB")
        wtok = dma("gpsimd", wa.ap()[:, 0:cw],
                   W[c * 128:(c + 1) * 128, hh * cw:(hh + 1) * cw],
                   deps=[ph.t(wfk)])
        q1 = op("vector", lambda h, w_=wa, o_=ob: h.tensor_scalar(
            o_.ap()[:, 0:cw], w_.ap()[:, 0:cw], nthr, None, ALU.is_le),
            deps=[wtok, t5, ph.t(obk)])
        q2 = op("vector", lambda h, w_=wa, o_=ob: h.scalar_tensor_tensor(
            o_.ap()[:, 0:cw], w_.ap()[:, 0:cw], thr,
            o_.ap()[:, 0:cw], ALU.is_ge, ALU.subtract), deps=[q1])
        ph.set(wfk, q2)
        stok = dma("scalar", dram[outname][c * 128:(c + 1) * 128, hh * cw:(hh + 1) * cw],
                   ob.ap()[:, 0:cw], deps=[q2])
        ph.set(obk, stok)
        stoks.append(stok)
    ph.set(kp + "stats_free", t5)
    return Tok2(stoks)


def emit_l0_ffn(ctx, g, dram, upto=None, w_ready=None):
    op, dma = ctx.op, ctx.dma
    ph = Ph()
    if w_ready is not None:
        for sl_ in range(8):
            ph.set(f"l0f_{sl_}", w_ready)
    qspills = []
    tb1 = dma("sync", g["c_bcol"].ap(), dram["b1_col"])
    tb2 = dma("sync", g["c_brep"].ap(), dram["b2_rep"])
    ctx.barrier()
    hbufs = ["q_tm", "k_tm", "qh_tm", "qh_tm2"]

    xfh, xfl = g["xnfh"].ap(), g["xnfl"].ap()

    def xn_loads(s_):
        t0_ = s_ * STRIPE
        tl = []
        for c in range(8):
            tl.append(dma("gpsimd", xfh[:, c * STRIPE:(c + 1) * STRIPE],
                          dram["xnh_fm"][c * 128:(c + 1) * 128, t0_:t0_ + STRIPE],
                          deps=[ph.t("xnf_war")]))
            tl.append(dma("gpsimd", xfl[:, c * STRIPE:(c + 1) * STRIPE],
                          dram["xnl_fm"][c * 128:(c + 1) * 128, t0_:t0_ + STRIPE],
                          deps=[ph.t("xnf_war")]))
        return tl
    xtoks = xn_loads(0)
    for s in range(T // STRIPE):
        t0 = s * STRIPE
        for fg in range(4):
            mmtok = {}
            for dc in range(8):
                sl = (fg * 8 + dc) % 8
                wbuf, woff = L0FSLOTS[sl]
                wtok = dma("sync", g[wbuf].ap()[:, woff:woff + 1024],
                           dram["w1q"][dc * 128:(dc + 1) * 128, fg * 1024:(fg + 1) * 1024],
                           deps=[ph.t(f"l0f_{sl}")])
                last = None
                for fl in range(8):
                    fc = fg * 8 + fl
                    bank = g[f"p{fl}"].ap()
                    deps = [wtok, xtoks[2 * dc], xtoks[2 * dc + 1]]
                    if dc == 0:
                        deps.append(ph.bank_free.get(f"p{fl}"))
                    wt = g[wbuf].ap()[:, woff + fl * 128:woff + (fl + 1) * 128]
                    last = op("tensor", lambda h, b_=bank, w_=wt,
                              x_=xfh[:, dc * STRIPE:(dc + 1) * STRIPE], s_=(dc == 0):
                              h.matmul(b_[:, 0:STRIPE], w_, x_, start=s_, stop=False), deps=deps)
                    last = op("tensor", lambda h, b_=bank, w_=wt,
                              x_=xfl[:, dc * STRIPE:(dc + 1) * STRIPE], e_=(dc == 7):
                              h.matmul(b_[:, 0:STRIPE], w_, x_, start=False, stop=e_))
                    mmtok[fl] = last
                ph.set(f"l0f_{sl}", last)
            for fl in range(8):
                fc = fg * 8 + fl
                rscr = g["q_tm"].ap()[:, fl * STRIPE:(fl + 1) * STRIPE]
                hi = g["hh"].ap()[:, fc * STRIPE:(fc + 1) * STRIPE]
                lo = g["hl"].ap()[:, fc * STRIPE:(fc + 1) * STRIPE]
                ev = op("scalar", lambda h, d_=rscr, b_=g[f"p{fl}"].ap(), fc_=fc:
                        h.activation(d_, b_[:, 0:STRIPE], AF.Relu,
                                     bias=g["c_bcol"].ap()[:, fc_:fc_ + 1]),
                        deps=[mmtok[fl], tb1, ph.t(f"rscr_{fl}")])
                e2 = op("vector", lambda h, d_=hi, r_=rscr: h.tensor_copy(d_, r_),
                        deps=[ev, ph.t(f"hfree_{fc}")])
                e3 = op("vector", lambda h, d_=lo, r_=rscr, i_=hi: h.tensor_tensor(
                    d_, r_, i_, ALU.subtract), deps=[e2])
                ph.bank_free[f"p{fl}"] = ev
                ph.set(f"h_{fc}", e3)
                ph.set(f"rscr_{fl}", e3)
        ph.set("xnf_war", Tok2([ph.t(f"h_{fc}") for fc in range(32)]))
        if s + 1 < T // STRIPE:
            next_xtoks = xn_loads(s + 1)
        banks = ["p0", "p1", "p2", "p3"]
        mmtok = {}
        for fc in range(32):
            sl = fc % 8
            wbuf, woff = L0FSLOTS[sl]
            wtok = dma("sync", g[wbuf].ap()[:, woff:woff + 1024],
                       dram["w2q"][fc * 128:(fc + 1) * 128, :], deps=[ph.t(f"l0f_{sl}")])
            last = None
            for i in range(NT):
                lh = g["hh"].ap()[:, fc * STRIPE + i * 128: fc * STRIPE + (i + 1) * 128]
                ll = g["hl"].ap()[:, fc * STRIPE + i * 128: fc * STRIPE + (i + 1) * 128]
                for hi in range(2):
                    bk = banks[i * 2 + hi]
                    deps = [wtok, ph.t(f"h_{fc}")]
                    if fc == 0:
                        deps.append(ph.bank_free.get(bk))
                    w_ = g[wbuf].ap()[:, woff + hi * 512:woff + (hi + 1) * 512]
                    last = op("tensor", lambda h, b_=g[bk].ap(), l_=lh, ww_=w_,
                              s_=(fc == 0): h.matmul(b_[:, 0:512], l_, ww_, start=s_, stop=False),
                              deps=deps)
                    last = op("tensor", lambda h, b_=g[bk].ap(), l_=ll, ww_=w_,
                              e_=(fc == 31): h.matmul(b_[:, 0:512], l_, ww_, start=False, stop=e_))
                    mmtok[(i, hi)] = last
            ph.set(f"l0f_{sl}", last)
            ph.set(f"hfree_{fc}", last)
        for i in range(NT):
            a_tok = dma("gpsimd", g["x_tm"].ap()[:, i * D:(i + 1) * D],
                        dram["attn0"][t0 + i * 128: t0 + (i + 1) * 128, :],
                        deps=[ph.t(f"xtm_free{i}")])
            x_tok = dma("gpsimd", g["xn_tm"].ap()[:, i * D:(i + 1) * D],
                        dram["x0"][t0 + i * 128: t0 + (i + 1) * 128, :],
                        deps=[ph.t(f"xntm_free{i}")])
            otoks = []
            for hi in range(2):
                dst = g["y_tm"].ap()[:, i * D + hi * 512: i * D + hi * 512 + 512]
                t1 = op("vector", lambda h, d_=dst, b_=g[banks[i * 2 + hi]].ap(), o_=hi * 512:
                        h.tensor_tensor(d_, b_[:, 0:512], g["c_brep"].ap()[:, o_:o_ + 512], ALU.add),
                        deps=[mmtok[(i, hi)], tb2, ph.t(f"ytm_free{i}")])
                t2 = op("vector", lambda h, d_=dst, x_=g["x_tm"].ap()[:, i * D + hi * 512: i * D + hi * 512 + 512]:
                        h.tensor_tensor(d_, d_, x_, ALU.add), deps=[t1, a_tok])
                t3 = op("vector", lambda h, d_=dst, x_=g["xn_tm"].ap()[:, i * D + hi * 512: i * D + hi * 512 + 512]:
                        h.tensor_tensor(d_, d_, x_, ALU.add), deps=[t2, x_tok])
                ph.bank_free[banks[i * 2 + hi]] = t3
                otoks.append(t3)
            sp = dma("scalar", dram["x1"][t0 + i * 128: t0 + (i + 1) * 128, :],
                     g["y_tm"].ap()[:, i * D:(i + 1) * D], deps=otoks)
            ph.set(f"ytm_free{i}", sp)
            ph.set(f"xtm_free{i}", otoks[-1])
            ph.set(f"xntm_free{i}", otoks[-1])
        if s + 1 < T // STRIPE:
            xtoks = next_xtoks
        if s == 0:
            tq_st = emit_tq_pass1(ctx, ph, g, dram, "wgT", "wgq", 2, l1=True, gcol=0,
                                  stg=("k_tm", "qh_tm"))
        if s == 2:
            tq_st = emit_tq_pass1(ctx, ph, g, dram, "wuT", "wuq", 3, l1=True, gcol=1,
                                  stg=("k_tm", "qh_tm"))
        if s == 4:
            tq_st = emit_tq_pass1(ctx, ph, g, dram, "wdT", "wdq", 4, l1=True, gcol=2,
                                  stg=("k_tm", "qh_tm"))
        if s in (1, 3, 5):
            qspills.append(emit_tq_finish(ctx, ph, g, dram, tq_st))
    return Tok2([t for q in qspills for t in q.toks])


def emit_act_quant(ctx, ph, g, src_ap, dst_ap, sx_dst, deps, tag,
                   zb=None, zfk="misc_free"):
    op = ctx.op
    N = src_ap.shape[1]
    st = g["c_stats"].ap()
    amax = st[:, 6:7]
    t1 = op("vector", lambda h: h.reduce_max(amax, src_ap, AX.X, apply_absolute_value=True),
            deps=deps + [ph.t("aq_free")])
    t2 = op("vector", lambda h: h.tensor_scalar(amax, amax, 1e-5, None, ALU.max), deps=[t1])
    s_col = st[:, 7:8]
    t3 = op("vector", lambda h: h.reciprocal(s_col, amax), deps=[t2])
    t4 = op("vector", lambda h: h.tensor_scalar(s_col, s_col, 127.0, None, ALU.mult), deps=[t3])
    tr = op("vector", lambda h: h.reciprocal(sx_dst, s_col), deps=[t4, ph.t("sx_free")])
    z = (zb if zb is not None else g["misc"].ap())[:, 0:N]
    t5 = op("vector", lambda h: h.tensor_scalar(z, src_ap, s_col, None, ALU.mult),
            deps=[t4, ph.t(zfk)])
    bfa, bfb = g["bfA"].ap()[:, 0:N], g["bfB"].ap()[:, 0:N]
    t6 = op("scalar", lambda h: h.activation(bfa, z, AF.Identity, bias=g["c_eps"].ap()[:, 3:4]),
            deps=[t5, ph.t("bf_free")])
    t7 = op("scalar", lambda h: h.activation(bfb, z, AF.Identity, bias=g["c_eps"].ap()[:, 4:5]),
            deps=[t5, ph.t("bf_free")])
    t8 = op("vector", lambda h: h.tensor_scalar(bfa, bfa, 128.0, None, ALU.subtract), deps=[t6])
    t9 = op("vector", lambda h: h.tensor_scalar(bfb, bfb, 128.0, None, ALU.add), deps=[t7])
    msk = g["u8"].ap()[:, 0:N]
    ta = op("vector", lambda h: h.tensor_scalar(msk, z, 0.0, None, ALU.is_ge),
            deps=[t5, ph.t("u8_free")])
    tsel = op("vector", lambda h: h.select(dst_ap, msk, bfa, bfb, add_drain=True),
              deps=[t8, t9, ta])
    ph.set("aq_free", tsel)
    ph.set(zfk, tsel)
    ph.set("bf_free", tsel)
    ph.set("u8_free", tsel)
    ph.set("sx_free", tr)
    return tsel, tr


def _haar_block(ctx, ph, g, Q, Vv, hw_q, hw_v, scr_q, scr_v, deps):
    op = ctx.op
    NTOK = Q.shape[1]
    C = INV_SQRT2

    def lvl(dst_d, dst_a, src, n, dps):
        e, o = src[:, 0:n:2], src[:, 1:n:2]
        t1 = op("vector", lambda h: h.tensor_tensor(dst_d, e, o, ALU.subtract), deps=dps)
        t1 = op("vector", lambda h: h.tensor_scalar(dst_d, dst_d, C, None, ALU.mult), deps=[t1])
        t2 = op("vector", lambda h: h.tensor_tensor(dst_a, e, o, ALU.add), deps=dps)
        t2 = op("vector", lambda h: h.tensor_scalar(dst_a, dst_a, C, None, ALU.mult), deps=[t2])
        return t1, t2

    def wavedec(X, scr, dps):
        d1, a1 = scr[:, 128:256], scr[:, 0:128]
        t1, t2 = lvl(d1, a1, X, 256, dps)
        d2, a2 = X[:, 0:64], X[:, 64:128]
        t3, t4 = lvl(d2, a2, a1, 128, [t1, t2])
        d3, a3 = X[:, 128:160], X[:, 160:192]
        t5, t6 = lvl(d3, a3, a2, 64, [t3, t4])
        return (a3, d3, d2, d1), [t5, t6, t3, t1]

    (qa3, qd3, qd2, qd1), tq = wavedec(Q, scr_q, deps)
    (va3, vd3, vd2, vd1), tv = wavedec(Vv, scr_v, deps)
    prods = []
    for (cq, cv, j) in ((qa3, va3, 0), (qd3, vd3, 1), (qd2, vd2, 2)):
        t1 = op("vector", lambda h, cq_=cq, j_=j: h.tensor_scalar(
            cq_, cq_, hw_q[:, j_:j_ + 1], None, ALU.mult), deps=tq + tv + [ph.t("hw")])
        t2 = op("vector", lambda h, cv_=cv, j_=j: h.tensor_scalar(
            cv_, cv_, hw_v[:, j_:j_ + 1], None, ALU.mult), deps=tq + tv + [ph.t("hw")])
        t3 = op("vector", lambda h, cq_=cq, cv_=cv: h.tensor_tensor(cq_, cq_, cv_, ALU.mult),
                deps=[t1, t2])
        prods.append(t3)
    t3 = op("vector", lambda h: h.tensor_tensor(qd1, qd1, vd1, ALU.mult), deps=tq + tv)
    prods.append(t3)

    def idwt(a, d, out, n, dps):
        oe, oo = out[:, 0:2 * n:2], out[:, 1:2 * n:2]
        t1 = op("vector", lambda h: h.tensor_tensor(oe, a, d, ALU.add), deps=dps)
        t1 = op("vector", lambda h: h.tensor_scalar(oe, oe, C, None, ALU.mult), deps=[t1])
        t2 = op("vector", lambda h: h.tensor_tensor(oo, a, d, ALU.subtract), deps=dps)
        t2 = op("vector", lambda h: h.tensor_scalar(oo, oo, C, None, ALU.mult), deps=[t2])
        return [t1, t2]

    s1 = Q[:, 192:256]
    ts1 = idwt(qa3, qd3, s1, 32, prods)
    s2 = scr_q[:, 0:128]
    ts2 = idwt(s1, qd2, s2, 64, ts1 + prods)
    ts3 = idwt(s2, qd1, Q, 128, ts2 + prods)
    return ts3


def emit_l1_attn(ctx, g, dram, upto=None):
    op, dma = ctx.op, ctx.dma
    ph = Ph()
    thw = dma("sync", bass.AP(g["c_hw"], 0, [[48, 128], [3, 8], [1, 3]]),
              bass.AP(dram["hwq_col"].tensor, 0, [[3, 128], [128 * 3, 8], [1, 3]]))
    thw2 = dma("sync", bass.AP(g["c_hw"], 24, [[48, 128], [3, 8], [1, 3]]),
               bass.AP(dram["hwv_col"].tensor, 0, [[3, 128], [128 * 3, 8], [1, 3]]))
    ph.set("hw", Tok2([thw, thw2]))
    tbq = dma("sync", g["c_bcol"].ap()[:, 0:16], dram["bqv_col"])
    tbo = dma("sync", g["c_brep"].ap(), dram["bout1_rep"])
    for s in range(T // STRIPE):
        emit_l1_attn_stripe(ctx, ph, g, dram, s, tbq, tbo, upto)
        if upto in ("l1_haar",) and s == 0:
            return


def emit_l1_attn_stripe(ctx, ph, g, dram, s, tbq, tbo, upto=None):
    op, dma = ctx.op, ctx.dma
    t0 = s * STRIPE
    x_tm, xn_tm = g["x_tm"].ap(), g["xn_tm"].ap()
    xfh, xfl = g["xnfh"].ap(), g["xnfl"].ap()
    for i in range(NT):
        xt = x_tm[:, i * D:(i + 1) * D]
        tk = dma("gpsimd", xt, dram["x1"][t0 + i * 128: t0 + (i + 1) * 128, :],
                 deps=[ph.t(f"x_free{i}")])
        xn = xn_tm[:, i * D:(i + 1) * D]
        zk = "misc_free" if i == 0 else "misc2_free"
        zb = (g["misc"] if i == 0 else g["misc2"]).ap()
        tln = emit_ln_tile(ctx, g, xt, xn, [tk, ph.t(f"xn_free{i}"), ph.t(zk)],
                           so=4 * i, scratch=zb[:, 0:D])
        ph.set(zk, tln)
        ph.set(f"ln{i}", tln)
        for c in range(8):
            co = c * STRIPE + i * 128
            bank = "p0" if (c + i) % 2 == 0 else "p1"
            pb = g[bank].ap()
            tm = op("tensor", lambda h, s_=xn[:, c * 128:(c + 1) * 128], p_=pb:
                    h.transpose(p_[:, 0:128], s_, g["c_ident"].ap()),
                    deps=[tln, ph.bank_free.get(bank)])
            te = op("scalar", lambda h, p_=pb, co_=co: h.copy(xfh[:, co_:co_ + 128], p_[:, 0:128]),
                    deps=[tm, ph.t("xnf_free")])
            te2 = op("vector", lambda h, p_=pb, co_=co: h.tensor_tensor(
                xfl[:, co_:co_ + 128], p_[:, 0:128], xfh[:, co_:co_ + 128], ALU.subtract),
                deps=[te])
            ph.bank_free[bank] = te2
            ph.set(f"xnf_{c}_{i}", te2)
        gi = s * NT + i
        tq, tr = emit_act_quant(ctx, ph, g, xn, xt, g["c_rsx"].ap()[:, gi:gi + 1],
                                [tln], f"xq{i}", zb=zb, zfk=zk)
        ph.set(f"xq_{i}", tq)
        ph.set(f"xn_free{i}",
               Tok2([ph.t(f"xnf_{c}_{i}") for c in range(8)] + [tq]))
        tp_toks = []
        for c in range(8):
            dst = g["hl"].ap()[:, 4096 + i * 1024 + c * 128:4096 + i * 1024 + (c + 1) * 128]
            bank = "p2" if c % 2 == 0 else "p3"
            tp_toks.append(_tp128(ctx, ph, g, xt[:, c * 128:(c + 1) * 128], dst, bank,
                                  [tq, ph.t(f"hlq{i}")]))
        sp = dma("scalar",
                 bass.AP(dram["xq_fm"].tensor, t0 + i * 128, [[T, 128], [128 * T, 8], [1, 128]]),
                 bass.AP(g["hl"], 4096 + i * 1024, [[8192, 128], [128, 8], [1, 128]]),
                 deps=tp_toks)
        ph.set(f"hlq{i}", sp)
        ph.set(f"x_free{i}", Tok2(tp_toks))
        ph.set("bf_free", Tok2([sp, ph.t("bf_free")]))
    xnf_toks = _tok_max([ph.t(f"xnf_{c}_{i}") for c in range(8) for i in range(NT)])

    qf, vf = g["qh_tm"].ap(), g["qh_tm2"].ap()
    for grp in range(2):
        mmtok = {}
        for dc in range(8):
            sl = (grp * 8 + dc) % 6
            wbuf, woff = L1SLOTS[sl]
            wtok = dma("sync", g[wbuf].ap()[:, woff:woff + 2048],
                       dram["wqv_hl0" if grp == 0 else "wqv_hl1"][dc * 128:(dc + 1) * 128, :],
                       deps=[ph.t(f"l1w_{sl}")])
            xh = xfh[:, dc * STRIPE:(dc + 1) * STRIPE]
            xl = xfl[:, dc * STRIPE:(dc + 1) * STRIPE]
            last = None
            for fl in range(8):
                bank = g[f"p{fl}"].ap()
                wh = g[wbuf].ap()[:, woff + fl * 128:woff + (fl + 1) * 128]
                wl = g[wbuf].ap()[:, woff + 1024 + fl * 128:woff + 1024 + (fl + 1) * 128]
                deps = [wtok, xnf_toks]
                if dc == 0:
                    deps.append(ph.bank_free.get(f"p{fl}"))
                last = op("tensor", lambda h, b_=bank, w_=wh, x_=xh, s_=(dc == 0):
                          h.matmul(b_[:, 0:STRIPE], w_, x_, start=s_, stop=False), deps=deps)
                last = op("tensor", lambda h, b_=bank, w_=wh, x_=xl:
                          h.matmul(b_[:, 0:STRIPE], w_, x_, start=False, stop=False))
                last = op("tensor", lambda h, b_=bank, w_=wl, x_=xh, e_=(dc == 7):
                          h.matmul(b_[:, 0:STRIPE], w_, x_, start=False, stop=e_))
                mmtok[fl] = last
            ph.set(f"l1w_{sl}", last)
        for fl in range(8):
            fcg = grp * 8 + fl
            half, fc = fcg // 8, fcg % 8
            dstb = qf if half == 0 else vf
            bank = g[f"p{fl}"].ap()
            ev = op("scalar", lambda h, d_=dstb[:, fc * STRIPE:(fc + 1) * STRIPE], b_=bank, f_=fcg:
                    h.activation(d_, b_[:, 0:STRIPE], AF.Identity,
                                 bias=g["c_bcol"].ap()[:, f_:f_ + 1]),
                    deps=[mmtok[fl], tbq, ph.t(f"qv_free{half}_{fc}")])
            ph.bank_free[f"p{fl}"] = ev
            ph.set(f"qv_{half}_{fc}", ev)

    for c in range(8):
        Q = qf[:, c * STRIPE:(c + 1) * STRIPE]
        Vv = vf[:, c * STRIPE:(c + 1) * STRIPE]
        scr_q = g["kt_tm"].ap()[:, c * STRIPE:(c + 1) * STRIPE]
        scr_v = g["kt_tm2"].ap()[:, c * STRIPE:(c + 1) * STRIPE]
        toks = _haar_block(ctx, ph, g, Q, Vv, g["c_hw"].ap()[:, c * 3:c * 3 + 3],
                           g["c_hw"].ap()[:, 24 + c * 3:24 + c * 3 + 3],
                           scr_q, scr_v,
                           [ph.t(f"qv_0_{c}"), ph.t(f"qv_1_{c}"), ph.t(f"hscr_{c}")])
        ph.set(f"haar_{c}", _tok_max(toks))
        ph.set(f"hscr_{c}", _tok_max(toks))
    if upto == "l1_haar" and s == 0:
        emit_debug_dump(ctx, g, dram, qf[:, 0:8 * STRIPE])
        return

    yh, yl = g["yh"].ap(), g["yl"].ap()
    for c in range(8):
        hsp1 = op("scalar", lambda h, c_=c: h.copy(yh[:, c_ * STRIPE:(c_ + 1) * STRIPE],
                                                   qf[:, c_ * STRIPE:(c_ + 1) * STRIPE]),
                  deps=[ph.t(f"haar_{c}"), ph.t("yhl_free")])
        hsp2 = op("vector", lambda h, c_=c: h.tensor_tensor(
            yl[:, c_ * STRIPE:(c_ + 1) * STRIPE], qf[:, c_ * STRIPE:(c_ + 1) * STRIPE],
            yh[:, c_ * STRIPE:(c_ + 1) * STRIPE], ALU.subtract), deps=[hsp1])
        ph.set(f"hsp_{c}", hsp2)
    banks = ["p2", "p3", "p4", "p5"]
    mm_tok = {}
    for c in range(8):
        sl = 6 + c % 3
        wbuf, woff = L1SLOTS[sl]
        wtok = dma("scalar", g[wbuf].ap()[:, woff:woff + 2 * D],
                   dram["wo1_hl"][c * 128:(c + 1) * 128, :],
                   deps=[ph.t(f"l1w_{sl}")])
        wtok2 = wtok
        last = None
        for i in range(NT):
            lh = yh[:, c * STRIPE + i * 128: c * STRIPE + (i + 1) * 128]
            ll = yl[:, c * STRIPE + i * 128: c * STRIPE + (i + 1) * 128]
            for hi in range(2):
                bk = banks[i * 2 + hi]
                deps = [wtok, wtok2, ph.t(f"hsp_{c}")]
                if c == 0:
                    deps.append(ph.bank_free.get(bk))
                wh_ = g[wbuf].ap()[:, woff + hi * 512:woff + (hi + 1) * 512]
                wl_ = g[wbuf].ap()[:, woff + D + hi * 512:woff + D + (hi + 1) * 512]
                last = op("tensor", lambda h, b_=g[bk].ap(), l_=lh, w_=wh_, s_=(c == 0):
                          h.matmul(b_[:, 0:512], l_, w_, start=s_, stop=False), deps=deps)
                last = op("tensor", lambda h, b_=g[bk].ap(), l_=ll, w_=wh_:
                          h.matmul(b_[:, 0:512], l_, w_, start=False, stop=False))
                last = op("tensor", lambda h, b_=g[bk].ap(), l_=lh, w_=wl_, e_=(c == 7):
                          h.matmul(b_[:, 0:512], l_, w_, start=False, stop=e_))
                mm_tok[(i, hi)] = last
        ph.set(f"l1w_{sl}", last)
    for i in range(NT):
        otoks = []
        for hi in range(2):
            dst = g["y_tm"].ap()[:, i * D + hi * 512: i * D + hi * 512 + 512]
            t1 = op("vector", lambda h, d_=dst, b_=g[banks[i * 2 + hi]].ap(), o_=hi * 512:
                    h.tensor_tensor(d_, b_[:, 0:512], g["c_brep"].ap()[:, o_:o_ + 512], ALU.add),
                    deps=[mm_tok[(i, hi)], tbo, ph.t("ytm_scratch_free")])
            ph.bank_free[banks[i * 2 + hi]] = t1
            otoks.append(t1)
        sp = dma("scalar", dram["attn1"][t0 + i * 128: t0 + (i + 1) * 128, :],
                 g["y_tm"].ap()[:, i * D:(i + 1) * D], deps=otoks)
        ph.set("ytm_scratch_free", sp)
    ph.set("yhl_free", _tok_max([mm_tok[k] for k in mm_tok]))
    ph.set("xnf_free", Tok2([ph.t(f"qv_{half}_{fc}") for half in range(2) for fc in range(8)]))
    for half in range(2):
        for fc in range(8):
            ph.set(f"qv_free{half}_{fc}",
                   Tok2([ph.t(f"hsp_{fc}"), ph.t(f"haar_{fc}"), ph.t("yhl_free")]))


def emit_l1_ffn(ctx, g, dram, upto=None, w_ready=None):
    op, dma = ctx.op, ctx.dma
    ph = Ph()
    if w_ready is not None:
        for sl_ in range(8):
            ph.set(f"l1w_{sl_}", w_ready)
    gate_bufs = ["q_tm", "k_tm", "qh_tm", "qh_tm2"]
    up_bufs = ["kt_tm", "kt_tm2", "y_tm", "v_aug"]
    for s in range(T // STRIPE):
        emit_l1_ffn_stripe(ctx, ph, g, dram, s, gate_bufs, up_bufs, upto)
        if upto == "l1_gq" and s == 0:
            return


def emit_l1_ffn_stripe(ctx, ph, g, dram, s, gate_bufs, up_bufs, upto=None):
    op, dma = ctx.op, ctx.dma
    t0 = s * STRIPE
    xqf = g["xqf"].ap()
    xtoks = []
    for c in range(8):
        xtoks.append(dma("gpsimd", xqf[:, c * STRIPE:(c + 1) * STRIPE],
                         dram["xq_fm"][c * 128:(c + 1) * 128, t0:t0 + STRIPE],
                         deps=[ph.t("xqf_free")]))
    xnf_tok = _tok_max(xtoks)
    st = g["c_stats"].ap()
    trs_tok = {}
    for i in range(NT):
        gi = s * NT + i
        rs = st[:, 4 + i:5 + i]
        trs_tok[i] = op("vector", lambda h, r_=rs, g_=gi: h.tensor_scalar(
            r_, g["c_rsx"].ap()[:, g_:g_ + 1], g["c_gam"].ap()[:, 0:1],
            None, ALU.mult), deps=[ph.t("gam"), ph.t(f"rs_free{i}")])
    for wname, (bufs, gcol) in (("g", (gate_bufs, 0)), ("u", (up_bufs, 1))):
        for fh in range(2):
            mmtok = {}
            for dc in range(8):
                sl = ((0 if wname == "g" else 16) + fh * 8 + dc) % 8
                wbuf, woff = L1SLOTS[sl]
                wtok = dma("sync" if wname == "g" else "scalar",
                           g[wbuf].ap()[:, woff:woff + 2048],
                           dram["w%sq" % ("g" if wname == "g" else "u")][dc * 128:(dc + 1) * 128,
                                                                         fh * 2048:(fh + 1) * 2048],
                           deps=[ph.t(f"l1w_{sl}")])
                last = None
                for i in range(NT):
                    lhsT = xqf[:, dc * STRIPE + i * 128: dc * STRIPE + (i + 1) * 128]
                    for ft in range(4):
                        bk = f"p{i * 4 + ft}"
                        deps = [wtok, xnf_tok]
                        if dc == 0:
                            deps.append(ph.bank_free.get(bk))
                        last = op("tensor", lambda h, b_=g[bk].ap(), l_=lhsT,
                                  w_=g[wbuf].ap()[:, woff + ft * 512:woff + (ft + 1) * 512],
                                  s_=(dc == 0), e_=(dc == 7):
                                  h.matmul(b_[:, 0:512], l_, w_, start=s_, stop=e_), deps=deps)
                        mmtok[(i, ft)] = last
                ph.set(f"l1w_{sl}", last)
            for i in range(NT):
                gi = s * NT + i
                rs = st[:, 4 + i:5 + i]
                trs = trs_tok[i]
                for ft in range(4):
                    fcol = fh * 2048 + ft * 512
                    bufi = fcol // 1024
                    off = i * 1024 + (fcol % 1024)
                    dst = g[bufs[bufi]].ap()[:, off:off + 512]
                    if wname == "g":
                        ev = op("scalar", lambda h, d_=dst, b_=g[f"p{i*4+ft}"].ap(), r_=rs:
                                h.activation(d_, b_[:, 0:512], AF.Silu, scale=r_),
                                deps=[mmtok[(i, ft)], trs, ph.t(f"gfree_{bufi}_{off}")])
                        ph.set(f"rs_free{i}", Tok2([ev, ph.t(f"rs_free{i}")]))
                        ev = op("vector", lambda h, d_=dst: h.tensor_scalar(
                            d_, d_, g["c_gam"].ap()[:, 1:2], None, ALU.mult), deps=[ev, ph.t("gam")])
                    else:
                        gsrc = g[gate_bufs[bufi]].ap()[:, off:off + 512]
                        ev = op("vector", lambda h, d_=dst, b_=g[f"p{i*4+ft}"].ap(),
                                r_=g["c_rsx"].ap()[:, gi:gi + 1], gs_=gsrc:
                                h.scalar_tensor_tensor(d_, b_[:, 0:512], r_, gs_,
                                                       ALU.mult, ALU.mult),
                                deps=[mmtok[(i, ft)], ph.t(f"gev_{i}_{fcol}"),
                                      ph.t(f"ufree_{bufi}_{off}")])
                    ph.bank_free[f"p{i * 4 + ft}"] = ev
                    ph.set(f"{wname}ev_{i}_{fcol}", ev)
    for i in range(NT):
        gi = s * NT + i
        deps = [ph.t(f"uev_{i}_{fc}") for fc in range(0, 4096, 512)]
        parts = []
        for bufi in range(4):
            seg = g[up_bufs[bufi]].ap()[:, i * 1024:(i + 1) * 1024]
            pcol = st[:, 8 + bufi:9 + bufi]
            parts.append((seg, pcol))
        tmax = None
        for bi, (seg, pcol) in enumerate(parts):
            tt = op("vector", lambda h, s_=seg, p_=pcol: h.reduce_max(
                p_, s_, AX.X, apply_absolute_value=True), deps=deps + [ph.t("stats2_free")])
            if tmax is None:
                tmax = tt
            else:
                tmax = op("vector", lambda h, p_=parts[0][1], q_=pcol: h.tensor_tensor(
                    p_, p_, q_, ALU.max), deps=[tmax, tt])
        amax = parts[0][1]
        t2 = op("vector", lambda h: h.tensor_scalar(amax, amax, 1e-5, None, ALU.max), deps=[tmax])
        sg = st[:, 12:13]
        t3 = op("vector", lambda h: h.reciprocal(sg, amax), deps=[t2])
        t4 = op("vector", lambda h: h.tensor_scalar(sg, sg, 127.0, None, ALU.mult), deps=[t3])
        rsg = st[:, 13:14]
        t5 = op("vector", lambda h: h.reciprocal(rsg, sg), deps=[t4])
        rsgd = st[:, 14 + i:15 + i]
        t6 = op("vector", lambda h, r_=rsgd: h.tensor_scalar(r_, rsg, g["c_gam"].ap()[:, 2:3],
                                                             None, ALU.mult),
                deps=[t5, ph.t("gam")])
        ph.set(f"rsgd_{i}", t6)
        for bufi in range(4):
            seg = g[up_bufs[bufi]].ap()[:, i * 1024:(i + 1) * 1024]
            z = g["misc"].ap()[:, 0:1024]
            tz = op("vector", lambda h, s_=seg, z_=z: h.tensor_scalar(z_, s_, sg, None, ALU.mult),
                    deps=[t4, ph.t("misc_free")] + deps)
            bfa, bfb = g["bfA"].ap(), g["bfB"].ap()
            ta = op("scalar", lambda h, z_=z: h.activation(bfa, z_, AF.Identity,
                                                           bias=g["c_eps"].ap()[:, 3:4]),
                    deps=[tz, ph.t("bf_free")])
            tb = op("scalar", lambda h, z_=z: h.activation(bfb, z_, AF.Identity,
                                                           bias=g["c_eps"].ap()[:, 4:5]),
                    deps=[tz, ph.t("bf_free")])
            t8 = op("vector", lambda h: h.tensor_scalar(bfa, bfa, 128.0, None, ALU.subtract),
                    deps=[ta])
            t9 = op("vector", lambda h: h.tensor_scalar(bfb, bfb, 128.0, None, ALU.add),
                    deps=[tb])
            msk = g["u8"].ap()
            tm = op("vector", lambda h, z_=z: h.tensor_scalar(msk, z_, 0.0, None, ALU.is_ge),
                    deps=[tz, ph.t("u8_free")])
            tsel = op("vector", lambda h, d_=seg: h.select(d_, msk, bfa, bfb, add_drain=True),
                      deps=[t8, t9, tm])
            ph.set("misc_free", tsel)
            ph.set("bf_free", tsel)
            ph.set("u8_free", tsel)
            ph.set("stats2_free", t6)
            ph.set(f"gq_{i}_{bufi}", tsel)
    if upto == "l1_gq":
        emit_debug_dump(ctx, g, dram, g[up_bufs[0]].ap()[:, 0:2048])
        return
    gq_all = _tok_max([ph.t(f"gq_{i}_{b}") for i in range(NT) for b in range(4)])
    sp_toks = []
    for i in range(NT):
        for fg in range(4):
            blk = []
            for fl in range(8):
                fc = fg * 8 + fl
                src_ap = g[up_bufs[fc // 8]].ap()[:, i * 1024 + (fc % 8) * 128:
                                                 i * 1024 + (fc % 8) * 128 + 128]
                dst = g["bfB"].ap()[:, fl * 128:(fl + 1) * 128]
                bank = ("p0", "p1")[fl % 2]
                blk.append(_tp128(ctx, ph, g, src_ap, dst, bank, [gq_all, ph.t("bf_free")]))
            sp = dma("scalar",
                     bass.AP(dram["gq_fm"].tensor, fg * 1024 * T + t0 + i * 128,
                             [[T, 128], [128 * T, 8], [1, 128]]),
                     bass.AP(g["bfB"], 0, [[1024, 128], [128, 8], [1, 128]]),
                     deps=blk)
            sp_toks.append(sp)
            ph.set(f"gqsp_{fg}_{i}", sp)
            ph.set("bf_free", Tok2([sp, ph.t("bf_free")]))
    gqfm_tok = Tok2(sp_toks)
    banks = ["p2", "p3", "p4", "p5"]
    mmtok = {}
    for fc in range(32):
        sl = fc % 8
        wbuf, woff = L1SLOTS[sl]
        wtok = dma("sync", g[wbuf].ap()[:, woff:woff + D], dram["wdq"][fc * 128:(fc + 1) * 128, :],
                   deps=[ph.t(f"l1w_{sl}")])
        slot = fc % 8
        gtok = dma("gpsimd", xqf[:, slot * STRIPE:(slot + 1) * STRIPE],
                   dram["gq_fm"][fc * 128:(fc + 1) * 128, t0:t0 + STRIPE],
                   deps=[ph.t(f"gqsp_{fc // 8}_0"), ph.t(f"gqsp_{fc // 8}_1"),
                         ph.t(f"slot_free{slot}")])
        last = None
        for i in range(NT):
            lhsT = xqf[:, slot * STRIPE + i * 128: slot * STRIPE + (i + 1) * 128]
            for hi in range(2):
                bk = banks[i * 2 + hi]
                deps = [wtok, gtok]
                if fc == 0:
                    deps.append(ph.bank_free.get(bk))
                last = op("tensor", lambda h, b_=g[bk].ap(), l_=lhsT,
                          w_=g[wbuf].ap()[:, woff + hi * 512:woff + (hi + 1) * 512],
                          s_=(fc == 0), e_=(fc == 31):
                          h.matmul(b_[:, 0:512], l_, w_, start=s_, stop=e_), deps=deps)
                mmtok[(i, hi)] = last
        ph.set(f"l1w_{sl}", last)
        ph.set(f"slot_free{slot}", last)
    for i in range(NT):
        a_tok = dma("gpsimd", g["x_tm"].ap()[:, i * D:(i + 1) * D],
                    dram["attn1"][t0 + i * 128: t0 + (i + 1) * 128, :],
                    deps=[ph.t(f"xa_free{i}")])
        x_tok = dma("gpsimd", g["xn_tm"].ap()[:, i * D:(i + 1) * D],
                    dram["x1"][t0 + i * 128: t0 + (i + 1) * 128, :],
                    deps=[ph.t(f"xb_free{i}")])
        otoks = []
        for hi in range(2):
            dst = g["misc"].ap()[:, hi * 512:(hi + 1) * 512]
            t1 = op("vector", lambda h, d_=dst, b_=g[banks[i * 2 + hi]].ap(), i_=i:
                    h.tensor_scalar(d_, b_[:, 0:512], g["c_stats"].ap()[:, 14 + i_:15 + i_],
                                    None, ALU.mult),
                    deps=[mmtok[(i, hi)], ph.t(f"rsgd_{i}"), ph.t("misc_free")])
            t2 = op("vector", lambda h, d_=dst, x_=g["x_tm"].ap()[:, i * D + hi * 512: i * D + hi * 512 + 512]:
                    h.tensor_tensor(d_, d_, x_, ALU.add), deps=[t1, a_tok])
            t3 = op("vector", lambda h, d_=dst, x_=g["xn_tm"].ap()[:, i * D + hi * 512: i * D + hi * 512 + 512]:
                    h.tensor_tensor(d_, d_, x_, ALU.add), deps=[t2, x_tok])
            ph.bank_free[banks[i * 2 + hi]] = t3
            otoks.append(t3)
        sp = dma("scalar", dram["out"][t0 + i * 128: t0 + (i + 1) * 128, :],
                 g["misc"].ap()[:, 0:D], deps=otoks)
        ph.set("misc_free", sp)
        ph.set(f"xa_free{i}", otoks[-1])
        ph.set(f"xb_free{i}", otoks[-1])
    dn = Tok2([mmtok[k] for k in mmtok])
    ph.set("xqf_free", dn)
    for i in range(NT):
        for bufi in range(4):
            for off in range(0, 1024, 512):
                ph.set(f"gfree_{bufi}_{i * 1024 + off}", Tok2([ph.t(f"uev_{i}_{bufi * 1024 + off}")]))
                ph.set(f"ufree_{bufi}_{i * 1024 + off}", gqfm_tok)


def emit_debug_dump(ctx, g, dram, src_ap, cols=None, dst_off=0):
    ctx.barrier()
    n = src_ap.shape[1] if cols is None else cols
    t = ctx.dma("sync", dram["dbg"][0:src_ap.shape[0], dst_off:dst_off + n], src_ap)
    ctx.dep("sync", [t])


# version 21
# speedup vs baseline: 1.0195x; 1.0195x over previous
import numpy as np
import concourse.bass as bass
import concourse.mybir as mybir
from concourse.bass_utils import run_bass_kernel_spmd

F32 = mybir.dt.float32
BF16 = mybir.dt.bfloat16
U8 = mybir.dt.uint8
ALU = mybir.AluOpType
AF = mybir.ActivationFunctionType
AX = mybir.AxisListType

B = 4
T_FULL = 4096
D = 1024
NCORES = 8
T = 2048
NH = 32
HD = 32
M = 2
CH = 128
NCH = T // CH
BMIN, BMAX = 0.85, 0.9995
EPS_DEN = 1e-6
LN_EPS = 1e-5
ROPE_BASE = 10000.0
H1 = 16
DH1 = 64
LVL = 3
DFF = 4096
INV_SQRT2 = 0.7071067811865476

STRIPE = 256
NT = STRIPE // 128

ENGINES = ("sync", "tensor", "vector", "scalar", "gpsimd")
NDMA = 16


class Tok:
    __slots__ = ("sem", "val")
    def __init__(self, sem, val):
        self.sem = sem
        self.val = val


class Ctx:

    def __init__(self, nc, sems):
        self.nc = nc
        self.sems = sems

    def begin(self, eng_name, handle):
        self.eng = eng_name
        self.h = handle
        self.dma_idx = {e: 0 for e in ENGINES}
        self.count = {k: 0 for k in self.sems}
        self.water = {e: {k: 0 for k in self.sems} for e in ENGINES}
        self.barrier_k = 0

    def _wait(self, sem, val):
        if val <= 0:
            return
        if self.water[self.eng][sem] >= val:
            return
        self.h.wait_ge(self.sems[sem], val)
        self.water[self.eng][sem] = val

    def dep(self, eng, toks):
        if self.eng != eng:
            return
        stack = list(toks)
        while stack:
            t = stack.pop()
            if t is None:
                continue
            if hasattr(t, "toks"):
                stack.extend(t.toks)
                continue
            self._wait(t.sem, t.val)

    def op(self, eng, fn, deps=()):
        self.dep(eng, deps)
        sem = "e_" + eng
        if self.eng == eng:
            inst = fn(self.h)
            inst.then_inc(self.sems[sem], 1)
        self.count[sem] += 1
        return Tok(sem, self.count[sem])

    def dma(self, eng, out, in_, deps=(), transpose=False):
        self.dep(eng, deps)
        idx = self.dma_idx[eng] % NDMA
        self.dma_idx[eng] += 1
        sem = f"d_{eng}{idx}"
        if self.eng == eng:
            self._wait(sem, self.count[sem])
            inst = self.h.dma_start(out=out, in_=in_, transpose=transpose)
            inst.then_inc(self.sems[sem], 16)
        self.count[sem] += 16
        return Tok(sem, self.count[sem])

    def barrier(self):
        self.barrier_k += 1
        target = 5 * self.barrier_k
        if self.eng in ("sync", "gpsimd", "scalar"):
            for i in range(NDMA):
                self._wait(f"d_{self.eng}{i}", self.count[f"d_{self.eng}{i}"])
        self.h.drain()
        self.h.sem_inc(self.sems["bar"], 1)
        self.h.wait_ge(self.sems["bar"], target)
        for e in ENGINES:
            for k in self.sems:
                self.water[e][k] = self.count[k]
        self.count["bar"] = target


class Buf:
    __slots__ = ("ap", "tok")
    def __init__(self, ap, tok=None):
        self.ap = ap
        self.tok = tok


def _rope_tables(positions):
    half = HD // 2
    inv_freq = 1.0 / ROPE_BASE ** (np.arange(half, dtype=np.float64) / half)
    freqs = positions[:, None].astype(np.float64) * inv_freq[None, :]
    cos = np.cos(freqs).astype(np.float32)
    sin = np.sin(freqs).astype(np.float32)
    nt = len(positions) // 128
    cos_tm = np.ascontiguousarray(cos.reshape(nt, 128, 16).transpose(1, 0, 2).reshape(128, nt * 16))
    sin_tm = np.ascontiguousarray(sin.reshape(nt, 128, 16).transpose(1, 0, 2).reshape(128, nt * 16))
    return cos_tm, sin_tm


def _consts():
    c = {}
    c["ident"] = np.eye(128, dtype=np.float32)
    jj, ii = np.meshgrid(np.arange(CH), np.arange(CH), indexing="ij")
    c["mask_causal"] = (jj <= ii).astype(np.float32)
    dd = np.arange(128)
    u_of_dd = dd // HD
    m_state = np.zeros((128, 132), dtype=np.float32)
    for u in range(4):
        rows = u_of_dd == u
        m_state[rows, u * 33:u * 33 + 33] = 1.0
    c["mask_state"] = m_state
    GA = np.zeros((2 * NH, 128), dtype=np.float32)
    MB = np.zeros((2 * NH, 16), dtype=np.float32)
    for r in range(2 * NH):
        m_r, h_r = r // NH, r % NH
        GA[r, :] = ((np.arange(128) // HD) == (h_r % 4)).astype(np.float32)
        col = (h_r // 4) * 2 + m_r
        MB[r, col] = 1.0
    c["GA"] = GA
    c["MB"] = MB
    import ml_dtypes
    c["identh"] = np.eye(128).astype(ml_dtypes.bfloat16)
    return c


def _per_core_inputs(inputs, core):
    b, half = core // 2, core % 2
    d = {}
    x = np.asarray(inputs["L0_attn_x"])
    d["x0"] = np.ascontiguousarray(x[b, half * T:(half + 1) * T, :])
    if half == 1:
        d["xpre"] = np.ascontiguousarray(x[b, 0:T, :])
    else:
        d["xpre"] = np.zeros((T, D), dtype=np.float32)
    d["pscale"] = np.full((128, 1), float(half), dtype=np.float32)
    pos_own = np.arange(half * T, (half + 1) * T, dtype=np.float64)
    pos_pre = np.arange(0, T, dtype=np.float64)
    d["cos_own"], d["sin_own"] = _rope_tables(pos_own)
    d["cos_pre"], d["sin_pre"] = _rope_tables(pos_pre)

    d["wqT"] = np.ascontiguousarray(np.asarray(inputs["L0_attn_Wq"]).T)
    d["wkT"] = np.ascontiguousarray(np.asarray(inputs["L0_attn_Wk"]).T)
    d["wvT"] = np.ascontiguousarray(np.asarray(inputs["L0_attn_Wv"]).T)
    d["woutT"] = np.ascontiguousarray(np.asarray(inputs["L0_attn_Wout"]).T)
    d["bout_rep"] = np.ascontiguousarray(
        np.broadcast_to(np.asarray(inputs["L0_attn_bout"])[None, :], (128, D)))
    Wbeta = np.asarray(inputs["L0_attn_Wbeta"])
    bbeta = np.asarray(inputs["L0_attn_bbeta"])
    perm_b = np.array([m * NH + h for m in range(M) for h in range(NH)])
    src = np.array([h * M + m for m in range(M) for h in range(NH)])
    d["wbetaT"] = np.ascontiguousarray(Wbeta[src, :].T)
    d["bbeta_col"] = np.ascontiguousarray(bbeta[src][:, None])
    Wmix = np.asarray(inputs["L0_attn_Wmix"])
    bmix = np.asarray(inputs["L0_attn_bmix"])
    d["wmixT"] = np.ascontiguousarray(Wmix[src, :].T)
    d["bmix_rep"] = np.ascontiguousarray(
        np.broadcast_to(bmix[src][None, :], (128, 2 * NH)))
    d["ln_g_rep"] = np.ascontiguousarray(
        np.broadcast_to(np.asarray(inputs["L0_attn_ln_g"])[None, :], (128, D)))
    d["ln_b_rep"] = np.ascontiguousarray(
        np.broadcast_to(np.asarray(inputs["L0_attn_ln_b"])[None, :], (128, D)))

    d["w1T"] = np.ascontiguousarray(np.asarray(inputs["L0_ffn_W1"]).T)
    d["b1_col"] = np.ascontiguousarray(np.asarray(inputs["L0_ffn_b1"]).reshape(DFF // 128, 128).T)
    d["w2T"] = np.ascontiguousarray(np.asarray(inputs["L0_ffn_W2"]).T)
    d["b2_rep"] = np.ascontiguousarray(
        np.broadcast_to(np.asarray(inputs["L0_ffn_b2"])[None, :], (128, D)))

    Wqkv = np.asarray(inputs["L1_attn_W_qkv"])
    bqkv = np.asarray(inputs["L1_attn_b_qkv"])
    sel = np.concatenate([np.arange(0, D), np.arange(2 * D, 3 * D)])
    d["wqvT"] = np.ascontiguousarray(Wqkv[sel, :].T)
    d["bqv_col"] = np.ascontiguousarray(bqkv[sel].reshape(16, 128).T)
    d["wout1T"] = np.ascontiguousarray(np.asarray(inputs["L1_attn_W_out"]).T)
    d["bout1_rep"] = np.ascontiguousarray(
        np.broadcast_to(np.asarray(inputs["L1_attn_b_out"])[None, :], (128, D)))
    wq1 = np.asarray(inputs["L1_attn_weight_q"])
    wv1 = np.asarray(inputs["L1_attn_weight_v"])
    d["hwq_col"] = np.ascontiguousarray(wq1.transpose(0, 2, 1).reshape(D, LVL))
    d["hwv_col"] = np.ascontiguousarray(wv1.transpose(0, 2, 1).reshape(D, LVL))

    d["wgT"] = np.ascontiguousarray(np.asarray(inputs["L1_ffn_w_gate"]).T)
    d["wuT"] = np.ascontiguousarray(np.asarray(inputs["L1_ffn_w_up"]).T)
    d["wdT"] = np.ascontiguousarray(np.asarray(inputs["L1_ffn_w_down"]).T)

    d.update(_consts())
    return {k: np.ascontiguousarray(v) for k, v in d.items()}


INPUT_SPECS = {
    "x0": ([T, D], F32), "xpre": ([T, D], F32),
    "cos_own": ([128, 16 * (T // 128)], F32), "sin_own": ([128, 16 * (T // 128)], F32),
    "cos_pre": ([128, 16 * (T // 128)], F32), "sin_pre": ([128, 16 * (T // 128)], F32),
    "wqT": ([D, D], F32), "wkT": ([D, D], F32), "wvT": ([D, D], F32),
    "woutT": ([D, D], F32), "bout_rep": ([128, D], F32),
    "wbetaT": ([D, 2 * NH], F32), "bbeta_col": ([2 * NH, 1], F32),
    "wmixT": ([D, 2 * NH], F32), "bmix_rep": ([128, 2 * NH], F32),
    "ln_g_rep": ([128, D], F32), "ln_b_rep": ([128, D], F32),
    "w1T": ([D, DFF], F32), "b1_col": ([128, DFF // 128], F32),
    "w2T": ([DFF, D], F32), "b2_rep": ([128, D], F32),
    "wqvT": ([D, 2 * D], F32), "bqv_col": ([128, 16], F32),
    "wout1T": ([D, D], F32), "bout1_rep": ([128, D], F32),
    "hwq_col": ([D, LVL], F32), "hwv_col": ([D, LVL], F32),
    "wgT": ([D, DFF], F32), "wuT": ([D, DFF], F32), "wdT": ([DFF, D], F32),
    "ident": ([128, 128], F32),
    "mask_causal": ([CH, CH], F32),
    "mask_state": ([128, 132], F32),
    "GA": ([2 * NH, 128], F32),
    "pscale": ([128, 1], F32),
    "identh": ([128, 128], BF16),
    "MB": ([2 * NH, 16], F32),
}


def kernel(**inputs):
    nc, run_order = build_program()
    in_maps = [_per_core_inputs(inputs, c) for c in range(NCORES)]
    res = run_bass_kernel_spmd(nc, in_maps, core_ids=list(range(NCORES)))
    out = np.empty((B, T_FULL, D), dtype=np.float32)
    for c in range(NCORES):
        b, half = c // 2, c % 2
        out[b, half * T:(half + 1) * T, :] = res.results[c]["out"]
    return out


def build_program(upto=None):
    nc = bass.Bass("TRN2", target_bir_lowering=False)
    g = {}

    dram = {}
    for name, (shape, dt) in INPUT_SPECS.items():
        dram[name] = nc.dram_tensor(name, shape, dt, kind="ExternalInput").ap()
    dram["out"] = nc.dram_tensor("out", [T, D], F32, kind="ExternalOutput").ap()
    scr = {
        "attn0": ([T, D], F32),
        "x1": ([T, D], F32),
        "h0": ([DFF, T], F32),
        "xn1_fm": ([D, T], F32),
        "xq_fm": ([D, T], BF16),
        "sx_col": ([T, 1], F32),
        "attn1": ([T, D], F32),
        "gate": ([T, DFF], F32),
        "gq_fm": ([DFF, T], BF16),
        "sg_col": ([T, 1], F32),
        "gammas": ([8, 1], F32),
        "w1q": ([D, DFF], BF16),
        "w2q": ([DFF, D], BF16),
        "wq_hl": ([D, 2 * D], BF16),
        "wk_hl": ([D, 2 * D], BF16),
        "wv_hl": ([D, 2 * D], BF16),
        "wo_hl": ([D, 2 * D], BF16),
        "wqv_hl0": ([D, 2 * D], BF16), "wqv_hl1": ([D, 2 * D], BF16),
        "wo1_hl": ([D, 2 * D], BF16),
        "xnh_fm": ([D, T], BF16), "xnl_fm": ([D, T], BF16),
        "wb_hl": ([D, 4 * NH], BF16),
        "wm_hl": ([D, 4 * NH], BF16),
        "wgq": ([D, DFF], BF16),
        "wuq": ([D, DFF], BF16),
        "wdq": ([DFF, D], BF16),
        "dbg": ([128, 4352], F32),
    }
    for name, (shape, dt) in scr.items():
        kind = "ExternalOutput" if (upto is not None and name == "dbg") else "Internal"
        dram[name] = nc.dram_tensor(name, shape, dt, kind=kind).ap()

    sem_names = (["e_" + e for e in ENGINES]
                 + [f"d_{e}{i}" for e in ("sync", "scalar", "gpsimd") for i in range(NDMA)]
                 + ["bar"])

    with _alloc(nc, g), _sems(nc, sem_names) as sems, nc.Block() as block:
        ctx = Ctx(nc, sems)

        def program(eng_name, handle):
            ctx.begin(eng_name, handle)
            emit_all(ctx, g, dram, upto)

        @block.sync
        def _(e):
            program("sync", e)

        @block.tensor
        def _(e):
            program("tensor", e)

        @block.vector
        def _(e):
            program("vector", e)

        @block.scalar
        def _(e):
            program("scalar", e)

        @block.gpsimd
        def _(e):
            program("gpsimd", e)

    return nc, None


class _sems:
    def __init__(self, nc, names):
        self.nc = nc
        self.names = names
        self.cms = []

    def __enter__(self):
        out = {}
        for n in self.names:
            cm = self.nc.semaphore(n)
            out[n] = cm.__enter__()
            self.cms.append(cm)
        return out

    def __exit__(self, *a):
        for cm in reversed(self.cms):
            cm.__exit__(*a)
        return False


class _alloc:
    def __init__(self, nc, g):
        self.nc = nc
        self.g = g
        self.cms = []

    def _sb(self, name, shape, dt=F32):
        cm = self.nc.sbuf_tensor(name, shape, dt)
        t = cm.__enter__()
        self.cms.append(cm)
        self.g[name] = t
        return t

    def _ps(self, name):
        cm = self.nc.psum_tensor(name, [128, 512], F32)
        t = cm.__enter__()
        self.cms.append(cm)
        self.g[name] = t
        return t

    def __enter__(self):
        sb = self._sb
        sb("c_ident", [128, 128])
        sb("c_identh", [128, 128], BF16)
        sb("c_mask", [CH, CH])
        sb("c_mstate", [128, 132])
        sb("c_cos", [128, 16 * (T // 128)]); sb("c_sin", [128, 16 * (T // 128)])
        sb("c_lng", [128, D]); sb("c_lnb", [128, D])
        sb("c_brep", [128, D])
        sb("c_bsmall", [128, 2 * NH])
        sb("c_stats", [128, 16]); sb("c_zero", [2 * NH, CH]); sb("c_eps", [128, 8])
        sb("c_GA", [2 * NH, 128]); sb("c_MB", [2 * NH, 16]); sb("c_pscale", [128, 1])
        sb("c_ones", [128, 128])
        sb("c_hw", [128, 48]); sb("c_gam", [128, 4]); sb("c_rsx", [128, 16])
        sb("bfA", [128, 1024], BF16); sb("bfB", [128, 1024], BF16)
        sb("c_bcol", [128, DFF // 128])
        sb("c_bbeta", [2 * NH, 1])
        sb("wAh", [128, 2048], BF16); sb("wBh", [128, 2048], BF16)
        sb("xqf", [128, 8 * STRIPE], BF16)
        sb("x_tm", [128, NT * D])
        sb("xn_tm", [128, NT * D])
        sb("xnfh", [128, (D // 128) * STRIPE], BF16)
        sb("xnfl", [128, (D // 128) * STRIPE], BF16)
        sb("hh", [128, 32 * STRIPE], BF16); sb("hl", [128, 32 * STRIPE], BF16)
        sb("yh", [128, NT * D], BF16); sb("yl", [128, NT * D], BF16)
        sb("q_tm", [128, NT * D])
        sb("k_tm", [128, NT * D])
        sb("v_aug", [128, NT * 8 * 132])
        sb("mix_tm", [128, NT * 2 * NH])
        sb("beta_fm", [2 * NH, STRIPE])
        sb("a_fm", [2 * NH, STRIPE]); sb("ra_fm", [2 * NH, STRIPE])
        sb("a_tm", [128, NT * 2 * NH]); sb("ra_tm", [128, NT * 2 * NH])
        sb("qh_tm", [128, NT * D]); sb("qh_tm2", [128, NT * D])
        sb("kt_tm", [128, NT * D]); sb("kt_tm2", [128, NT * D])
        sb("qh_fm", [128, D]); sb("kt_fm", [128, D])
        sb("at_sb", [128, 8 * CH])
        sb("y_tm", [128, NT * D])
        sb("s_all", [128, 2 * NH]); sb("dens", [128, 2 * NH])
        sb("aCd", [128, 16])
        sb("aC_small", [2 * NH, 16])
        sb("S0", [128, 16 * 132]); sb("S1", [128, 16 * 132])
        sb("kinit", [128, 2])
        sb("misc", [128, 1024])
        sb("misc2", [128, 1024])
        sb("u8", [128, 1024], U8)
        total = 0
        for name, t in self.g.items():
            if hasattr(t, "shape"):
                sh = list(t.shape)
                dtsz = 1 if name == "u8" else 4
                if sh[0] <= 128:
                    total += sh[1] * dtsz if len(sh) == 2 else 0
        for i in range(8):
            self._ps(f"p{i}")
        return self.g

    def __exit__(self, *a):
        for cm in reversed(self.cms):
            cm.__exit__(*a)
        return False


def emit_all(ctx, g, dram, upto=None):
    emit_load_consts(ctx, g, dram)
    if upto == "consts":
        ctx.barrier()
        emit_debug_dump(ctx, g, dram, g["c_cos"].ap()[:, 0:512])
        return
    emit_weight_prep(ctx, g, dram)
    ctx.barrier()
    w01_ready = emit_l0_attn(ctx, g, dram, prefix=True, upto=upto)
    if upto in ("pre_ln", "pre_scan", "pre_full"):
        return
    ctx.barrier()
    emit_l0_attn(ctx, g, dram, prefix=False, upto=upto)
    if upto in ("l0_ln", "l0_scan", "l0_attn", "l0_y", "l0_at", "l0_at2"):
        return
    ctx.barrier()
    wl1_ready = emit_l0_ffn(ctx, g, dram, upto=upto, w_ready=w01_ready)
    if upto in ("l0_ffn", "x1"):
        return
    ctx.barrier()
    emit_l1_attn(ctx, g, dram, upto=upto)
    if upto in ("l1_attn", "xq", "l1_haar", "l1_attn_chk"):
        return
    ctx.barrier()
    emit_l1_ffn(ctx, g, dram, upto=upto, w_ready=wl1_ready)


def emit_ln_tile(ctx, g, x_ap, xn_ap, toks, affine=None, scratch=None, so=0):
    op = ctx.op
    st = g["c_stats"].ap()
    mu = st[:, so:so + 1]
    var = st[:, so + 1:so + 2]
    rstd = st[:, so + 2:so + 3]
    sq = (scratch if scratch is not None else g["misc"].ap()[:, 0:D])
    t1 = op("vector", lambda h: h.reduce_sum(mu, x_ap, AX.X), deps=toks)
    t2 = op("vector", lambda h: h.tensor_scalar(mu, mu, 1.0 / D, None, ALU.mult), deps=[t1])
    t3 = op("vector", lambda h: h.tensor_scalar(xn_ap, x_ap, mu, None, ALU.subtract), deps=[t2])
    t4 = op("scalar", lambda h: h.activation(sq, xn_ap, AF.Square, accum_out=var), deps=[t3])
    t5 = op("vector", lambda h: h.tensor_scalar(var, var, 1.0 / D, None, ALU.mult), deps=[t4])
    t6 = op("scalar", lambda h: h.activation(rstd, var, AF.Sqrt,
                                             bias=g["c_eps"].ap()[:, 0:1]), deps=[t5])
    t7 = op("vector", lambda h: h.reciprocal(rstd, rstd), deps=[t6])
    t8 = op("vector", lambda h: h.tensor_scalar(xn_ap, xn_ap, rstd, None, ALU.mult), deps=[t7])
    if affine is not None:
        gg, bb = affine
        t8 = op("vector", lambda h: h.tensor_tensor(xn_ap, xn_ap, gg, ALU.mult), deps=[t8])
        t8 = op("vector", lambda h: h.tensor_tensor(xn_ap, xn_ap, bb, ALU.add), deps=[t8])
    return t8


def emit_transpose_128(ctx, g, src_ap, dst_ap, toks, bank, evac_eng="scalar"):
    pb = g[bank].ap()
    tm = ctx.op("tensor", lambda h: h.transpose(pb[:, 0:128], src_ap, g["c_ident"].ap()),
                deps=toks)
    te = ctx.op(evac_eng, lambda h: h.copy(dst_ap, pb[:, 0:128]) if evac_eng == "scalar"
                else h.tensor_copy(dst_ap, pb[:, 0:128]), deps=[tm])
    return te, te


class Ph:
    def __init__(self):
        self.tok = {}
        self.bank_free = {}

    def t(self, name):
        return self.tok.get(name)

    def set(self, name, tok):
        self.tok[name] = tok


def emit_l0_attn(ctx, g, dram, prefix, upto=None):
    op, dma = ctx.op, ctx.dma
    ph = Ph()
    qspills = []
    x_src = dram["xpre"] if prefix else dram["x0"]
    cos_src = dram["cos_pre"] if prefix else dram["cos_own"]
    sin_src = dram["sin_pre"] if prefix else dram["sin_own"]
    nstripes = T // STRIPE

    tcos = dma("sync", g["c_cos"].ap(), cos_src)
    tsin = dma("sync", g["c_sin"].ap(), sin_src)
    ph.set("tbl", Tok2([tcos, tsin]))

    vaug = g["v_aug"].ap()
    t_ones = op("vector", lambda h: h.memset(
        bass.AP(g["v_aug"], 32, [[NT * 8 * 132, 128], [132, NT * 8], [33, 4], [1, 1]]), 1.0))

    if prefix:
        tz = op("vector", lambda h: h.memset(g["S0"].ap(), 0.0))
        ph.cur_S, ph.new_S = "S0", "S1"
        for hg in range(8):
            for m in range(M):
                ph.set(f"S_{hg}_{m}", tz)
    else:
        ph.cur_S, ph.new_S = ctx.l0_S_cur, ("S1" if ctx.l0_S_cur == "S0" else "S0")
        for hg in range(8):
            for m in range(M):
                ph.set(f"S_{hg}_{m}", ctx.l0_S_tok)

    for s in range(nstripes):
        emit_l0_stripe(ctx, ph, g, dram, s, x_src, prefix, t_ones, upto)
        if prefix and s == 0:
            tq_st = emit_tq_pass1(ctx, ph, g, dram, "w1T", "w1q", 0,
                                  stg=("q_tm", "y_tm"), bankname="p2")
        if prefix and s == 1:
            qspills.append(emit_tq_finish(ctx, ph, g, dram, tq_st))
        if prefix and s == 3:
            tq_st = emit_tq_pass1(ctx, ph, g, dram, "w2T", "w2q", 1,
                                  stg=("q_tm", "y_tm"), bankname="p2")
        if prefix and s == 4:
            qspills.append(emit_tq_finish(ctx, ph, g, dram, tq_st))
        if upto in ("pre_ln", "l0_ln") and s == 0:
            return
        if upto in ("pre_scan", "l0_scan") and s == 0:
            return
        if upto in ("l0_y", "l0_at", "l0_at2") and not prefix and s == 0:
            return
    if upto == "pre_full" and prefix:
        emit_debug_dump(ctx, g, dram, g[ph.cur_S].ap()[:, 0:16 * 132])
        return
    ctx.l0_S_cur = ph.cur_S
    ctx.l0_S_tok = _tok_max([ph.t(f"S_{hg}_{m}") for hg in range(8) for m in range(M)])
    return Tok2([t for q in qspills for t in q.toks])


def emit_l0_stripe(ctx, ph, g, dram, s, x_src, prefix, t_ones, upto=None):
    op, dma = ctx.op, ctx.dma
    t0 = s * STRIPE

    x_tm, xn_tm = g["x_tm"].ap(), g["xn_tm"].ap()
    xfh, xfl = g["xnfh"].ap(), g["xnfl"].ap()
    for i in range(NT):
        xt = x_tm[:, i * D:(i + 1) * D]
        tk = dma("sync" if prefix else "gpsimd", xt,
                 x_src[t0 + i * 128: t0 + (i + 1) * 128, :],
                 deps=[ph.t(f"x_free{i}")] if ph.t(f"x_free{i}") else ())
        xn = xn_tm[:, i * D:(i + 1) * D]
        tln = emit_ln_tile(ctx, g, xt, xn, [tk, ph.t(f"xn_free{i}"), ph.t("misc_free")], so=4 * i)
        ph.set("misc_free", tln)
        ph.set(f"ln{i}", tln)
        ph.set(f"x_free{i}", tln)
        for c in range(8):
            src = xn[:, c * 128:(c + 1) * 128]
            co = c * STRIPE + i * 128
            bank = "p0" if (c + i) % 2 == 0 else "p1"
            dep = [tln, ph.bank_free.get(bank), ph.t("xnf_free")]
            pb = g[bank].ap()
            tm = op("tensor", lambda h, s_=src, p_=pb: h.transpose(p_[:, 0:128], s_, g["c_ident"].ap()),
                    deps=dep)
            te = op("scalar", lambda h, p_=pb, co_=co: h.copy(xfh[:, co_:co_ + 128], p_[:, 0:128]),
                    deps=[tm])
            te2 = op("vector", lambda h, p_=pb, co_=co: h.tensor_tensor(
                xfl[:, co_:co_ + 128], p_[:, 0:128], xfh[:, co_:co_ + 128], ALU.subtract),
                deps=[te])
            ph.bank_free[bank] = te2
            ph.set(f"xnf_{c}_{i}", te2)
    sp_xnf = []
    if not prefix:
        for c in range(8):
            dd = [ph.t(f"xnf_{c}_{i}") for i in range(NT)]
            sp_xnf.append(dma("scalar", dram["xnh_fm"][c * 128:(c + 1) * 128, t0:t0 + STRIPE],
                              xfh[:, c * STRIPE:(c + 1) * STRIPE], deps=dd))
            sp_xnf.append(dma("scalar", dram["xnl_fm"][c * 128:(c + 1) * 128, t0:t0 + STRIPE],
                              xfl[:, c * STRIPE:(c + 1) * STRIPE], deps=dd))
    if upto in ("pre_ln", "l0_ln"):
        emit_debug_dump(ctx, g, dram, g["xn_tm"].ap()[:, 0:NT * D])
        return

    xnf_toks = [ph.t(f"xnf_{c}_{i}") for c in range(8) for i in range(NT)]

    emit_l0_beta(ctx, ph, g, dram, s, xnf_toks)

    if not prefix:
        emit_proj_tm(ctx, ph, g, dram["wm_hl"], g["mix_tm"].ap(), 2 * NH,
                     xnf_toks, key="mix", bias_rep=g["c_bsmall"].ap(), bias_dram=dram["bmix_rep"])
        emit_mix_finish(ctx, ph, g)
        emit_proj_tm(ctx, ph, g, dram["wq_hl"], g["q_tm"].ap(), D, xnf_toks, key="q")
    emit_proj_tm(ctx, ph, g, dram["wk_hl"], g["k_tm"].ap(), D, xnf_toks, key="k")
    emit_proj_tm_vaug(ctx, ph, g, dram["wv_hl"], xnf_toks, t_ones)

    if not prefix:
        emit_rope_phi(ctx, ph, g, s, "q")
        emit_qk_scale(ctx, ph, g, "q")
    emit_rope_phi(ctx, ph, g, s, "k")
    emit_qk_scale(ctx, ph, g, "k", prefix=prefix)

    for ci in range(STRIPE // CH):
        emit_scan_chunk(ctx, ph, g, s, ci, prefix, dram=dram,
                        upto=(upto if s == 0 and ci == 0 else None))
        if upto == "l0_at2" and ci == 0 and not prefix:
            return
        if upto in ("pre_scan", "l0_scan") and ci == 0:
            emit_debug_dump(ctx, g, dram, g[ph.cur_S].ap()[:, 0:16 * 132])
            return
        if upto == "l0_at" and ci == 0:
            return

    keys = ["a_done", "k_done", "v_done"] + ([] if prefix else ["mix_done", "q_done"])
    ph.set("xnf_free", Tok2([ph.t(k) for k in keys] + sp_xnf))
    for i in range(NT):
        if not prefix:
            ph.set(f"q_free{i}", Tok2([ph.t(f"qh{m_}_{i}") for m_ in range(M)]))
        ph.set(f"k_free{i}", Tok2([ph.t(f"kt{m_}_{i}") for m_ in range(M)]))
    lastc = STRIPE // CH - 1
    vfree = [ph.t(f"S_{hg}_{m_}") for hg in range(8) for m_ in range(M)]
    if not prefix:
        vfree += [ph.t(f"y_{hg}_{lastc}") for hg in range(8)]
    ph.set("vaug_free", Tok2(vfree))
    ph.set("beta_free", Tok2(vfree + [ph.t("aCd_t1")]))
    if not prefix:
        for i in range(NT):
            ph.set(f"mix_free{i}", Tok2([ph.t(f"s_{hg}_{m_}") for hg in range(8) for m_ in range(M)]))

    if prefix:
        return

    if upto == "l0_y" and s == 0:
        emit_debug_dump(ctx, g, dram, g["y_tm"].ap()[:, 0:NT * D])
        return

    emit_l0_attn_out(ctx, ph, g, dram, s)


def emit_load_consts(ctx, g, dram):
    d = ctx.dma
    d("sync", g["c_ident"].ap(), dram["ident"])
    d("sync", g["c_identh"].ap(), dram["identh"])
    d("sync", g["c_mask"].ap(), dram["mask_causal"])
    d("sync", g["c_mstate"].ap(), dram["mask_state"])
    d("sync", g["c_lng"].ap(), dram["ln_g_rep"])
    d("sync", g["c_lnb"].ap(), dram["ln_b_rep"])
    d("sync", g["c_GA"].ap(), dram["GA"])
    d("sync", g["c_pscale"].ap(), dram["pscale"])
    d("sync", g["c_MB"].ap(), dram["MB"])
    d("sync", g["c_bbeta"].ap(), dram["bbeta_col"])
    ctx.op("vector", lambda h: h.memset(g["c_zero"].ap(), 0.0))
    ctx.op("vector", lambda h: h.memset(g["c_eps"].ap()[:, 0:1], LN_EPS))
    ctx.op("vector", lambda h: h.memset(g["c_eps"].ap()[:, 1:2], EPS_DEN))
    ctx.op("vector", lambda h: h.memset(g["c_eps"].ap()[:, 2:3], -1.0))
    ctx.op("vector", lambda h: h.memset(g["c_ones"].ap(), 1.0))
    ctx.op("vector", lambda h: h.memset(g["c_eps"].ap()[:, 3:4], 128.0))
    ctx.op("vector", lambda h: h.memset(g["c_eps"].ap()[:, 4:5], -128.0))
    ctx.op("vector", lambda h: h.memset(g["c_eps"].ap()[:, 5:6], 1.0 / 127.0))


def emit_proj_tm(ctx, ph, g, wsrc, out_ap, ncols, xnf_toks, key,
                 bias_rep=None, bias_dram=None, banks=("p0", "p1", "p2", "p3"),
                 evac_fn=None):
    op, dma = ctx.op, ctx.dma
    halves = [(h * 512, min(512, ncols - h * 512)) for h in range((ncols + 511) // 512)]
    xh, xl = g["xnfh"].ap(), g["xnfl"].ap()
    if bias_dram is not None and ph.t(key + "_bias") is None:
        ph.set(key + "_bias", dma("sync", bias_rep, bias_dram))
    mm_tok = {}
    slots = (("wAh", 0, "wfree_Ah"), ("wBh", 0, "wfree_Bh"),
             ("hh", 0, "wfree_hh0"), ("hh", 4096, "wfree_hh1"))
    for c in range(8):
        bn, soff, wfk = slots[c % 4]
        wbuf = g[bn]
        wtok = dma("sync", wbuf.ap()[:, soff:soff + 2 * ncols], wsrc[c * 128:(c + 1) * 128, :],
                   deps=[ph.t(wfk)])
        wtok2 = wtok
        whi = wbuf.ap()[:, soff:soff + ncols]
        wlo = wbuf.ap()[:, soff + ncols:soff + 2 * ncols]
        last = None
        for i in range(NT):
            lh = xh[:, c * STRIPE + i * 128: c * STRIPE + (i + 1) * 128]
            ll = xl[:, c * STRIPE + i * 128: c * STRIPE + (i + 1) * 128]
            for hi_, (off, n) in enumerate(halves):
                bk = banks[i * len(halves) + hi_]
                bank = g[bk].ap()
                deps = [wtok, wtok2]
                if c == 0:
                    deps = deps + xnf_toks + [ph.bank_free.get(bk)]
                last = op("tensor", lambda h, b_=bank, l_=lh, w_=whi[:, off:off + n], n_=n,
                          s_=(c == 0): h.matmul(b_[:, 0:n_], l_, w_, start=s_, stop=False),
                          deps=deps)
                last = op("tensor", lambda h, b_=bank, l_=ll, w_=whi[:, off:off + n], n_=n:
                          h.matmul(b_[:, 0:n_], l_, w_, start=False, stop=False))
                last = op("tensor", lambda h, b_=bank, l_=lh, w_=wlo[:, off:off + n], n_=n,
                          e_=(c == 7): h.matmul(b_[:, 0:n_], l_, w_, start=False, stop=e_))
                mm_tok[(i, hi_)] = last
        ph.set(wfk, last)
    ev_toks = []
    for i in range(NT):
        for hi_, (off, n) in enumerate(halves):
            bk = banks[i * len(halves) + hi_]
            bank = g[bk].ap()
            if evac_fn is not None:
                ev = evac_fn(i, hi_, off, n, bank, [mm_tok[(i, hi_)]])
            elif bias_rep is not None:
                ev = op("vector", lambda h, b_=bank, n_=n, o_=off, i_=i:
                        h.tensor_tensor(out_ap[:, i_ * ncols + o_: i_ * ncols + o_ + n_],
                                        b_[:, 0:n_], bias_rep[:, o_:o_ + n_], ALU.add),
                        deps=[mm_tok[(i, hi_)], ph.t(key + "_bias")])
            else:
                ev = op("scalar", lambda h, b_=bank, n_=n, o_=off, i_=i:
                        h.copy(out_ap[:, i_ * ncols + o_: i_ * ncols + o_ + n_], b_[:, 0:n_]),
                        deps=[mm_tok[(i, hi_)]])
            ph.bank_free[bk] = ev
            ev_toks.append(ev)
    ph.set(key + "_done", ev_toks[-1])
    ph.set(key + "_evs", ev_toks)
    return ev_toks


def emit_proj_tm_vaug(ctx, ph, g, wT_dram, xnf_toks, t_ones):
    op = ctx.op
    VROW = 8 * 132

    def evac(i, hi, off, n, bank, deps):
        dst = bass.AP(g["v_aug"], i * VROW + (off // 128) * 132,
                      [[NT * VROW, 128], [132, n // 128], [33, 4], [1, 32]])
        return op("scalar", lambda h: h.copy(dst, bank[:, 0:n]),
                  deps=deps + [t_ones, ph.t("vaug_free")])

    return emit_proj_tm(ctx, ph, g, wT_dram, None, D, xnf_toks, key="v",
                        evac_fn=evac)


def emit_l0_beta(ctx, ph, g, dram, s, xnf_toks):
    op, dma = ctx.op, ctx.dma
    beta, a_fm, ra_fm = g["beta_fm"].ap(), g["a_fm"].ap(), g["ra_fm"].ap()
    win = bass.AP(dram["wb_hl"].tensor, 0, [[128, 128], [128 * 128, 8], [1, 128]])
    wout = bass.AP(g["wAh"], 0, [[2048, 128], [128, 8], [1, 128]])
    wtok = dma("sync", wout, win, deps=[ph.t("wfree_Ah")])
    wtok2 = wtok
    xfh, xfl = g["xnfh"].ap(), g["xnfl"].ap()
    bank = g["p0"].ap()
    last = None
    for c in range(8):
        wh = g["wAh"].ap()[:, c * 128:c * 128 + 64]
        wl = g["wAh"].ap()[:, c * 128 + 64:c * 128 + 128]
        rh = xfh[:, c * STRIPE:(c + 1) * STRIPE]
        rl = xfl[:, c * STRIPE:(c + 1) * STRIPE]
        last = op("tensor", lambda h, w_=wh, r_=rh, s_=(c == 0): h.matmul(
            bank[0:64, 0:STRIPE], w_, r_, start=s_, stop=False),
            deps=[wtok, wtok2] + (xnf_toks + [ph.bank_free.get("p0")] if c == 0 else []))
        last = op("tensor", lambda h, w_=wh, r_=rl: h.matmul(
            bank[0:64, 0:STRIPE], w_, r_, start=False, stop=False))
        last = op("tensor", lambda h, w_=wl, r_=rh, e_=(c == 7): h.matmul(
            bank[0:64, 0:STRIPE], w_, r_, start=False, stop=e_))
    ph.set("wfree_Ah", last)
    t1 = op("scalar", lambda h: h.activation(beta, bank[0:64, 0:STRIPE], AF.Sigmoid,
                                             bias=g["c_bbeta"].ap()), deps=[last])
    ph.bank_free["p0"] = t1
    t2 = op("vector", lambda h: h.tensor_scalar(beta, beta, BMIN, BMAX, ALU.max, ALU.min),
            deps=[t1])
    sc_toks = []
    for ci in range(STRIPE // CH):
        sc = op("vector", lambda h, c_=ci: h.tensor_tensor_scan(
            a_fm[:, c_ * CH:(c_ + 1) * CH], beta[:, c_ * CH:(c_ + 1) * CH],
            g["c_zero"].ap()[0:2 * NH, 0:CH], 1.0, ALU.mult, ALU.add), deps=[t2])
        sc_toks.append(sc)
    tr = op("vector", lambda h: h.reciprocal(ra_fm, a_fm), deps=sc_toks)
    ph.set("a_done", tr)
    for i in range(NT):
        for nm, src_, dst in (("a", a_fm, g["a_tm"].ap()), ("ra", ra_fm, g["ra_tm"].ap())):
            bank2 = g["p1"].ap()
            tm = op("tensor", lambda h, s_=src_, i_=i: h.transpose(
                bank2[:, 0:64], s_[:, i_ * 128:(i_ + 1) * 128], g["c_ident"].ap()[0:64, 0:64]),
                deps=[tr, ph.bank_free.get("p1")])
            te = op("scalar", lambda h, d_=dst, i_=i: h.copy(d_[:, i_ * 64:(i_ + 1) * 64],
                                                             bank2[:, 0:64]), deps=[tm])
            ph.bank_free["p1"] = te
            ph.set(f"atm_{i}", te)
    return


def emit_mix_finish(ctx, ph, g):
    op = ctx.op
    mix = g["mix_tm"].ap()
    for i in range(NT):
        u0 = mix[:, i * 64: i * 64 + 32]
        u1 = mix[:, i * 64 + 32: i * 64 + 64]
        d01 = g["misc"].ap()[:, i * 32:(i + 1) * 32]
        t1 = op("vector", lambda h, u0_=u0, u1_=u1, d_=d01:
                h.tensor_tensor(d_, u0_, u1_, ALU.subtract),
                deps=[ph.t("mix_done"), ph.t("misc_free")])
        t2 = op("scalar", lambda h, u0_=u0, d_=d01:
                h.activation(u0_, d_, AF.Sigmoid), deps=[t1])
        t3 = op("scalar", lambda h, u1_=u1, d_=d01:
                h.activation(u1_, d_, AF.Sigmoid, scale=g["c_eps"].ap()[:, 2:3]), deps=[t1])
        ph.set(f"mixf_{i}", t3)
        ph.set(f"mixf2_{i}", t2)
        ph.set("misc_free", Tok2([t2, t3, ph.t("misc_free")]))


def emit_rope_phi(ctx, ph, g, s, key):
    op = ctx.op
    buf = g[key + "_tm"]
    FL = NT * D
    for i in range(NT):
        gi = s * NT + i
        q = buf.ap()[:, i * D:(i + 1) * D]
        qc, qs = g["misc"].ap()[:, 0:D], g["misc2"].ap()[:, 0:D]
        cosap = bass.AP(g["c_cos"], gi * 16, [[16 * (T // 128), 128], [0, 32], [0, 2], [1, 16]])
        sinap = bass.AP(g["c_sin"], gi * 16, [[16 * (T // 128), 128], [0, 32], [0, 2], [1, 16]])
        dep = [ph.t(key + "_done"), ph.t("tbl"), ph.t("misc_free"), ph.t("misc2_free")]
        t1 = op("vector", lambda h, q_=q, d_=qc, c_=cosap: h.tensor_tensor(d_, q_, c_, ALU.mult), deps=dep)
        t2 = op("vector", lambda h, q_=q, d_=qs, s_=sinap: h.tensor_tensor(d_, q_, s_, ALU.mult), deps=dep)
        lo = lambda t, o: bass.AP(t, o, [[D, 128], [32, 32], [1, 16]])
        loq = lambda o: bass.AP(buf, i * D + o, [[FL, 128], [32, 32], [1, 16]])
        t3 = op("vector", lambda h: h.tensor_tensor(loq(0), lo(g["misc"], 0), lo(g["misc2"], 16), ALU.subtract),
                deps=[t1, t2])
        t4 = op("vector", lambda h: h.tensor_tensor(loq(16), lo(g["misc2"], 0), lo(g["misc"], 16), ALU.add),
                deps=[t1, t2])
        t5 = op("vector", lambda h, q_=q: h.tensor_scalar(qc, q_, 0.0, None, ALU.min), deps=[t3, t4])
        t6 = op("scalar", lambda h: h.activation(qc, qc, AF.Exp), deps=[t5])
        t7 = op("vector", lambda h, q_=q: h.scalar_tensor_tensor(q_, q_, 0.0, qc, ALU.max, ALU.add),
                deps=[t6])
        ph.set(f"{key}phi_{i}", t7)
        ph.set("misc_free", t7)
        ph.set("misc2_free", t4)


def emit_qk_scale(ctx, ph, g, key, prefix=False):
    op = ctx.op
    src = g[key + "_tm"].ap()
    sc = g["a_tm"] if key == "q" else g["ra_tm"]
    for m in range(M):
        dstbuf = g[("qh_tm", "qh_tm2")[m] if key == "q" else ("kt_tm", "kt_tm2")[m]]
        for i in range(NT):
            scap = bass.AP(sc, i * 64 + m * 32, [[NT * 64, 128], [1, 32], [0, 32]])
            deps = [ph.t(f"{key}phi_{i}"), ph.t(f"atm_{i}")]
            if prefix:
                t = op("vector", lambda h, d_=dstbuf.ap()[:, i * D:(i + 1) * D],
                       s_=src[:, i * D:(i + 1) * D], a_=scap:
                       h.scalar_tensor_tensor(d_, s_, g["c_pscale"].ap(), a_,
                                              ALU.mult, ALU.mult), deps=deps)
            else:
                t = op("vector", lambda h, d_=dstbuf.ap()[:, i * D:(i + 1) * D],
                       s_=src[:, i * D:(i + 1) * D], a_=scap:
                       h.tensor_tensor(d_, s_, a_, ALU.mult), deps=deps)
            ph.set(f"{'qh' if key == 'q' else 'kt'}{m}_{i}", t)


def emit_scan_chunk(ctx, ph, g, s, ci, prefix, dram=None, upto=None):
    op = ctx.op
    i = ci
    VROW = 8 * 132
    vaug = g["v_aug"]
    Scur, Snew = g[ph.cur_S].ap(), g[ph.new_S].ap()

    ce = ci * CH + CH - 1
    aC = g["aC_small"].ap()
    t1 = op("vector", lambda h: h.tensor_scalar(aC[:, 0:16], g["c_MB"].ap(),
                                                g["a_fm"].ap()[:, ce:ce + 1], None, ALU.mult),
            deps=[ph.t("a_done"), ph.t("aCs_free")])
    ph.set("aCd_t1", t1)
    bank = g["p0"].ap()
    t2 = op("tensor", lambda h: h.matmul(bank[:, 0:16], g["c_GA"].ap()[0:64, :], aC[:, 0:16],
                                         start=True, stop=True),
            deps=[t1, ph.bank_free.get("p0")])
    t3 = op("scalar", lambda h: h.copy(g["aCd"].ap()[:, 0:16], bank[:, 0:16]), deps=[t2])
    ph.bank_free["p0"] = t3
    ph.set("aCs_free", t3)
    aCd_tok = t3

    seq = [(hg, m) for hg in range(8) for m in range(M)]

    def front(idx):
        hg, m = seq[idx]
        sl = idx % 2
        qh = g[("qh_tm", "qh_tm2")[m]]
        kt = g[("kt_tm", "kt_tm2")[m]]
        qh_fm = g["qh_fm"].ap()[:, sl * 128:(sl + 1) * 128]
        kt_fm = g["kt_fm"].ap()[:, sl * 128:(sl + 1) * 128]
        tb0, tb1 = ("p0", "p1") if sl == 0 else ("p1", "p0")
        tq = _tp128(ctx, ph, g, qh.ap()[:, i * D + hg * 128: i * D + (hg + 1) * 128],
                    qh_fm, tb0, [ph.t(f"qh{m}_{i}"), ph.t(f"qhfm_free{sl}")])
        tk = _tp128(ctx, ph, g, kt.ap()[:, i * D + hg * 128: i * D + (hg + 1) * 128],
                    kt_fm, tb1, [ph.t(f"kt{m}_{i}"), ph.t(f"ktfm_free{sl}")])
        at = g["at_sb"].ap()[:, sl * 512:(sl + 1) * 512]
        ev_toks = []
        mm1_toks = []
        for u in range(4):
            ab = ("p2", "p3")[u % 2]
            bank_a = g[ab].ap()
            mm = op("tensor", lambda h, u_=u, b_=bank_a: h.matmul(
                b_[:, 0:128], kt_fm[32 * u_:32 * (u_ + 1), :],
                qh_fm[32 * u_:32 * (u_ + 1), :],
                start=True, stop=True, tile_position=(32 * u_, 0)),
                deps=[tq, tk, ph.bank_free.get(ab)])
            mm1_toks.append(mm)
            ev = op("vector", lambda h, u_=u, b_=bank_a: h.tensor_tensor(
                at[:, u_ * 128:(u_ + 1) * 128], b_[:, 0:128], g["c_mask"].ap(),
                ALU.mult), deps=[mm, ph.t(f"at_free{sl}")])
            ph.bank_free[ab] = ev
            ev_toks.append(ev)
        ph.set(f"ktfm_free{sl}", mm1_toks[-1])
        ph.set(f"front_ev{sl}", ev_toks)

    def back(idx):
        hg, m = seq[idx]
        sl = idx % 2
        sblk = (hg * 2 + m) * 132
        Scur_blk = Scur[:, sblk:sblk + 132]
        Snew_blk = Snew[:, sblk:sblk + 132]
        qh_fm = g["qh_fm"].ap()[:, sl * 128:(sl + 1) * 128]
        kt = g[("kt_tm", "kt_tm2")[m]]
        ps_y = g[("p4", "p5")[m]].ap()
        ps_s = g[("p6", "p7")[m]].ap()
        at = g["at_sb"].ap()[:, sl * 512:(sl + 1) * 512]
        ev_toks = ph.t(f"front_ev{sl}")
        if not prefix:
            last = None
            for u in range(4):
                vap = bass.AP(vaug, i * VROW + hg * 132 + 33 * u, [[NT * VROW, 128], [1, 33]])
                last = op("tensor", lambda h, u_=u, v_=vap: h.matmul(
                    ps_y[:, 33 * u_:33 * u_ + 33], at[:, u_ * 128:(u_ + 1) * 128], v_,
                    start=(u_ == 0), stop=False),
                    deps=[ev_toks[u], ph.t("v_done")] + ([ph.bank_free.get(("p4", "p5")[m])] if u == 0 else []))
            mm3 = op("tensor", lambda h: h.matmul(ps_y[:, 0:132], qh_fm, Scur_blk,
                                                  start=False, stop=True),
                     deps=[ph.t(f"S_{hg}_{m}")])
            ph.set(f"at_free{sl}", mm3)
            ph.set(f"qhfm_free{sl}", mm3)
            ph.set(f"Sread_{hg}_{m}_y", mm3)
            scol = g["s_all"].ap()[:, m * 32 + hg * 4: m * 32 + hg * 4 + 4]
            dcol = g["dens"].ap()[:, m * 32 + hg * 4: m * 32 + hg * 4 + 4]
            dsrc = bass.AP(g[("p4", "p5")[m]], 32, [[512, 128], [33, 4]])
            td = op("scalar", lambda h: h.activation(dcol, dsrc, AF.Identity,
                                                     bias=g["c_eps"].ap()[:, 1:2]), deps=[mm3])
            tr = op("vector", lambda h: h.reciprocal(scol, dcol), deps=[td])
            tmx = op("vector", lambda h: h.tensor_tensor(
                scol, scol, g["mix_tm"].ap()[:, i * 64 + m * 32 + hg * 4:
                                             i * 64 + m * 32 + hg * 4 + 4], ALU.mult),
                deps=[tr, ph.t(f"mixf_{i}"), ph.t(f"mixf2_{i}")])
            ph.set(f"s_{hg}_{m}", tmx)
        else:
            ph.set(f"at_free{sl}", None)
            ph.set(f"qhfm_free{sl}", None)

        ktblk = kt.ap()[:, i * D + hg * 128: i * D + (hg + 1) * 128]
        vblk = bass.AP(vaug, i * VROW + hg * 132, [[NT * VROW, 128], [1, 132]])
        d1 = op("tensor", lambda h: h.matmul(ps_s[:, 0:132], g["c_ident"].ap(), Scur_blk,
                                             start=True, stop=False),
                deps=[ph.t(f"S_{hg}_{m}"), ph.bank_free.get(("p6", "p7")[m])])
        d2 = op("tensor", lambda h: h.matmul(ps_s[:, 0:132], ktblk, vblk,
                                             start=False, stop=True),
                deps=[ph.t(f"kt{m}_{i}"), ph.t("v_done")])
        ph.set(f"Sread_{hg}_{m}_s", d2)
        sev = op("vector", lambda h: h.scalar_tensor_tensor(
            Snew_blk, ps_s[:, 0:132], g["aCd"].ap()[:, hg * 2 + m: hg * 2 + m + 1],
            g["c_mstate"].ap(), ALU.mult, ALU.mult),
            deps=[d2, aCd_tok, ph.t(f"Snew_free_{hg}_{m}")])
        ph.bank_free[("p6", "p7")[m]] = sev
        ph.set(f"S_{hg}_{m}_next", sev)

        if not prefix and m == 1:
            p4, p5 = g["p4"].ap(), g["p5"].ap()
            ylast = []
            for u in range(4):
                yblk = g["y_tm"].ap()[:, i * D + hg * 128 + 32 * u: i * D + hg * 128 + 32 * (u + 1)]
                s0 = g["s_all"].ap()[:, hg * 4 + u: hg * 4 + u + 1]
                s1 = g["s_all"].ap()[:, 32 + hg * 4 + u: 32 + hg * 4 + u + 1]
                e1 = op("vector", lambda h, y_=yblk, u_=u, s_=s0: h.tensor_scalar(
                    y_, p4[:, 33 * u_:33 * u_ + 32], s_, None, ALU.mult),
                    deps=[ph.t(f"s_{hg}_0"), ph.t(f"Sread_{hg}_0_y"), ph.t(f"ytmfree_{i}")])
                e2 = op("vector", lambda h, y_=yblk, u_=u, s_=s1: h.scalar_tensor_tensor(
                    y_, p5[:, 33 * u_:33 * u_ + 32], s_, y_, ALU.mult, ALU.add),
                    deps=[e1, ph.t(f"s_{hg}_1"), ph.t(f"Sread_{hg}_1_y")])
                ylast.append(e2)
            ph.bank_free["p4"] = ylast[-1]
            ph.bank_free["p5"] = ylast[-1]
            ph.set(f"y_{hg}_{i}", ylast[-1])

    if prefix:
        for idx in range(16):
            back(idx)
    else:
        front(0)
        for idx in range(16):
            if idx + 1 < 16:
                front(idx + 1)
            back(idx)

    for hg in range(8):
        for m in range(M):
            rd = [ph.t(f"Sread_{hg}_{m}_s"), ph.t(f"Sread_{hg}_{m}_y")]
            ph.set(f"Snew_free_{hg}_{m}", _tok_max(rd))
            ph.set(f"S_{hg}_{m}", ph.t(f"S_{hg}_{m}_next"))
    ph.cur_S, ph.new_S = ph.new_S, ph.cur_S

    if upto == "l0_at2" and not prefix:
        emit_debug_dump(ctx, g, dram, g["dens"].ap()[:, 0:64], dst_off=0)
        emit_debug_dump(ctx, g, dram, g["s_all"].ap()[:, 0:64], dst_off=64)
        emit_debug_dump(ctx, g, dram, g["y_tm"].ap()[:, 0:D], dst_off=128)


def _tok_max(toks):
    toks = [t for t in toks if t is not None]
    if not toks:
        return None
    best = toks[0]
    for t in toks[1:]:
        if t.sem == best.sem:
            if t.val > best.val:
                best = t
        else:
            return Tok2(toks)
    return best


class Tok2:
    def __init__(self, toks):
        self.toks = [t for t in toks if t is not None]


L1SLOTS = (("wAh", 0), ("wBh", 0), ("hh", 0), ("hh", 2048),
           ("hh", 4096), ("hh", 6144), ("hl", 0), ("hl", 2048),
           ("hl", 6144))

L0FSLOTS = (("wAh", 0), ("wAh", 1024), ("wBh", 0), ("wBh", 1024),
            ("yh", 0), ("yh", 1024), ("yl", 0), ("yl", 1024))


def _tp128(ctx, ph, g, src_ap, dst_ap, bank, deps):
    pb = g[bank].ap()
    ident = g["c_identh" if src_ap.tensor.dtype == BF16 else "c_ident"].ap()
    tm = ctx.op("tensor", lambda h: h.transpose(pb[:, 0:128], src_ap, ident),
                deps=deps + [ph.bank_free.get(bank)])
    te = ctx.op("scalar", lambda h: h.copy(dst_ap, pb[:, 0:128]), deps=[tm])
    ph.bank_free[bank] = te
    return te


def emit_l0_attn_out(ctx, ph, g, dram, s):
    op, dma = ctx.op, ctx.dma
    t0 = s * STRIPE
    yh, yl = g["yh"].ap(), g["yl"].ap()
    tp_toks = {}
    for i in range(NT):
        for c in range(8):
            tb = ("p0", "p1")[c % 2]
            co = i * D + c * 128
            pb = g[tb].ap()
            tm = op("tensor", lambda h, s_=g["y_tm"].ap()[:, co:co + 128], p_=pb:
                    h.transpose(p_[:, 0:128], s_, g["c_ident"].ap()),
                    deps=[ph.t(f"y_{c}_{i}"), ph.bank_free.get(tb), ph.t("yhl_free")])
            te = op("scalar", lambda h, p_=pb, co_=co: h.copy(yh[:, co_:co_ + 128], p_[:, 0:128]),
                    deps=[tm])
            te2 = op("vector", lambda h, p_=pb, co_=co: h.tensor_tensor(
                yl[:, co_:co_ + 128], p_[:, 0:128], yh[:, co_:co_ + 128], ALU.subtract),
                deps=[te])
            ph.bank_free[tb] = te2
            tp_toks[(i, c)] = te2
    for i_ in range(NT):
        ph.set(f"ytmfree_{i_}", Tok2([tp_toks[(i_, c_)] for c_ in range(8)]))
    if ph.t("bout") is None:
        ph.set("bout", dma("sync", g["c_brep"].ap(), dram["bout_rep"]))
    banks = ["p2", "p3", "p4", "p5"]
    mm_tok = {}
    for c in range(8):
        bn = "Ah" if c % 2 == 0 else "Bh"
        wbuf = g["w" + bn]
        wtok = dma("sync", wbuf.ap()[:, 0:2 * D], dram["wo_hl"][c * 128:(c + 1) * 128, :],
                   deps=[ph.t("wfree_" + bn)])
        wtok2 = wtok
        last = None
        for i in range(NT):
            lh = yh[:, i * D + c * 128: i * D + (c + 1) * 128]
            ll = yl[:, i * D + c * 128: i * D + (c + 1) * 128]
            for hi in range(2):
                bk = banks[i * 2 + hi]
                deps = [wtok, wtok2, tp_toks[(i, c)]]
                if c == 0:
                    deps.append(ph.bank_free.get(bk))
                wh_ = wbuf.ap()[:, hi * 512:(hi + 1) * 512]
                wl_ = wbuf.ap()[:, D + hi * 512:D + (hi + 1) * 512]
                last = op("tensor", lambda h, b_=g[bk].ap(), l_=lh, w_=wh_, s_=(c == 0):
                          h.matmul(b_[:, 0:512], l_, w_, start=s_, stop=False), deps=deps)
                last = op("tensor", lambda h, b_=g[bk].ap(), l_=ll, w_=wh_:
                          h.matmul(b_[:, 0:512], l_, w_, start=False, stop=False))
                last = op("tensor", lambda h, b_=g[bk].ap(), l_=lh, w_=wl_, e_=(c == 7):
                          h.matmul(b_[:, 0:512], l_, w_, start=False, stop=e_))
                mm_tok[(i, hi)] = last
        ph.set("wfree_" + bn, last)
    ph.set("yhl_free", _tok_max([mm_tok[k] for k in mm_tok]))
    for i in range(NT):
        z = g["x_tm"].ap()[:, i * D:(i + 1) * D]
        zt = []
        for hi in range(2):
            bank = g[banks[i * 2 + hi]].ap()
            t1 = op("vector", lambda h, z_=z[:, hi * 512:(hi + 1) * 512], b_=bank,
                    o_=hi * 512: h.tensor_tensor(z_, b_[:, 0:512],
                                                 g["c_brep"].ap()[:, o_:o_ + 512], ALU.add),
                    deps=[mm_tok[(i, hi)], ph.t("bout"), ph.t(f"ln{i}")])
            t2 = op("vector", lambda h, z_=z[:, hi * 512:(hi + 1) * 512],
                    x_=g["xn_tm"].ap()[:, i * D + hi * 512: i * D + hi * 512 + 512]:
                    h.tensor_tensor(z_, z_, x_, ALU.add), deps=[t1])
            ph.bank_free[banks[i * 2 + hi]] = t2
            zt.append(t2)
        to = emit_ln_tile(ctx, g, z, g["xn_tm"].ap()[:, i * D:(i + 1) * D],
                          zt, affine=(g["c_lng"].ap(), g["c_lnb"].ap()),
                          scratch=(g["misc"] if i == 0 else g["misc2"]).ap()[:, 0:D],
                          so=8 + 4 * i)
        ph.set("misc_free" if i == 0 else "misc2_free", to)
        tk = dma("scalar", dram["attn0"][t0 + i * 128: t0 + (i + 1) * 128, :],
                 g["xn_tm"].ap()[:, i * D:(i + 1) * D], deps=[to])
        ph.set(f"x_free{i}", tk)
        ph.set(f"xn_free{i}", tk)


def emit_hilo_weight(ctx, ph, g, dram, wname, outname, col0=0, ncols=None):
    op, dma = ctx.op, ctx.dma
    W = dram[wname]
    nrows = W.shape[0]
    if ncols is None:
        ncols = W.shape[1] - col0
    cw = min(1024, ncols)
    nh = max(ncols // cw, 1)
    for c in range(nrows // 128):
        for hh in range(nh):
            bn = "A" if (c * nh + hh) % 2 == 0 else "B"
            win = g["x_tm" if bn == "A" else "xn_tm"]
            ob = g["wAh" if bn == "A" else "wBh"]
            obn = "wAh" if bn == "A" else "wBh"
            cc = cw
            wtok = dma("sync", win.ap()[:, 0:cc],
                       W[c * 128:(c + 1) * 128, col0 + hh * cw:col0 + hh * cw + cc],
                       deps=[ph.t("hw_in_" + bn)])
            t1 = op("scalar", lambda h: h.copy(ob.ap()[:, 0:cc], win.ap()[:, 0:cc]),
                    deps=[wtok, ph.t("hwfree_" + obn)])
            t2 = op("vector", lambda h: h.tensor_tensor(ob.ap()[:, cc:2 * cc], win.ap()[:, 0:cc],
                                                        ob.ap()[:, 0:cc], ALU.subtract),
                    deps=[t1, ph.t("hwfree_" + obn)])
            ph.set("hw_in_" + bn, t2)
            d1 = dma("scalar", dram[outname][c * 128:(c + 1) * 128, hh * cw:hh * cw + cc],
                     ob.ap()[:, 0:cc], deps=[t1])
            d2 = dma("scalar", dram[outname][c * 128:(c + 1) * 128, ncols + hh * cw:ncols + hh * cw + cc],
                     ob.ap()[:, cc:2 * cc], deps=[t2])
            ph.set("hwfree_" + obn, Tok2([d1, d2]))
    ph.set("wfree_Ah", ph.t("hwfree_wAh"))
    ph.set("wfree_Bh", ph.t("hwfree_wBh"))
    ph.set(f"x_free0", Tok2([ph.t("hw_in_A"), ph.t("hw_in_B")]))


def emit_weight_prep(ctx, g, dram):
    ph = Ph()
    emit_hilo_weight(ctx, ph, g, dram, "wqT", "wq_hl")
    emit_hilo_weight(ctx, ph, g, dram, "wkT", "wk_hl")
    emit_hilo_weight(ctx, ph, g, dram, "wvT", "wv_hl")
    emit_hilo_weight(ctx, ph, g, dram, "woutT", "wo_hl")
    emit_hilo_weight(ctx, ph, g, dram, "wqvT", "wqv_hl0", col0=0, ncols=D)
    emit_hilo_weight(ctx, ph, g, dram, "wqvT", "wqv_hl1", col0=D, ncols=D)
    emit_hilo_weight(ctx, ph, g, dram, "wout1T", "wo1_hl")
    emit_hilo_weight(ctx, ph, g, dram, "wbetaT", "wb_hl")
    emit_hilo_weight(ctx, ph, g, dram, "wmixT", "wm_hl")


def emit_tq_pass1(ctx, ph, g, dram, wname, outname, qidx,
                  l1=False, gcol=None,
                  stg=("x_tm", "xn_tm"), outb=("bfA", "bfB"),
                  bankname="p0", kp="tq", cwmax=1024):
    op, dma = ctx.op, ctx.dma
    W = dram[wname]
    nrows, ncols = W.shape
    nch = nrows // 128
    cw = min(cwmax, ncols)
    nh = ncols // cw
    st = g["c_stats"].ap()
    tot = st[:, 10:11]
    acc = st[:, 9:10]
    tz = op("vector", lambda h: h.memset(tot, 0.0), deps=[ph.t(kp + "stats_free")])
    prev = tz
    for ci in range(nch * nh):
        c, hh = divmod(ci, nh)
        wa = g[stg[ci % 2]]
        wfk = kp + ("wfA" if ci % 2 == 0 else "wfB")
        wtok = dma("gpsimd", wa.ap()[:, 0:cw],
                   W[c * 128:(c + 1) * 128, hh * cw:(hh + 1) * cw],
                   deps=[ph.t(wfk)])
        ta = op("scalar", lambda h, w_=wa: h.activation(w_.ap()[:, 0:cw], w_.ap()[:, 0:cw],
                                                        AF.Abs, accum_out=acc),
                deps=[wtok, prev])
        ph.set(wfk, ta)
        prev = op("vector", lambda h: h.tensor_tensor(tot, tot, acc, ALU.add),
                  deps=[ta, prev])
    return dict(W=W, outname=outname, qidx=qidx, l1=l1, gcol=gcol, stg=stg,
                outb=outb, bankname=bankname, kp=kp, cw=cw, nh=nh, nch=nch,
                nrows=nrows, ncols=ncols, prev=prev)


def emit_tq_finish(ctx, ph, g, dram, stq):
    op, dma = ctx.op, ctx.dma
    W, outname, l1, gcol = stq["W"], stq["outname"], stq["l1"], stq["gcol"]
    stg, outb, bankname, kp = stq["stg"], stq["outb"], stq["bankname"], stq["kp"]
    cw, nh, nch = stq["cw"], stq["nh"], stq["nch"]
    st = g["c_stats"].ap()
    tot = st[:, 10:11]
    bank = g[bankname].ap()
    t1 = op("tensor", lambda h: h.matmul(bank[0:1, 0:1], g["c_ones"].ap()[:, 0:1], tot,
                                         start=True, stop=True),
            deps=[stq["prev"], ph.bank_free.get(bankname)])
    cell = st[0:1, 11:12]
    sc_cell = st[0:1, 12:13]
    b_cell = st[0:1, 13:14]
    tm1 = op("vector", lambda h: h.memset(sc_cell, 1.0 / (stq["nrows"] * stq["ncols"])),
             deps=[ph.t(kp + "stats_free")])
    if not l1:
        tm2 = op("vector", lambda h: h.memset(b_cell, 1e-5), deps=[ph.t(kp + "stats_free")])
        t2 = op("scalar", lambda h: h.activation(cell, bank[0:1, 0:1], AF.Identity,
                                                 scale=sc_cell, bias=b_cell), deps=[t1, tm1, tm2])
    else:
        t2 = op("scalar", lambda h: h.activation(cell, bank[0:1, 0:1], AF.Identity,
                                                 scale=sc_cell), deps=[t1, tm1])
        t2 = op("vector", lambda h: h.tensor_scalar(cell, cell, 1e-5, None, ALU.max), deps=[t2])
    thr_cell = st[0:1, 13:14]
    t2b = op("vector", lambda h: h.tensor_scalar(thr_cell, cell, 0.5, None, ALU.mult), deps=[t2])
    ph.bank_free[bankname] = t2b
    thr = st[:, 14:15]
    nthr = st[:, 15:16]
    t3 = op("tensor", lambda h: h.matmul(bank[:, 3:4], g["c_ones"].ap()[0:1, :], thr_cell,
                                         start=True, stop=True), deps=[t2b, ph.bank_free.get(bankname)])
    t4 = op("scalar", lambda h: h.copy(thr, bank[:, 3:4]), deps=[t3])
    ph.bank_free[bankname] = t4
    if l1:
        t3g = op("tensor", lambda h: h.matmul(bank[:, 4:5], g["c_ones"].ap()[0:1, :], cell,
                                              start=True, stop=True), deps=[t2, t4])
        t4g = op("scalar", lambda h, gc_=gcol: h.copy(g["c_gam"].ap()[:, gc_:gc_ + 1], bank[:, 4:5]),
                 deps=[t3g])
        ph.bank_free[bankname] = t4g
        ph.set("gam", Tok2([t4g, ph.t("gam")]))
    t5 = op("vector", lambda h: h.tensor_scalar(nthr, thr, -1.0, None, ALU.mult), deps=[t4])
    dma("sync", dram["gammas"][stq["qidx"]:stq["qidx"] + 1, :], cell, deps=[t2])
    stoks = []
    for ci in range(nch * nh):
        c, hh = divmod(ci, nh)
        wa = g[stg[ci % 2]]
        wfk = kp + ("wfA" if ci % 2 == 0 else "wfB")
        ob = g[outb[ci % 2]]
        obk = kp + ("obA" if ci % 2 == 0 else "obB")
        wtok = dma("gpsimd", wa.ap()[:, 0:cw],
                   W[c * 128:(c + 1) * 128, hh * cw:(hh + 1) * cw],
                   deps=[ph.t(wfk)])
        q1 = op("vector", lambda h, w_=wa, o_=ob: h.tensor_scalar(
            o_.ap()[:, 0:cw], w_.ap()[:, 0:cw], nthr, None, ALU.is_le),
            deps=[wtok, t5, ph.t(obk)])
        q2 = op("vector", lambda h, w_=wa, o_=ob: h.scalar_tensor_tensor(
            o_.ap()[:, 0:cw], w_.ap()[:, 0:cw], thr,
            o_.ap()[:, 0:cw], ALU.is_ge, ALU.subtract), deps=[q1])
        ph.set(wfk, q2)
        stok = dma("scalar", dram[outname][c * 128:(c + 1) * 128, hh * cw:(hh + 1) * cw],
                   ob.ap()[:, 0:cw], deps=[q2])
        ph.set(obk, stok)
        stoks.append(stok)
    ph.set(kp + "stats_free", t5)
    return Tok2(stoks)


def emit_l0_ffn(ctx, g, dram, upto=None, w_ready=None):
    op, dma = ctx.op, ctx.dma
    ph = Ph()
    if w_ready is not None:
        for sl_ in range(8):
            ph.set(f"l0f_{sl_}", w_ready)
    qspills = []
    tb1 = dma("sync", g["c_bcol"].ap(), dram["b1_col"])
    tb2 = dma("sync", g["c_brep"].ap(), dram["b2_rep"])
    ctx.barrier()
    hbufs = ["q_tm", "k_tm", "qh_tm", "qh_tm2"]

    xfh, xfl = g["xnfh"].ap(), g["xnfl"].ap()

    def xn_loads(s_):
        t0_ = s_ * STRIPE
        tl = [None] * 16
        for c in (0, 2, 4, 6):
            sap = [[T, 128], [128 * T, 2], [1, STRIPE]]
            th = dma("gpsimd", xfh[:, c * STRIPE:(c + 2) * STRIPE],
                     bass.AP(dram["xnh_fm"].tensor, c * 128 * T + t0_, sap),
                     deps=[ph.t("xnf_war")])
            tlo = dma("gpsimd", xfl[:, c * STRIPE:(c + 2) * STRIPE],
                      bass.AP(dram["xnl_fm"].tensor, c * 128 * T + t0_, sap),
                      deps=[ph.t("xnf_war")])
            tl[2 * c] = tl[2 * c + 2] = th
            tl[2 * c + 1] = tl[2 * c + 3] = tlo
        return tl
    xtoks = xn_loads(0)
    for s in range(T // STRIPE):
        t0 = s * STRIPE
        for fg in range(4):
            mmtok = {}
            for dc in range(8):
                sl = (fg * 8 + dc) % 8
                wbuf, woff = L0FSLOTS[sl]
                wtok = dma("sync", g[wbuf].ap()[:, woff:woff + 1024],
                           dram["w1q"][dc * 128:(dc + 1) * 128, fg * 1024:(fg + 1) * 1024],
                           deps=[ph.t(f"l0f_{sl}")])
                last = None
                for fl in range(8):
                    fc = fg * 8 + fl
                    bank = g[f"p{fl}"].ap()
                    deps = [wtok, xtoks[2 * dc], xtoks[2 * dc + 1]]
                    if dc == 0:
                        deps.append(ph.bank_free.get(f"p{fl}"))
                    wt = g[wbuf].ap()[:, woff + fl * 128:woff + (fl + 1) * 128]
                    last = op("tensor", lambda h, b_=bank, w_=wt,
                              x_=xfh[:, dc * STRIPE:(dc + 1) * STRIPE], s_=(dc == 0):
                              h.matmul(b_[:, 0:STRIPE], w_, x_, start=s_, stop=False), deps=deps)
                    last = op("tensor", lambda h, b_=bank, w_=wt,
                              x_=xfl[:, dc * STRIPE:(dc + 1) * STRIPE], e_=(dc == 7):
                              h.matmul(b_[:, 0:STRIPE], w_, x_, start=False, stop=e_))
                    mmtok[fl] = last
                ph.set(f"l0f_{sl}", last)
            for fl in range(8):
                fc = fg * 8 + fl
                rscr = g["q_tm"].ap()[:, fl * STRIPE:(fl + 1) * STRIPE]
                hi = g["hh"].ap()[:, fc * STRIPE:(fc + 1) * STRIPE]
                lo = g["hl"].ap()[:, fc * STRIPE:(fc + 1) * STRIPE]
                ev = op("scalar", lambda h, d_=rscr, b_=g[f"p{fl}"].ap(), fc_=fc:
                        h.activation(d_, b_[:, 0:STRIPE], AF.Relu,
                                     bias=g["c_bcol"].ap()[:, fc_:fc_ + 1]),
                        deps=[mmtok[fl], tb1, ph.t(f"rscr_{fl}")])
                e2 = op("vector", lambda h, d_=hi, r_=rscr: h.tensor_copy(d_, r_),
                        deps=[ev, ph.t(f"hfree_{fc}")])
                e3 = op("vector", lambda h, d_=lo, r_=rscr, i_=hi: h.tensor_tensor(
                    d_, r_, i_, ALU.subtract), deps=[e2])
                ph.bank_free[f"p{fl}"] = ev
                ph.set(f"h_{fc}", e3)
                ph.set(f"rscr_{fl}", e3)
        ph.set("xnf_war", Tok2([ph.t(f"h_{fc}") for fc in range(32)]))
        if s + 1 < T // STRIPE:
            next_xtoks = xn_loads(s + 1)
        banks = ["p0", "p1", "p2", "p3"]
        mmtok = {}
        for fc in range(32):
            sl = fc % 8
            wbuf, woff = L0FSLOTS[sl]
            wtok = dma("sync", g[wbuf].ap()[:, woff:woff + 1024],
                       dram["w2q"][fc * 128:(fc + 1) * 128, :], deps=[ph.t(f"l0f_{sl}")])
            last = None
            for i in range(NT):
                lh = g["hh"].ap()[:, fc * STRIPE + i * 128: fc * STRIPE + (i + 1) * 128]
                ll = g["hl"].ap()[:, fc * STRIPE + i * 128: fc * STRIPE + (i + 1) * 128]
                for hi in range(2):
                    bk = banks[i * 2 + hi]
                    deps = [wtok, ph.t(f"h_{fc}")]
                    if fc == 0:
                        deps.append(ph.bank_free.get(bk))
                    w_ = g[wbuf].ap()[:, woff + hi * 512:woff + (hi + 1) * 512]
                    last = op("tensor", lambda h, b_=g[bk].ap(), l_=lh, ww_=w_,
                              s_=(fc == 0): h.matmul(b_[:, 0:512], l_, ww_, start=s_, stop=False),
                              deps=deps)
                    last = op("tensor", lambda h, b_=g[bk].ap(), l_=ll, ww_=w_,
                              e_=(fc == 31): h.matmul(b_[:, 0:512], l_, ww_, start=False, stop=e_))
                    mmtok[(i, hi)] = last
            ph.set(f"l0f_{sl}", last)
            ph.set(f"hfree_{fc}", last)
        for i in range(NT):
            a_tok = dma("gpsimd", g["x_tm"].ap()[:, i * D:(i + 1) * D],
                        dram["attn0"][t0 + i * 128: t0 + (i + 1) * 128, :],
                        deps=[ph.t(f"xtm_free{i}")])
            x_tok = dma("gpsimd", g["xn_tm"].ap()[:, i * D:(i + 1) * D],
                        dram["x0"][t0 + i * 128: t0 + (i + 1) * 128, :],
                        deps=[ph.t(f"xntm_free{i}")])
            otoks = []
            for hi in range(2):
                dst = g["y_tm"].ap()[:, i * D + hi * 512: i * D + hi * 512 + 512]
                t1 = op("vector", lambda h, d_=dst, b_=g[banks[i * 2 + hi]].ap(), o_=hi * 512:
                        h.tensor_tensor(d_, b_[:, 0:512], g["c_brep"].ap()[:, o_:o_ + 512], ALU.add),
                        deps=[mmtok[(i, hi)], tb2, ph.t(f"ytm_free{i}")])
                t2 = op("vector", lambda h, d_=dst, x_=g["x_tm"].ap()[:, i * D + hi * 512: i * D + hi * 512 + 512]:
                        h.tensor_tensor(d_, d_, x_, ALU.add), deps=[t1, a_tok])
                t3 = op("vector", lambda h, d_=dst, x_=g["xn_tm"].ap()[:, i * D + hi * 512: i * D + hi * 512 + 512]:
                        h.tensor_tensor(d_, d_, x_, ALU.add), deps=[t2, x_tok])
                ph.bank_free[banks[i * 2 + hi]] = t3
                otoks.append(t3)
            sp = dma("scalar", dram["x1"][t0 + i * 128: t0 + (i + 1) * 128, :],
                     g["y_tm"].ap()[:, i * D:(i + 1) * D], deps=otoks)
            ph.set(f"ytm_free{i}", sp)
            ph.set(f"xtm_free{i}", otoks[-1])
            ph.set(f"xntm_free{i}", otoks[-1])
        if s + 1 < T // STRIPE:
            xtoks = next_xtoks
        if s == 0:
            tq_st = emit_tq_pass1(ctx, ph, g, dram, "wgT", "wgq", 2, l1=True, gcol=0,
                                  stg=("k_tm", "qh_tm"))
        if s == 2:
            tq_st = emit_tq_pass1(ctx, ph, g, dram, "wuT", "wuq", 3, l1=True, gcol=1,
                                  stg=("k_tm", "qh_tm"))
        if s == 4:
            tq_st = emit_tq_pass1(ctx, ph, g, dram, "wdT", "wdq", 4, l1=True, gcol=2,
                                  stg=("k_tm", "qh_tm"))
        if s in (1, 3, 5):
            qspills.append(emit_tq_finish(ctx, ph, g, dram, tq_st))
    return Tok2([t for q in qspills for t in q.toks])


def emit_act_quant(ctx, ph, g, src_ap, dst_ap, sx_dst, deps, tag,
                   zb=None, zfk="misc_free"):
    op = ctx.op
    N = src_ap.shape[1]
    st = g["c_stats"].ap()
    amax = st[:, 6:7]
    t1 = op("vector", lambda h: h.reduce_max(amax, src_ap, AX.X, apply_absolute_value=True),
            deps=deps + [ph.t("aq_free")])
    t2 = op("vector", lambda h: h.tensor_scalar(amax, amax, 1e-5, None, ALU.max), deps=[t1])
    s_col = st[:, 7:8]
    t3 = op("vector", lambda h: h.reciprocal(s_col, amax), deps=[t2])
    t4 = op("vector", lambda h: h.tensor_scalar(s_col, s_col, 127.0, None, ALU.mult), deps=[t3])
    tr = op("vector", lambda h: h.reciprocal(sx_dst, s_col), deps=[t4, ph.t("sx_free")])
    z = (zb if zb is not None else g["misc"].ap())[:, 0:N]
    t5 = op("vector", lambda h: h.tensor_scalar(z, src_ap, s_col, None, ALU.mult),
            deps=[t4, ph.t(zfk)])
    bfa, bfb = g["bfA"].ap()[:, 0:N], g["bfB"].ap()[:, 0:N]
    t6 = op("scalar", lambda h: h.activation(bfa, z, AF.Identity, bias=g["c_eps"].ap()[:, 3:4]),
            deps=[t5, ph.t("bf_free")])
    t7 = op("scalar", lambda h: h.activation(bfb, z, AF.Identity, bias=g["c_eps"].ap()[:, 4:5]),
            deps=[t5, ph.t("bf_free")])
    t8 = op("vector", lambda h: h.tensor_scalar(bfa, bfa, 128.0, None, ALU.subtract), deps=[t6])
    t9 = op("vector", lambda h: h.tensor_scalar(bfb, bfb, 128.0, None, ALU.add), deps=[t7])
    msk = g["u8"].ap()[:, 0:N]
    ta = op("vector", lambda h: h.tensor_scalar(msk, z, 0.0, None, ALU.is_ge),
            deps=[t5, ph.t("u8_free")])
    tsel = op("vector", lambda h: h.select(dst_ap, msk, bfa, bfb, add_drain=True),
              deps=[t8, t9, ta])
    ph.set("aq_free", tsel)
    ph.set(zfk, tsel)
    ph.set("bf_free", tsel)
    ph.set("u8_free", tsel)
    ph.set("sx_free", tr)
    return tsel, tr


def _haar_block(ctx, ph, g, Q, Vv, hw_q, hw_v, scr_q, scr_v, deps):
    op = ctx.op
    NTOK = Q.shape[1]
    C = INV_SQRT2

    def lvl(dst_d, dst_a, src, n, dps):
        e, o = src[:, 0:n:2], src[:, 1:n:2]
        t1 = op("vector", lambda h: h.tensor_tensor(dst_d, e, o, ALU.subtract), deps=dps)
        t1 = op("vector", lambda h: h.tensor_scalar(dst_d, dst_d, C, None, ALU.mult), deps=[t1])
        t2 = op("vector", lambda h: h.tensor_tensor(dst_a, e, o, ALU.add), deps=dps)
        t2 = op("vector", lambda h: h.tensor_scalar(dst_a, dst_a, C, None, ALU.mult), deps=[t2])
        return t1, t2

    def wavedec(X, scr, dps):
        d1, a1 = scr[:, 128:256], scr[:, 0:128]
        t1, t2 = lvl(d1, a1, X, 256, dps)
        d2, a2 = X[:, 0:64], X[:, 64:128]
        t3, t4 = lvl(d2, a2, a1, 128, [t1, t2])
        d3, a3 = X[:, 128:160], X[:, 160:192]
        t5, t6 = lvl(d3, a3, a2, 64, [t3, t4])
        return (a3, d3, d2, d1), [t5, t6, t3, t1]

    (qa3, qd3, qd2, qd1), tq = wavedec(Q, scr_q, deps)
    (va3, vd3, vd2, vd1), tv = wavedec(Vv, scr_v, deps)
    prods = []
    for (cq, cv, j) in ((qa3, va3, 0), (qd3, vd3, 1), (qd2, vd2, 2)):
        t1 = op("vector", lambda h, cq_=cq, j_=j: h.tensor_scalar(
            cq_, cq_, hw_q[:, j_:j_ + 1], None, ALU.mult), deps=tq + tv + [ph.t("hw")])
        t2 = op("vector", lambda h, cv_=cv, j_=j: h.tensor_scalar(
            cv_, cv_, hw_v[:, j_:j_ + 1], None, ALU.mult), deps=tq + tv + [ph.t("hw")])
        t3 = op("vector", lambda h, cq_=cq, cv_=cv: h.tensor_tensor(cq_, cq_, cv_, ALU.mult),
                deps=[t1, t2])
        prods.append(t3)
    t3 = op("vector", lambda h: h.tensor_tensor(qd1, qd1, vd1, ALU.mult), deps=tq + tv)
    prods.append(t3)

    def idwt(a, d, out, n, dps):
        oe, oo = out[:, 0:2 * n:2], out[:, 1:2 * n:2]
        t1 = op("vector", lambda h: h.tensor_tensor(oe, a, d, ALU.add), deps=dps)
        t1 = op("vector", lambda h: h.tensor_scalar(oe, oe, C, None, ALU.mult), deps=[t1])
        t2 = op("vector", lambda h: h.tensor_tensor(oo, a, d, ALU.subtract), deps=dps)
        t2 = op("vector", lambda h: h.tensor_scalar(oo, oo, C, None, ALU.mult), deps=[t2])
        return [t1, t2]

    s1 = Q[:, 192:256]
    ts1 = idwt(qa3, qd3, s1, 32, prods)
    s2 = scr_q[:, 0:128]
    ts2 = idwt(s1, qd2, s2, 64, ts1 + prods)
    ts3 = idwt(s2, qd1, Q, 128, ts2 + prods)
    return ts3


def emit_l1_attn(ctx, g, dram, upto=None):
    op, dma = ctx.op, ctx.dma
    ph = Ph()
    thw = dma("sync", bass.AP(g["c_hw"], 0, [[48, 128], [3, 8], [1, 3]]),
              bass.AP(dram["hwq_col"].tensor, 0, [[3, 128], [128 * 3, 8], [1, 3]]))
    thw2 = dma("sync", bass.AP(g["c_hw"], 24, [[48, 128], [3, 8], [1, 3]]),
               bass.AP(dram["hwv_col"].tensor, 0, [[3, 128], [128 * 3, 8], [1, 3]]))
    ph.set("hw", Tok2([thw, thw2]))
    tbq = dma("sync", g["c_bcol"].ap()[:, 0:16], dram["bqv_col"])
    tbo = dma("sync", g["c_brep"].ap(), dram["bout1_rep"])
    for s in range(T // STRIPE):
        emit_l1_attn_stripe(ctx, ph, g, dram, s, tbq, tbo, upto)
        if upto in ("l1_haar",) and s == 0:
            return


def emit_l1_attn_stripe(ctx, ph, g, dram, s, tbq, tbo, upto=None):
    op, dma = ctx.op, ctx.dma
    t0 = s * STRIPE
    x_tm, xn_tm = g["x_tm"].ap(), g["xn_tm"].ap()
    xfh, xfl = g["xnfh"].ap(), g["xnfl"].ap()
    for i in range(NT):
        xt = x_tm[:, i * D:(i + 1) * D]
        tk = dma("gpsimd", xt, dram["x1"][t0 + i * 128: t0 + (i + 1) * 128, :],
                 deps=[ph.t(f"x_free{i}")])
        xn = xn_tm[:, i * D:(i + 1) * D]
        zk = "misc_free" if i == 0 else "misc2_free"
        zb = (g["misc"] if i == 0 else g["misc2"]).ap()
        tln = emit_ln_tile(ctx, g, xt, xn, [tk, ph.t(f"xn_free{i}"), ph.t(zk)],
                           so=4 * i, scratch=zb[:, 0:D])
        ph.set(zk, tln)
        ph.set(f"ln{i}", tln)
        for c in range(8):
            co = c * STRIPE + i * 128
            bank = "p0" if (c + i) % 2 == 0 else "p1"
            pb = g[bank].ap()
            tm = op("tensor", lambda h, s_=xn[:, c * 128:(c + 1) * 128], p_=pb:
                    h.transpose(p_[:, 0:128], s_, g["c_ident"].ap()),
                    deps=[tln, ph.bank_free.get(bank)])
            te = op("scalar", lambda h, p_=pb, co_=co: h.copy(xfh[:, co_:co_ + 128], p_[:, 0:128]),
                    deps=[tm, ph.t("xnf_free")])
            te2 = op("vector", lambda h, p_=pb, co_=co: h.tensor_tensor(
                xfl[:, co_:co_ + 128], p_[:, 0:128], xfh[:, co_:co_ + 128], ALU.subtract),
                deps=[te])
            ph.bank_free[bank] = te2
            ph.set(f"xnf_{c}_{i}", te2)
        gi = s * NT + i
        tq, tr = emit_act_quant(ctx, ph, g, xn, xt, g["c_rsx"].ap()[:, gi:gi + 1],
                                [tln], f"xq{i}", zb=zb, zfk=zk)
        ph.set(f"xq_{i}", tq)
        ph.set(f"xn_free{i}",
               Tok2([ph.t(f"xnf_{c}_{i}") for c in range(8)] + [tq]))
        tp_toks = []
        for c in range(8):
            dst = g["hl"].ap()[:, 4096 + i * 1024 + c * 128:4096 + i * 1024 + (c + 1) * 128]
            bank = "p2" if c % 2 == 0 else "p3"
            tp_toks.append(_tp128(ctx, ph, g, xt[:, c * 128:(c + 1) * 128], dst, bank,
                                  [tq, ph.t(f"hlq{i}")]))
        sp = dma("scalar",
                 bass.AP(dram["xq_fm"].tensor, t0 + i * 128, [[T, 128], [128 * T, 8], [1, 128]]),
                 bass.AP(g["hl"], 4096 + i * 1024, [[8192, 128], [128, 8], [1, 128]]),
                 deps=tp_toks)
        ph.set(f"hlq{i}", sp)
        ph.set(f"x_free{i}", Tok2(tp_toks))
        ph.set("bf_free", Tok2([sp, ph.t("bf_free")]))
    xnf_toks = _tok_max([ph.t(f"xnf_{c}_{i}") for c in range(8) for i in range(NT)])

    qf, vf = g["qh_tm"].ap(), g["qh_tm2"].ap()
    for grp in range(2):
        mmtok = {}
        for dc in range(8):
            sl = (grp * 8 + dc) % 6
            wbuf, woff = L1SLOTS[sl]
            wtok = dma("sync", g[wbuf].ap()[:, woff:woff + 2048],
                       dram["wqv_hl0" if grp == 0 else "wqv_hl1"][dc * 128:(dc + 1) * 128, :],
                       deps=[ph.t(f"l1w_{sl}")])
            xh = xfh[:, dc * STRIPE:(dc + 1) * STRIPE]
            xl = xfl[:, dc * STRIPE:(dc + 1) * STRIPE]
            last = None
            for fl in range(8):
                bank = g[f"p{fl}"].ap()
                wh = g[wbuf].ap()[:, woff + fl * 128:woff + (fl + 1) * 128]
                wl = g[wbuf].ap()[:, woff + 1024 + fl * 128:woff + 1024 + (fl + 1) * 128]
                deps = [wtok, xnf_toks]
                if dc == 0:
                    deps.append(ph.bank_free.get(f"p{fl}"))
                last = op("tensor", lambda h, b_=bank, w_=wh, x_=xh, s_=(dc == 0):
                          h.matmul(b_[:, 0:STRIPE], w_, x_, start=s_, stop=False), deps=deps)
                last = op("tensor", lambda h, b_=bank, w_=wh, x_=xl:
                          h.matmul(b_[:, 0:STRIPE], w_, x_, start=False, stop=False))
                last = op("tensor", lambda h, b_=bank, w_=wl, x_=xh, e_=(dc == 7):
                          h.matmul(b_[:, 0:STRIPE], w_, x_, start=False, stop=e_))
                mmtok[fl] = last
            ph.set(f"l1w_{sl}", last)
        for fl in range(8):
            fcg = grp * 8 + fl
            half, fc = fcg // 8, fcg % 8
            dstb = qf if half == 0 else vf
            bank = g[f"p{fl}"].ap()
            ev = op("scalar", lambda h, d_=dstb[:, fc * STRIPE:(fc + 1) * STRIPE], b_=bank, f_=fcg:
                    h.activation(d_, b_[:, 0:STRIPE], AF.Identity,
                                 bias=g["c_bcol"].ap()[:, f_:f_ + 1]),
                    deps=[mmtok[fl], tbq, ph.t(f"qv_free{half}_{fc}")])
            ph.bank_free[f"p{fl}"] = ev
            ph.set(f"qv_{half}_{fc}", ev)

    for c in range(8):
        Q = qf[:, c * STRIPE:(c + 1) * STRIPE]
        Vv = vf[:, c * STRIPE:(c + 1) * STRIPE]
        scr_q = g["kt_tm"].ap()[:, c * STRIPE:(c + 1) * STRIPE]
        scr_v = g["kt_tm2"].ap()[:, c * STRIPE:(c + 1) * STRIPE]
        toks = _haar_block(ctx, ph, g, Q, Vv, g["c_hw"].ap()[:, c * 3:c * 3 + 3],
                           g["c_hw"].ap()[:, 24 + c * 3:24 + c * 3 + 3],
                           scr_q, scr_v,
                           [ph.t(f"qv_0_{c}"), ph.t(f"qv_1_{c}"), ph.t(f"hscr_{c}")])
        ph.set(f"haar_{c}", _tok_max(toks))
        ph.set(f"hscr_{c}", _tok_max(toks))
    if upto == "l1_haar" and s == 0:
        emit_debug_dump(ctx, g, dram, qf[:, 0:8 * STRIPE])
        return

    yh, yl = g["yh"].ap(), g["yl"].ap()
    for c in range(8):
        hsp1 = op("scalar", lambda h, c_=c: h.copy(yh[:, c_ * STRIPE:(c_ + 1) * STRIPE],
                                                   qf[:, c_ * STRIPE:(c_ + 1) * STRIPE]),
                  deps=[ph.t(f"haar_{c}"), ph.t("yhl_free")])
        hsp2 = op("vector", lambda h, c_=c: h.tensor_tensor(
            yl[:, c_ * STRIPE:(c_ + 1) * STRIPE], qf[:, c_ * STRIPE:(c_ + 1) * STRIPE],
            yh[:, c_ * STRIPE:(c_ + 1) * STRIPE], ALU.subtract), deps=[hsp1])
        ph.set(f"hsp_{c}", hsp2)
    banks = ["p2", "p3", "p4", "p5"]
    mm_tok = {}
    for c in range(8):
        sl = 6 + c % 3
        wbuf, woff = L1SLOTS[sl]
        wtok = dma("sync", g[wbuf].ap()[:, woff:woff + 2 * D],
                   dram["wo1_hl"][c * 128:(c + 1) * 128, :],
                   deps=[ph.t(f"l1w_{sl}")])
        wtok2 = wtok
        last = None
        for i in range(NT):
            lh = yh[:, c * STRIPE + i * 128: c * STRIPE + (i + 1) * 128]
            ll = yl[:, c * STRIPE + i * 128: c * STRIPE + (i + 1) * 128]
            for hi in range(2):
                bk = banks[i * 2 + hi]
                deps = [wtok, wtok2, ph.t(f"hsp_{c}")]
                if c == 0:
                    deps.append(ph.bank_free.get(bk))
                wh_ = g[wbuf].ap()[:, woff + hi * 512:woff + (hi + 1) * 512]
                wl_ = g[wbuf].ap()[:, woff + D + hi * 512:woff + D + (hi + 1) * 512]
                last = op("tensor", lambda h, b_=g[bk].ap(), l_=lh, w_=wh_, s_=(c == 0):
                          h.matmul(b_[:, 0:512], l_, w_, start=s_, stop=False), deps=deps)
                last = op("tensor", lambda h, b_=g[bk].ap(), l_=ll, w_=wh_:
                          h.matmul(b_[:, 0:512], l_, w_, start=False, stop=False))
                last = op("tensor", lambda h, b_=g[bk].ap(), l_=lh, w_=wl_, e_=(c == 7):
                          h.matmul(b_[:, 0:512], l_, w_, start=False, stop=e_))
                mm_tok[(i, hi)] = last
        ph.set(f"l1w_{sl}", last)
    for i in range(NT):
        otoks = []
        for hi in range(2):
            dst = g["y_tm"].ap()[:, i * D + hi * 512: i * D + hi * 512 + 512]
            t1 = op("vector", lambda h, d_=dst, b_=g[banks[i * 2 + hi]].ap(), o_=hi * 512:
                    h.tensor_tensor(d_, b_[:, 0:512], g["c_brep"].ap()[:, o_:o_ + 512], ALU.add),
                    deps=[mm_tok[(i, hi)], tbo, ph.t("ytm_scratch_free")])
            ph.bank_free[banks[i * 2 + hi]] = t1
            otoks.append(t1)
        sp = dma("scalar", dram["attn1"][t0 + i * 128: t0 + (i + 1) * 128, :],
                 g["y_tm"].ap()[:, i * D:(i + 1) * D], deps=otoks)
        ph.set("ytm_scratch_free", sp)
    ph.set("yhl_free", _tok_max([mm_tok[k] for k in mm_tok]))
    ph.set("xnf_free", Tok2([ph.t(f"qv_{half}_{fc}") for half in range(2) for fc in range(8)]))
    for half in range(2):
        for fc in range(8):
            ph.set(f"qv_free{half}_{fc}",
                   Tok2([ph.t(f"hsp_{fc}"), ph.t(f"haar_{fc}"), ph.t("yhl_free")]))


def emit_l1_ffn(ctx, g, dram, upto=None, w_ready=None):
    op, dma = ctx.op, ctx.dma
    ph = Ph()
    if w_ready is not None:
        for sl_ in range(8):
            ph.set(f"l1w_{sl_}", w_ready)
    gate_bufs = ["q_tm", "k_tm", "qh_tm", "qh_tm2"]
    up_bufs = ["kt_tm", "kt_tm2", "y_tm", "v_aug"]
    for s in range(T // STRIPE):
        emit_l1_ffn_stripe(ctx, ph, g, dram, s, gate_bufs, up_bufs, upto)
        if upto == "l1_gq" and s == 0:
            return


def emit_l1_ffn_stripe(ctx, ph, g, dram, s, gate_bufs, up_bufs, upto=None):
    op, dma = ctx.op, ctx.dma
    t0 = s * STRIPE
    xqf = g["xqf"].ap()
    xtoks = []
    for c in (0, 2, 4, 6):
        xtoks.append(dma("gpsimd", xqf[:, c * STRIPE:(c + 2) * STRIPE],
                         bass.AP(dram["xq_fm"].tensor, c * 128 * T + t0,
                                 [[T, 128], [128 * T, 2], [1, STRIPE]]),
                         deps=[ph.t("xqf_free")]))
    xnf_tok = _tok_max(xtoks)
    st = g["c_stats"].ap()
    trs_tok = {}
    for i in range(NT):
        gi = s * NT + i
        rs = st[:, 4 + i:5 + i]
        trs_tok[i] = op("vector", lambda h, r_=rs, g_=gi: h.tensor_scalar(
            r_, g["c_rsx"].ap()[:, g_:g_ + 1], g["c_gam"].ap()[:, 0:1],
            None, ALU.mult), deps=[ph.t("gam"), ph.t(f"rs_free{i}")])
    for wname, (bufs, gcol) in (("g", (gate_bufs, 0)), ("u", (up_bufs, 1))):
        for fh in range(2):
            mmtok = {}
            for dc in range(8):
                sl = ((0 if wname == "g" else 16) + fh * 8 + dc) % 8
                wbuf, woff = L1SLOTS[sl]
                wtok = dma("sync", g[wbuf].ap()[:, woff:woff + 2048],
                           dram["w%sq" % ("g" if wname == "g" else "u")][dc * 128:(dc + 1) * 128,
                                                                         fh * 2048:(fh + 1) * 2048],
                           deps=[ph.t(f"l1w_{sl}")])
                last = None
                for i in range(NT):
                    lhsT = xqf[:, dc * STRIPE + i * 128: dc * STRIPE + (i + 1) * 128]
                    for ft in range(4):
                        bk = f"p{i * 4 + ft}"
                        deps = [wtok, xnf_tok]
                        if dc == 0:
                            deps.append(ph.bank_free.get(bk))
                        last = op("tensor", lambda h, b_=g[bk].ap(), l_=lhsT,
                                  w_=g[wbuf].ap()[:, woff + ft * 512:woff + (ft + 1) * 512],
                                  s_=(dc == 0), e_=(dc == 7):
                                  h.matmul(b_[:, 0:512], l_, w_, start=s_, stop=e_), deps=deps)
                        mmtok[(i, ft)] = last
                ph.set(f"l1w_{sl}", last)
            for i in range(NT):
                gi = s * NT + i
                rs = st[:, 4 + i:5 + i]
                trs = trs_tok[i]
                for ft in range(4):
                    fcol = fh * 2048 + ft * 512
                    bufi = fcol // 1024
                    off = i * 1024 + (fcol % 1024)
                    dst = g[bufs[bufi]].ap()[:, off:off + 512]
                    if wname == "g":
                        ev = op("scalar", lambda h, d_=dst, b_=g[f"p{i*4+ft}"].ap(), r_=rs:
                                h.activation(d_, b_[:, 0:512], AF.Silu, scale=r_),
                                deps=[mmtok[(i, ft)], trs, ph.t(f"gfree_{bufi}_{off}")])
                        ph.set(f"rs_free{i}", Tok2([ev, ph.t(f"rs_free{i}")]))
                        ev = op("vector", lambda h, d_=dst: h.tensor_scalar(
                            d_, d_, g["c_gam"].ap()[:, 1:2], None, ALU.mult), deps=[ev, ph.t("gam")])
                    else:
                        gsrc = g[gate_bufs[bufi]].ap()[:, off:off + 512]
                        ev = op("vector", lambda h, d_=dst, b_=g[f"p{i*4+ft}"].ap(),
                                r_=g["c_rsx"].ap()[:, gi:gi + 1], gs_=gsrc:
                                h.scalar_tensor_tensor(d_, b_[:, 0:512], r_, gs_,
                                                       ALU.mult, ALU.mult),
                                deps=[mmtok[(i, ft)], ph.t(f"gev_{i}_{fcol}"),
                                      ph.t(f"ufree_{bufi}_{off}")])
                    ph.bank_free[f"p{i * 4 + ft}"] = ev
                    ph.set(f"{wname}ev_{i}_{fcol}", ev)
    for i in range(NT):
        gi = s * NT + i
        deps = [ph.t(f"uev_{i}_{fc}") for fc in range(0, 4096, 512)]
        parts = []
        for bufi in range(4):
            seg = g[up_bufs[bufi]].ap()[:, i * 1024:(i + 1) * 1024]
            pcol = st[:, 8 + bufi:9 + bufi]
            parts.append((seg, pcol))
        tmax = None
        for bi, (seg, pcol) in enumerate(parts):
            tt = op("vector", lambda h, s_=seg, p_=pcol: h.reduce_max(
                p_, s_, AX.X, apply_absolute_value=True), deps=deps + [ph.t("stats2_free")])
            if tmax is None:
                tmax = tt
            else:
                tmax = op("vector", lambda h, p_=parts[0][1], q_=pcol: h.tensor_tensor(
                    p_, p_, q_, ALU.max), deps=[tmax, tt])
        amax = parts[0][1]
        t2 = op("vector", lambda h: h.tensor_scalar(amax, amax, 1e-5, None, ALU.max), deps=[tmax])
        sg = st[:, 12:13]
        t3 = op("vector", lambda h: h.reciprocal(sg, amax), deps=[t2])
        t4 = op("vector", lambda h: h.tensor_scalar(sg, sg, 127.0, None, ALU.mult), deps=[t3])
        rsg = st[:, 13:14]
        t5 = op("vector", lambda h: h.reciprocal(rsg, sg), deps=[t4])
        rsgd = st[:, 14 + i:15 + i]
        t6 = op("vector", lambda h, r_=rsgd: h.tensor_scalar(r_, rsg, g["c_gam"].ap()[:, 2:3],
                                                             None, ALU.mult),
                deps=[t5, ph.t("gam")])
        ph.set(f"rsgd_{i}", t6)
        for bufi in range(4):
            seg = g[up_bufs[bufi]].ap()[:, i * 1024:(i + 1) * 1024]
            z = g["misc"].ap()[:, 0:1024]
            tz = op("vector", lambda h, s_=seg, z_=z: h.tensor_scalar(z_, s_, sg, None, ALU.mult),
                    deps=[t4, ph.t("misc_free")] + deps)
            bfa, bfb = g["bfA"].ap(), g["bfB"].ap()
            ta = op("scalar", lambda h, z_=z: h.activation(bfa, z_, AF.Identity,
                                                           bias=g["c_eps"].ap()[:, 3:4]),
                    deps=[tz, ph.t("bf_free")])
            tb = op("scalar", lambda h, z_=z: h.activation(bfb, z_, AF.Identity,
                                                           bias=g["c_eps"].ap()[:, 4:5]),
                    deps=[tz, ph.t("bf_free")])
            t8 = op("vector", lambda h: h.tensor_scalar(bfa, bfa, 128.0, None, ALU.subtract),
                    deps=[ta])
            t9 = op("vector", lambda h: h.tensor_scalar(bfb, bfb, 128.0, None, ALU.add),
                    deps=[tb])
            msk = g["u8"].ap()
            tm = op("vector", lambda h, z_=z: h.tensor_scalar(msk, z_, 0.0, None, ALU.is_ge),
                    deps=[tz, ph.t("u8_free")])
            tsel = op("vector", lambda h, d_=seg: h.select(d_, msk, bfa, bfb, add_drain=True),
                      deps=[t8, t9, tm])
            ph.set("misc_free", tsel)
            ph.set("bf_free", tsel)
            ph.set("u8_free", tsel)
            ph.set("stats2_free", t6)
            ph.set(f"gq_{i}_{bufi}", tsel)
    if upto == "l1_gq":
        emit_debug_dump(ctx, g, dram, g[up_bufs[0]].ap()[:, 0:2048])
        return
    gq_all = _tok_max([ph.t(f"gq_{i}_{b}") for i in range(NT) for b in range(4)])
    sp_toks = []
    for i in range(NT):
        for fg in range(4):
            blk = []
            for fl in range(8):
                fc = fg * 8 + fl
                src_ap = g[up_bufs[fc // 8]].ap()[:, i * 1024 + (fc % 8) * 128:
                                                 i * 1024 + (fc % 8) * 128 + 128]
                dst = g["bfB"].ap()[:, fl * 128:(fl + 1) * 128]
                bank = ("p0", "p1")[fl % 2]
                blk.append(_tp128(ctx, ph, g, src_ap, dst, bank, [gq_all, ph.t("bf_free")]))
            sp = dma("scalar",
                     bass.AP(dram["gq_fm"].tensor, fg * 1024 * T + t0 + i * 128,
                             [[T, 128], [128 * T, 8], [1, 128]]),
                     bass.AP(g["bfB"], 0, [[1024, 128], [128, 8], [1, 128]]),
                     deps=blk)
            sp_toks.append(sp)
            ph.set(f"gqsp_{fg}_{i}", sp)
            ph.set("bf_free", Tok2([sp, ph.t("bf_free")]))
    gqfm_tok = Tok2(sp_toks)
    banks = ["p2", "p3", "p4", "p5"]
    mmtok = {}
    for fc in range(32):
        sl = fc % 8
        wbuf, woff = L1SLOTS[sl]
        wtok = dma("sync", g[wbuf].ap()[:, woff:woff + D], dram["wdq"][fc * 128:(fc + 1) * 128, :],
                   deps=[ph.t(f"l1w_{sl}")])
        slot = fc % 8
        gtok = dma("gpsimd", xqf[:, slot * STRIPE:(slot + 1) * STRIPE],
                   dram["gq_fm"][fc * 128:(fc + 1) * 128, t0:t0 + STRIPE],
                   deps=[ph.t(f"gqsp_{fc // 8}_0"), ph.t(f"gqsp_{fc // 8}_1"),
                         ph.t(f"slot_free{slot}")])
        last = None
        for i in range(NT):
            lhsT = xqf[:, slot * STRIPE + i * 128: slot * STRIPE + (i + 1) * 128]
            for hi in range(2):
                bk = banks[i * 2 + hi]
                deps = [wtok, gtok]
                if fc == 0:
                    deps.append(ph.bank_free.get(bk))
                last = op("tensor", lambda h, b_=g[bk].ap(), l_=lhsT,
                          w_=g[wbuf].ap()[:, woff + hi * 512:woff + (hi + 1) * 512],
                          s_=(fc == 0), e_=(fc == 31):
                          h.matmul(b_[:, 0:512], l_, w_, start=s_, stop=e_), deps=deps)
                mmtok[(i, hi)] = last
        ph.set(f"l1w_{sl}", last)
        ph.set(f"slot_free{slot}", last)
    for i in range(NT):
        a_tok = dma("gpsimd", g["x_tm"].ap()[:, i * D:(i + 1) * D],
                    dram["attn1"][t0 + i * 128: t0 + (i + 1) * 128, :],
                    deps=[ph.t(f"xa_free{i}")])
        x_tok = dma("gpsimd", g["xn_tm"].ap()[:, i * D:(i + 1) * D],
                    dram["x1"][t0 + i * 128: t0 + (i + 1) * 128, :],
                    deps=[ph.t(f"xb_free{i}")])
        otoks = []
        for hi in range(2):
            dst = g["misc"].ap()[:, hi * 512:(hi + 1) * 512]
            t1 = op("vector", lambda h, d_=dst, b_=g[banks[i * 2 + hi]].ap(), i_=i:
                    h.tensor_scalar(d_, b_[:, 0:512], g["c_stats"].ap()[:, 14 + i_:15 + i_],
                                    None, ALU.mult),
                    deps=[mmtok[(i, hi)], ph.t(f"rsgd_{i}"), ph.t("misc_free")])
            t2 = op("vector", lambda h, d_=dst, x_=g["x_tm"].ap()[:, i * D + hi * 512: i * D + hi * 512 + 512]:
                    h.tensor_tensor(d_, d_, x_, ALU.add), deps=[t1, a_tok])
            t3 = op("vector", lambda h, d_=dst, x_=g["xn_tm"].ap()[:, i * D + hi * 512: i * D + hi * 512 + 512]:
                    h.tensor_tensor(d_, d_, x_, ALU.add), deps=[t2, x_tok])
            ph.bank_free[banks[i * 2 + hi]] = t3
            otoks.append(t3)
        sp = dma("scalar", dram["out"][t0 + i * 128: t0 + (i + 1) * 128, :],
                 g["misc"].ap()[:, 0:D], deps=otoks)
        ph.set("misc_free", sp)
        ph.set(f"xa_free{i}", otoks[-1])
        ph.set(f"xb_free{i}", otoks[-1])
    dn = Tok2([mmtok[k] for k in mmtok])
    ph.set("xqf_free", dn)
    for i in range(NT):
        for bufi in range(4):
            for off in range(0, 1024, 512):
                ph.set(f"gfree_{bufi}_{i * 1024 + off}", Tok2([ph.t(f"uev_{i}_{bufi * 1024 + off}")]))
                ph.set(f"ufree_{bufi}_{i * 1024 + off}", gqfm_tok)


def emit_debug_dump(ctx, g, dram, src_ap, cols=None, dst_off=0):
    ctx.barrier()
    n = src_ap.shape[1] if cols is None else cols
    t = ctx.dma("sync", dram["dbg"][0:src_ap.shape[0], dst_off:dst_off + n], src_ap)
    ctx.dep("sync", [t])
